# version 09
# baseline (speedup 1.0000x reference)
import numpy as np
import ml_dtypes
from contextlib import ExitStack

import concourse.bass as bass
import concourse.mybir as mybir
from concourse.bass_utils import run_bass_kernel_spmd

F32 = mybir.dt.float32
BF16 = mybir.dt.bfloat16
AF = mybir.ActivationFunctionType
ALU = mybir.AluOpType
AX = mybir.AxisListType

H = 1024
INNER = 2048
NB = 2
L = 2048
T = NB * L
NE = 32
I_EXP = 512
IS = 1024
EPS = 1e-5
HT = H // 128
CT = INNER // 128


class Ctr:
    def __init__(self, handle):
        self.h = handle
        self.n = 0

    def now(self):
        return (self.h, self.n)


class Prog:
    ENGINES = ("sync", "gpsimd", "vector", "scalar", "tensor")

    def __init__(self, nc):
        self.nc = nc
        self.q = {e: [] for e in self.ENGINES}
        self.pending = {e: [] for e in self.ENGINES}

    def em(self, eng, waits, f, incs=()):
        w = list(self.pending[eng]) + [x for x in waits if x]
        self.pending[eng] = []
        bumped = []
        for c, k in incs:
            c.n += k
            bumped.append((c.h, k))
        self.q[eng].append((tuple(w), f, tuple(bumped)))

    def join(self, tokens):
        for e in self.ENGINES:
            self.pending[e].extend(tokens)

    def run(self, block):
        hw = {e: {} for e in self.ENGINES}

        def make(eng_name):
            items = self.q[eng_name]
            marks = hw[eng_name]

            def body(engine):
                for waits, f, incs in items:
                    for (h, v) in waits:
                        if marks.get(h, -1) < v:
                            engine.wait_ge(h, v)
                            marks[h] = v
                    ins = f(engine)
                    for (h, k) in incs:
                        ins = ins.then_inc(h, k)
            return body

        block.sync(make("sync"))
        block.gpsimd(make("gpsimd"))
        block.vector(make("vector"))
        block.scalar(make("scalar"))
        block.tensor(make("tensor"))



def emit_ln(p, c, xr, x_ln_out, Tc, sc):
    SS, VS, TS = c["SS"], c["VS"], c["TS"]
    g = c["g"]
    n = HT * Tc
    fp32s = sc.get("fp32_stats", False)
    if fp32s:
        xb_tok = None
        p.em("scalar", sc["xr_ready"] + sc.get("xsq_free", []),
             lambda e: e.activation(sc["d32"][:, 0:n], xr[:, 0:n], AF.Square),
             incs=[(SS, 1)])
        xsq_tok = SS.now()
        for h in range(HT):
            p.em("tensor", (sc["xr_ready"] + sc["stat_free"]) if h == 0 else [],
                 lambda e, h=h: e.matmul(sc["pm"][:, 0:Tc], g["onem_f"][:, 0:1],
                                         xr[:, h * Tc:(h + 1) * Tc],
                                         start=(h == 0), stop=(h == HT - 1)),
                 incs=[(TS, 1)] if h == HT - 1 else [])
        pm_tok = TS.now()
        for h in range(HT):
            p.em("tensor", [xsq_tok] if h == 0 else [],
                 lambda e, h=h: e.matmul(sc["pv"][:, 0:Tc], g["onem_f"][:, 0:1],
                                         sc["d32"][:, h * Tc:(h + 1) * Tc],
                                         start=(h == 0), stop=(h == HT - 1)),
                 incs=[(TS, 1)] if h == HT - 1 else [])
        pv_tok = TS.now()
    else:
        p.em("scalar", sc["xr_ready"] + sc.get("xb_free", []),
             lambda e: e.copy(sc["xb"][:, 0:n], xr[:, 0:n]), incs=[(SS, 1)])
        xb_tok = SS.now()
        p.em("scalar", sc.get("xsq_free", []),
             lambda e: e.activation(sc["xsq"][:, 0:n], xr[:, 0:n], AF.Square),
             incs=[(SS, 1)])
        xsq_tok = SS.now()
        for h in range(HT):
            p.em("tensor", ([xb_tok] + sc["stat_free"]) if h == 0 else [],
                 lambda e, h=h: e.matmul(sc["pm"][:, 0:Tc], g["onem"][:, 0:1],
                                         sc["xb"][:, h * Tc:(h + 1) * Tc],
                                         start=(h == 0), stop=(h == HT - 1)),
                 incs=[(TS, 1)] if h == HT - 1 else [])
        pm_tok = TS.now()
        for h in range(HT):
            p.em("tensor", [xsq_tok] if h == 0 else [],
                 lambda e, h=h: e.matmul(sc["pv"][:, 0:Tc], g["onem"][:, 0:1],
                                         sc["xsq"][:, h * Tc:(h + 1) * Tc],
                                         start=(h == 0), stop=(h == HT - 1)),
                 incs=[(TS, 1)] if h == HT - 1 else [])
        pv_tok = TS.now()
    p.em("vector", [pm_tok], lambda e: e.tensor_copy(sc["mrow"][:, 0:Tc], sc["pm"][:, 0:Tc]),
         incs=[(VS, 1)])
    m_tok = VS.now()
    p.em("vector", [m_tok], lambda e: e.tensor_mul(
        sc["msq"][:, 0:Tc], sc["mrow"][:, 0:Tc], sc["mrow"][:, 0:Tc]), incs=[(VS, 1)])
    p.em("vector", [VS.now(), pv_tok], lambda e: e.tensor_sub(
        sc["vrow"][:, 0:Tc], sc["pv"][:, 0:Tc], sc["msq"][:, 0:Tc]), incs=[(VS, 1)])
    v_tok = VS.now()
    p.em("scalar", [v_tok], lambda e: e.activation(
        sc["lrow"][:, 0:Tc], sc["vrow"][:, 0:Tc], AF.Ln, bias=c["g"]["cbias"][0:1, 0:1]), incs=[(SS, 1)])
    p.em("scalar", [SS.now()], lambda e: e.activation(
        sc["rrow"][:, 0:Tc], sc["lrow"][:, 0:Tc], AF.Exp, scale=-0.5), incs=[(SS, 1)])
    r_tok = SS.now()
    sc["stat_free"] = [m_tok, v_tok]
    if fp32s:
        p.em("tensor", [m_tok] + sc["bc_free"], lambda e: e.matmul(
            sc["pmb"][:, 0:Tc], g["onesk_f"][:, 0:128], sc["mrow"][:, 0:Tc],
            start=True, stop=True), incs=[(TS, 1)])
        p.em("tensor", [r_tok], lambda e: e.matmul(
            sc["prb"][:, 0:Tc], g["onesk_f"][:, 0:128], sc["rrow"][:, 0:Tc],
            start=True, stop=True), incs=[(TS, 1)])
        bc_tok = TS.now()
    else:
        p.em("vector", [m_tok], lambda e: e.tensor_copy(
            sc["mrow_b"][:, 0:Tc], sc["mrow"][:, 0:Tc]), incs=[(VS, 1)])
        mb_tok = VS.now()
        p.em("scalar", [r_tok], lambda e: e.copy(
            sc["rrow_b"][:, 0:Tc], sc["rrow"][:, 0:Tc]), incs=[(SS, 1)])
        rb_tok = SS.now()
        p.em("tensor", [mb_tok] + sc["bc_free"], lambda e: e.matmul(
            sc["pmb"][:, 0:Tc], g["onesk"][:, 0:128], sc["mrow_b"][:, 0:Tc],
            start=True, stop=True), incs=[(TS, 1)])
        p.em("tensor", [rb_tok], lambda e: e.matmul(
            sc["prb"][:, 0:Tc], g["onesk"][:, 0:128], sc["rrow_b"][:, 0:Tc],
            start=True, stop=True), incs=[(TS, 1)])
        bc_tok = TS.now()
    x32 = sc.get("xln32_out")
    for h in range(HT):
        dt_v = sc["d32"] if x32 is not None else sc["dtile"]
        p.em("vector", [bc_tok] if h == 0 else [],
             lambda e, h=h, dt_v=dt_v: e.tensor_sub(dt_v[:, h * Tc:(h + 1) * Tc],
                                                    xr[:, h * Tc:(h + 1) * Tc],
                                                    sc["pmb"][:, 0:Tc]),
             incs=[])
        p.em("vector", sc.get("out_free", []) if h == 0 else [],
             lambda e, h=h, dt_v=dt_v: e.tensor_mul(x_ln_out[:, h * Tc:(h + 1) * Tc],
                                                    dt_v[:, h * Tc:(h + 1) * Tc],
                                                    sc["prb"][:, 0:Tc]),
             incs=[(VS, 1)])
        if x32 is not None:
            p.em("vector", [],
                 lambda e, h=h, dt_v=dt_v: e.tensor_mul(x32[:, h * Tc:(h + 1) * Tc],
                                                        dt_v[:, h * Tc:(h + 1) * Tc],
                                                        sc["prb"][:, 0:Tc]),
                 incs=[(VS, 1)])
    done = VS.now()
    sc["out_free"] = []
    sc["bc_free"] = [done]
    sc["xb_free"] = [pm_tok]
    sc["xsq_free"] = [pv_tok, done]
    return done



DEBUG_COMBT = False


def build_program(upto="all"):
    nc = bass.Bass("TRN2", target_bir_lowering=False)
    dram = {}

    def din(name, shape, dtype=F32):
        dram[name] = nc.dram_tensor(name, shape, dtype, kind="ExternalInput").ap()
        return dram[name]

    def dint(name, shape, dtype):
        dram[name] = nc.dram_tensor(name, shape, dtype, kind="Internal").ap()
        return dram[name]

    x0 = din("x0", [NB, H, L])
    din("W_in_T", [HT, 128, 2 * INNER])
    din("cw_p", [128, CT * 4])
    din("cb_p", [128, CT])
    din("W_BC_T", [CT, 128, 32])
    din("W_dtin_T", [CT, 128, INNER])
    din("W_dt_T", [CT, 128, INNER])
    din("bdt_p", [128, CT])
    din("dv_p", [128, CT])
    din("W_outA_T", [CT, 128, H])
    din("Wq_T", [HT, 128, H])
    din("Wk_T", [HT, 128, H])
    din("Wv_T", [HT, 128, H])
    din("W_outB_T", [HT, 128, H])
    din("Wr_T", [HT, 128, NE])
    din("e_bias", [1, NE])
    din("Wg_T", [HT, 128, IS])
    din("Wu_T", [HT, 128, IS])
    din("Wd_T", [IS // 128, 128, H])
    din("gup_T", [NE, HT, 128, 2 * I_EXP])
    din("down_T", [NE, 4, 128, H])

    xout = nc.dram_tensor("xout", [NB, H, L], F32, kind="ExternalOutput").ap()
    if DEBUG_COMBT:
        dram["dbg_combT"] = nc.dram_tensor("dbg_combT", [32, T], BF16,
                                           kind="ExternalOutput").ap()
        dram["dbg_xln"] = nc.dram_tensor("dbg_xln", [HT, 128, T], BF16,
                                         kind="ExternalOutput").ap()
        dram["dbg_sh"] = nc.dram_tensor("dbg_sh", [HT, 128, T], BF16,
                                        kind="ExternalOutput").ap()
        dram["dbg_acc"] = nc.dram_tensor("dbg_acc", [128, HT * L], F32,
                                         kind="ExternalOutput").ap()
        dram["dbg_hm"] = nc.dram_tensor("dbg_hm", [128, 4 * 512], BF16,
                                        kind="ExternalOutput").ap()
        dram["dbg_sg"] = nc.dram_tensor("dbg_sg", [128, 4 * 512], BF16,
                                        kind="ExternalOutput").ap()
    xa = dint("xa", [NB, H, L], F32)
    xb_h = dint("xb", [NB, H, L], F32)
    dint("sp_xconv", [NB, CT, 128, L], BF16)
    dint("sp_z", [NB, CT, 128, L], BF16)
    dint("sp_dtin", [NB, CT, 128, L], BF16)
    dint("sp_xln", [HT, 128, T], BF16)
    dint("sp_shared", [HT, 128, T], BF16)

    with ExitStack() as top:
        EC = top.enter_context
        sems = {}
        for nm in ("SD", "GD", "CD", "TS", "VS", "SS", "GS"):
            sems[nm] = Ctr(EC(nc.semaphore(nm)))
        p = Prog(nc)
        c = dict(p=p, nc=nc, dram=dram, sems=sems,
                 SD=sems["SD"], GD=sems["GD"], CD=sems["CD"], TS=sems["TS"],
                 VS=sems["VS"], SS=sems["SS"], GS=sems["GS"])

        g = {}
        g["onesk"] = EC(nc.sbuf_tensor("onesk", [1, 128], BF16))
        g["onem"] = EC(nc.sbuf_tensor("onem", [128, 1], BF16))
        g["selBC"] = EC(nc.sbuf_tensor("selBC", [32, 2], BF16))
        g["sel2"] = EC(nc.sbuf_tensor("sel2", [2, 256], BF16))
        g["ident"] = EC(nc.sbuf_tensor("ident", [128, 128], BF16))
        g["cw"] = EC(nc.sbuf_tensor("cw", [128, CT * 4], F32))
        g["cb"] = EC(nc.sbuf_tensor("cb", [128, CT], F32))
        g["bdt"] = EC(nc.sbuf_tensor("bdt", [128, CT], F32))
        g["dvec"] = EC(nc.sbuf_tensor("dvec", [128, CT], F32))
        g["ebias_bc"] = EC(nc.sbuf_tensor("ebias_bc", [128, NE], F32))
        g["onesk_f"] = EC(nc.sbuf_tensor("onesk_f", [1, 128], F32))
        g["cbias"] = EC(nc.sbuf_tensor("cbias", [128, 2], F32))
        g["onem_f"] = EC(nc.sbuf_tensor("onem_f", [128, 1], F32))
        g["ebias_row"] = EC(nc.sbuf_tensor("ebias_row", [1, NE], F32))
        c["g"] = g

        build_constants(p, c)

        phase_mamba(p, nc, c, x_in=x0, x_out=xa,
                    final=(upto == "mamba"), xout=xout)

        if upto in ("ffn1", "attn", "all"):
            phase_ffn(p, nc, c, x_in=xa, x_out=xb_h,
                      final=(upto == "ffn1"), xout=xout, tag="f1")
        if upto in ("attn", "all"):
            phase_attn(p, nc, c, x_in=xb_h, x_out=xa,
                       final=(upto == "attn"), xout=xout, tag="at")
        if upto == "all":
            phase_ffn(p, nc, c, x_in=xa, x_out=xout,
                      final=False, xout=None, tag="f2")

        SD, GD, CD = sems["SD"], sems["GD"], sems["CD"]
        p.em("sync", [SD.now(), GD.now(), CD.now()],
             lambda e: e.wait_ge(SD.h, SD.n))

        block = EC(nc.Block())
        p.run(block)
    return nc


def build_constants(p, c):
    nc, g, dram = c["nc"], c["g"], c["dram"]
    GS, VS, SD, GD = c["GS"], c["VS"], c["SD"], c["GD"]
    with ExitStack() as st:
        iosc = st.enter_context(nc.sbuf_tensor("iosc", [128, 512], F32))
        p.em("gpsimd", [], lambda e: e.iota(
            iosc[:, 0:128], pattern=[[1, 128]], base=0, channel_multiplier=-1,
            allow_small_or_imprecise_dtypes=True), incs=[(GS, 1)])
        p.em("vector", [GS.now()], lambda e: e.tensor_single_scalar(
            g["ident"][:], iosc[:, 0:128], 0.0, ALU.is_equal), incs=[(VS, 1)])
        p.em("vector", [], lambda e: e.memset(g["onesk"][:, :], 1.0), incs=[(VS, 1)])
        p.em("vector", [], lambda e: e.memset(g["onesk_f"][:, :], 1.0), incs=[(VS, 1)])
        p.em("vector", [], lambda e: e.memset(g["onem"][:, :], 1.0 / 1024.0),
             incs=[(VS, 1)])
        p.em("vector", [], lambda e: e.memset(g["cbias"][:, 0:1], EPS), incs=[(VS, 1)])
        p.em("vector", [], lambda e: e.memset(g["onem_f"][:, :], 1.0 / 1024.0),
             incs=[(VS, 1)])
        p.em("vector", [], lambda e: e.memset(g["cbias"][:, 1:2], 1.0), incs=[(VS, 1)])
        p.em("gpsimd", [], lambda e: e.iota(
            iosc[0:32, 128:130], pattern=[[0, 2]], base=0, channel_multiplier=1,
            allow_small_or_imprecise_dtypes=True), incs=[(GS, 1)])
        p.em("vector", [GS.now()], lambda e: e.tensor_single_scalar(
            iosc[0:32, 130:132], iosc[0:32, 128:130], 16.0, ALU.is_lt), incs=[(VS, 1)])
        tk2 = VS.now()
        p.em("vector", [tk2], lambda e: e.tensor_scalar(
            g["selBC"][0:32, 0:1], iosc[0:32, 130:131], 1.0 / 16.0, None, ALU.mult),
            incs=[(VS, 1)])
        p.em("vector", [tk2], lambda e: e.tensor_scalar(
            iosc[0:32, 132:133], iosc[0:32, 131:132], -1.0, 1.0, ALU.mult, ALU.add),
            incs=[(VS, 1)])
        p.em("vector", [VS.now()], lambda e: e.tensor_scalar(
            g["selBC"][0:32, 1:2], iosc[0:32, 132:133], 1.0 / 16.0, None, ALU.mult),
            incs=[(VS, 1)])
        p.em("gpsimd", [], lambda e: e.iota(
            iosc[0:2, 256:512], pattern=[[1, 2], [0, 128]], base=0,
            channel_multiplier=-1, allow_small_or_imprecise_dtypes=True),
            incs=[(GS, 1)])
        p.em("vector", [GS.now()], lambda e: e.tensor_single_scalar(
            g["sel2"][:], iosc[0:2, 256:512], 0.0, ALU.is_equal), incs=[(VS, 1)])
        for nm, dst in (("cw_p", g["cw"]), ("cb_p", g["cb"]), ("bdt_p", g["bdt"]),
                        ("dv_p", g["dvec"])):
            p.em("sync", [], lambda e, nm=nm, dst=dst: e.dma_start(
                out=dst[:, :], in_=dram[nm][:, :]), incs=[(SD, 16)])
        p.em("sync", [], lambda e: e.dma_start(
            out=g["ebias_row"][:, :], in_=dram["e_bias"][:, :]), incs=[(SD, 16)])
        p.join([VS.now(), GS.now(), SD.now()])



def phase_mamba(p, nc, c, x_in, x_out, final=False, xout=None):
    dram, g = c["dram"], c["g"]
    SD, GD, CD = c["SD"], c["GD"], c["CD"]
    TS, VS, SS, GS = c["TS"], c["VS"], c["SS"], c["GS"]
    Tc = 512
    NCH = L // Tc

    with ExitStack() as mst:
        g["bgcg"] = mst.enter_context(nc.sbuf_tensor("bgcg", [2, T], BF16))
        _phase_mamba_body(p, nc, c, x_in, x_out, final, xout, Tc, NCH)
    p.join([c["TS"].now(), c["VS"].now(), c["SS"].now(), c["GS"].now(),
            c["SD"].now(), c["GD"].now(), c["CD"].now()])


def _phase_mamba_body(p, nc, c, x_in, x_out, final, xout, Tc, NCH):
    TS, VS, SS, GS = c["TS"], c["VS"], c["SS"], c["GS"]
    SD, GD, CD = c["SD"], c["GD"], c["CD"]
    _m1(p, nc, c, x_in, Tc, NCH)
    p.join([TS.now(), VS.now(), SS.now(), GS.now(), SD.now(), GD.now(), CD.now()])
    _m2a(p, nc, c, Tc, NCH)
    p.join([TS.now(), VS.now(), SS.now(), GS.now(), SD.now(), GD.now(), CD.now()])
    _m2b(p, nc, c, x_in, x_out, final, xout)


def _m1(p, nc, c, x_in, Tc, NCH):
    dram, g = c["dram"], c["g"]
    SD, GD, CD = c["SD"], c["GD"], c["CD"]
    TS, VS, SS, GS = c["TS"], c["VS"], c["SS"], c["GS"]

    with ExitStack() as st:
        EC = st.enter_context
        W = EC(nc.sbuf_tensor("m1_W", [128, HT * 2 * INNER], BF16))
        xr = EC(nc.sbuf_tensor("m1_xr", [128, 2 * HT * Tc], F32))
        xlnb = EC(nc.sbuf_tensor("m1_xln", [128, HT * Tc], BF16))
        xbq = EC(nc.sbuf_tensor("m1_xbq", [128, 2 * HT * Tc], BF16))
        rows = EC(nc.sbuf_tensor("m1_rows", [1, 5 * Tc], F32))
        rows_b = EC(nc.sbuf_tensor("m1_rows_b", [1, 2 * Tc], BF16))
        xp = EC(nc.sbuf_tensor("m1_xp", [128, CT * (Tc + 3)], BF16))
        xcv = EC(nc.sbuf_tensor("m1_xcv", [128, CT * Tc], BF16))
        zs = EC(nc.sbuf_tensor("m1_zs", [128, CT * Tc], BF16))
        cacc = EC(nc.sbuf_tensor("m1_cacc", [128, 2 * Tc], F32))
        pmm = [EC(nc.psum_tensor(f"m1_p{i}", [128, Tc], F32)) for i in range(4)]
        pst = [EC(nc.psum_tensor(f"m1_s{i}", [1, Tc], F32)) for i in range(2)]
        pbc = [EC(nc.psum_tensor(f"m1_b{i}", [128, Tc], F32)) for i in range(2)]

        p.em("gpsimd", [], lambda e: e.dma_start(
            out=W[:, :], in_=dram["W_in_T"].rearrange("k p f -> p k f")),
            incs=[(GD, 16)])
        w_tok = GD.now()

        lnsc = dict(onesk=g["onesk"], onem=g["onem"], pm=pst[0], pv=pst[1],
                    pmb=pbc[0], prb=pbc[1], bc_free=[], stat_free=[])
        bank_free = {i: [] for i in range(4)}
        xr_free = {0: [], 1: []}
        xp_free = {cix: [] for cix in range(CT)}
        zs_guard = []
        xcv_guard = []

        for b in range(NB):
            p.em("vector", [], lambda e: e.memset(
                xp.rearrange("p (c t) -> p c t", c=CT)[:, :, 0:3], 0.0),
                incs=[(VS, 1)])
            halo0 = VS.now()
            for ch in range(NCH):
                buf = ch % 2
                t0 = ch * Tc
                xr_v = xr[:, buf * HT * Tc:(buf + 1) * HT * Tc]
                p.em("sync", xr_free[buf], lambda e, b=b, t0=t0, xr_v=xr_v: e.dma_start(
                    out=xr_v,
                    in_=x_in[b, :, t0:t0 + Tc].rearrange("(h p) t -> p h t", p=128)),
                    incs=[(SD, 16)])
                xr_tok = SD.now()

                lnsc.update(
                    xb=xbq[:, 0:HT * Tc], xsq=xbq[:, HT * Tc:2 * HT * Tc],
                    mrow=rows[:, 0:Tc], msq=rows[:, Tc:2 * Tc],
                    vrow=rows[:, 2 * Tc:3 * Tc], lrow=rows[:, 3 * Tc:4 * Tc],
                    rrow=rows[:, 4 * Tc:5 * Tc],
                    mrow_b=rows_b[:, 0:Tc], rrow_b=rows_b[:, Tc:2 * Tc],
                    dtile=xbq[:, HT * Tc:2 * HT * Tc],
                    xr_ready=[xr_tok],
                )
                xln_tok = emit_ln(p, c, xr_v, xlnb, Tc, lnsc)
                xr_free[buf] = [xln_tok]

                evac_tok = {}
                for o in range(32):
                    bank = o % 4
                    for k in range(HT):
                        waits = []
                        if k == 0:
                            waits += bank_free[bank]
                            if o == 0:
                                waits += [xln_tok, w_tok]
                        p.em("tensor", waits,
                             lambda e, o=o, k=k, bank=bank: e.matmul(
                                 pmm[bank][:],
                                 W[:, (k * 32 + o) * 128:(k * 32 + o + 1) * 128],
                                 xlnb[:, k * Tc:(k + 1) * Tc],
                                 start=(k == 0), stop=(k == HT - 1)),
                             incs=[(TS, 1)] if k == HT - 1 else [])
                    mm_tok = TS.now()
                    if o < CT:
                        dst = xp[:, o * (Tc + 3) + 3:o * (Tc + 3) + 3 + Tc]
                        w2 = [mm_tok] + xp_free[o]
                        if ch == 0 and o == 0:
                            w2.append(halo0)
                        p.em("scalar", w2,
                             lambda e, dst=dst, bank=bank: e.copy(dst, pmm[bank][:]),
                             incs=[(SS, 1)])
                    else:
                        cix = o - CT
                        p.em("scalar", [mm_tok] + zs_guard,
                             lambda e, cix=cix, bank=bank: e.activation(
                                 zs[:, cix * Tc:(cix + 1) * Tc], pmm[bank][:], AF.Silu),
                             incs=[(SS, 1)])
                        zs_guard = []
                    evac_tok[o] = SS.now()
                    bank_free[bank] = [evac_tok[o]]

                p.em("scalar", [evac_tok[31]], lambda e, b=b, t0=t0: e.dma_start(
                    out=dram["sp_z"][b, :, :, t0:t0 + Tc].rearrange("c p t -> p c t"),
                    in_=zs[:, :]), incs=[(CD, 16)])
                zst_tok = CD.now()

                conv_last = None
                acc_free = {0: [], 1: []}
                for cix in range(CT):
                    base = cix * (Tc + 3)
                    par = cix % 2
                    eng = "vector"
                    ECt = VS
                    acc = cacc[:, par * Tc:(par + 1) * Tc]
                    p.em(eng, [evac_tok[cix]] + acc_free[par],
                         lambda e, cix=cix, base=base, acc=acc: e.tensor_scalar(
                             acc, xp[:, base + 0:base + 0 + Tc],
                             g["cw"][:, cix * 4 + 0:cix * 4 + 1], None, ALU.mult),
                         incs=[])
                    for k in range(1, 4):
                        p.em(eng, [],
                             lambda e, cix=cix, k=k, base=base, acc=acc:
                             e.scalar_tensor_tensor(
                                 acc, xp[:, base + k:base + k + Tc],
                                 g["cw"][:, cix * 4 + k:cix * 4 + k + 1], acc,
                                 ALU.mult, ALU.add),
                             incs=[(ECt, 1)] if k == 3 else [])
                    cv_tok = ECt.now()
                    p.em("scalar", [cv_tok] + xcv_guard,
                         lambda e, cix=cix, acc=acc: e.activation(
                             xcv[:, cix * Tc:(cix + 1) * Tc], acc, AF.Silu,
                             bias=g["cb"][:, cix:cix + 1]),
                         incs=[(SS, 1)])
                    xcv_guard = []
                    conv_last = SS.now()
                    acc_free[par] = [conv_last]
                    if ch < NCH - 1:
                        p.em(eng, [],
                             lambda e, base=base: e.tensor_copy(
                                 xp[:, base:base + 3], xp[:, base + Tc:base + Tc + 3]),
                             incs=[(ECt, 1)])
                        xp_free[cix] = [ECt.now()]
                    else:
                        xp_free[cix] = [cv_tok]
                p.em("scalar", [conv_last], lambda e, b=b, t0=t0: e.dma_start(
                    out=dram["sp_xconv"][b, :, :, t0:t0 + Tc].rearrange("c p t -> p c t"),
                    in_=xcv[:, :]), incs=[(CD, 16)])
                zs_guard = [zst_tok]
                xcv_guard = [CD.now()]

    p.join([TS.now(), VS.now(), SS.now(), GS.now(), SD.now(), GD.now(), CD.now()])

def _m2a(p, nc, c, Tc, NCH):
    dram, g = c["dram"], c["g"]
    SD, GD, CD = c["SD"], c["GD"], c["CD"]
    TS, VS, SS, GS = c["TS"], c["VS"], c["SS"], c["GS"]

    with ExitStack() as st:
        EC = st.enter_context
        W = EC(nc.sbuf_tensor("m2a_W", [128, CT * INNER], BF16))
        Wbc = EC(nc.sbuf_tensor("m2a_Wbc", [128, CT * 32], BF16))
        xcv = EC(nc.sbuf_tensor("m2a_xcv", [128, 2 * CT * Tc], BF16))
        dti = EC(nc.sbuf_tensor("m2a_dti", [128, CT * Tc], BF16))
        bcv = EC(nc.sbuf_tensor("m2a_bcv", [32, Tc], BF16))
        pmm = [EC(nc.psum_tensor(f"m2a_p{i}", [128, Tc], F32)) for i in range(6)]
        pbcm = EC(nc.psum_tensor("m2a_bc", [32, Tc], F32))
        pmean = EC(nc.psum_tensor("m2a_mean", [2, Tc], F32))

        p.em("gpsimd", [], lambda e: e.dma_start(
            out=W[:, :], in_=dram["W_dtin_T"].rearrange("k p f -> p k f")),
            incs=[(GD, 16)])
        p.em("gpsimd", [], lambda e: e.dma_start(
            out=Wbc[:, :], in_=dram["W_BC_T"].rearrange("k p f -> p k f")),
            incs=[(GD, 16)])
        w_tok = GD.now()

        bank_free = {i: [] for i in range(6)}
        xcv_free = {0: [], 1: []}
        bc_free = []
        mean_free = []
        store_free = []

        for b in range(NB):
            for ch in range(NCH):
                buf = ch % 2
                t0 = ch * Tc
                xcv_v = xcv[:, buf * CT * Tc:(buf + 1) * CT * Tc]
                p.em("sync", xcv_free[buf], lambda e, b=b, t0=t0, xcv_v=xcv_v:
                     e.dma_start(
                         out=xcv_v,
                         in_=dram["sp_xconv"][b, :, :, t0:t0 + Tc]
                         .rearrange("c p t -> p c t")),
                     incs=[(SD, 16)])
                x_tok = SD.now()

                evac_tok = {}
                for o in range(CT):
                    bank = o % 6
                    for k in range(CT):
                        waits = []
                        if k == 0:
                            waits += bank_free[bank]
                            if o == 0:
                                waits += [x_tok, w_tok]
                        p.em("tensor", waits,
                             lambda e, o=o, k=k, bank=bank, xcv_v=xcv_v: e.matmul(
                                 pmm[bank][:],
                                 W[:, (k * CT + o) * 128:(k * CT + o + 1) * 128],
                                 xcv_v[:, k * Tc:(k + 1) * Tc],
                                 start=(k == 0), stop=(k == CT - 1)),
                             incs=[(TS, 1)] if k == CT - 1 else [])
                    mm_tok = TS.now()
                    p.em("scalar", [mm_tok] + (store_free if o == 0 else []),
                         lambda e, o=o, bank=bank: e.copy(
                             dti[:, o * Tc:(o + 1) * Tc], pmm[bank][:]),
                         incs=[(SS, 1)])
                    evac_tok[o] = SS.now()
                    bank_free[bank] = [evac_tok[o]]
                    if o == 0:
                        store_free = []
                for k in range(CT):
                    p.em("tensor", bc_free if k == 0 else [],
                         lambda e, k=k, xcv_v=xcv_v: e.matmul(
                             pbcm[:], Wbc[:, k * 32:(k + 1) * 32],
                             xcv_v[:, k * Tc:(k + 1) * Tc],
                             start=(k == 0), stop=(k == CT - 1)),
                         incs=[(TS, 1)] if k == CT - 1 else [])
                bcm_tok = TS.now()
                p.em("scalar", [bcm_tok], lambda e: e.copy(bcv[:, :], pbcm[:]),
                     incs=[(SS, 1)])
                bcv_tok = SS.now()
                bc_free = [bcv_tok]
                p.em("tensor", [bcv_tok] + mean_free, lambda e: e.matmul(
                    pmean[:], g["selBC"][:, :], bcv[:, :], start=True, stop=True),
                    incs=[(TS, 1)])
                mean_tok = TS.now()
                p.em("scalar", [mean_tok], lambda e, b=b, t0=t0: e.activation(
                    g["bgcg"][0:2, b * L + t0:b * L + t0 + Tc], pmean[:], AF.Sigmoid),
                    incs=[(SS, 1)])
                mean_free = [SS.now()]
                xcv_free[buf] = [mm_tok]
                p.em("scalar", [evac_tok[CT - 1]], lambda e, b=b, t0=t0: e.dma_start(
                    out=dram["sp_dtin"][b, :, :, t0:t0 + Tc].rearrange("c p t -> p c t"),
                    in_=dti[:, :]), incs=[(CD, 16)])
                store_free = [CD.now()]


def _m2b(p, nc, c, x_in, x_out, final, xout):
    dram, g = c["dram"], c["g"]
    SD, GD, CD = c["SD"], c["GD"], c["CD"]
    TS, VS, SS, GS = c["TS"], c["VS"], c["SS"], c["GS"]

    Tc2 = 256
    NCH2 = L // Tc2
    with ExitStack() as st:
        EC = st.enter_context
        Wdt = EC(nc.sbuf_tensor("m2b_Wdt", [128, CT * INNER], BF16))
        Wo = EC(nc.sbuf_tensor("m2b_Wo", [128, CT * H], BF16))
        dti = EC(nc.sbuf_tensor("m2b_dti", [128, CT * Tc2], BF16))
        xcv = EC(nc.sbuf_tensor("m2b_xcv", [128, CT * Tc2], BF16))
        zsb = EC(nc.sbuf_tensor("m2b_zs", [128, CT * Tc2], BF16))
        xrs = EC(nc.sbuf_tensor("m2b_xrs", [128, HT * Tc2], F32))
        U = EC(nc.sbuf_tensor("m2b_U", [128, CT * Tc2], F32))
        A = EC(nc.sbuf_tensor("m2b_A", [128, CT * Tc2], F32))
        Hs = EC(nc.sbuf_tensor("m2b_H", [128, CT * Tc2], F32))
        YB = EC(nc.sbuf_tensor("m2b_YB", [128, 2 * CT * Tc2], BF16))
        carry = EC(nc.sbuf_tensor("m2b_carry", [128, CT], F32))
        pmm = [EC(nc.psum_tensor(f"m2b_p{i}", [128, Tc2], F32)) for i in range(4)]
        pout = [EC(nc.psum_tensor(f"m2b_o{i}", [128, Tc2], F32)) for i in range(2)]
        pg = [EC(nc.psum_tensor(f"m2b_g{i}", [128, Tc2], F32)) for i in range(2)]

        p.em("gpsimd", [], lambda e: e.dma_start(
            out=Wdt[:, :], in_=dram["W_dt_T"].rearrange("k p f -> p k f")),
            incs=[(GD, 16)])
        p.em("gpsimd", [], lambda e: e.dma_start(
            out=Wo[:, :], in_=dram["W_outA_T"].rearrange("k p f -> p k f")),
            incs=[(GD, 16)])
        w_tok = GD.now()

        bank_free = {i: [] for i in range(4)}
        pout_free = {i: [] for i in range(2)}
        dti_free = {0: [], 1: []}
        xcv_free = []
        zs_free = []
        xr_free = []
        yb_free = {0: [], 1: []}
        chain_free = []
        pg_free = []
        n2 = CT * Tc2
        pend = {"v": None}

        def _emit_outproj(state):
            b2, t02, par2, y2 = state
            yv = YB[:, par2 * n2:(par2 + 1) * n2]
            p.em("sync", xr_free, lambda e, b2=b2, t02=t02: e.dma_start(
                out=xrs[:, :],
                in_=x_in[b2, :, t02:t02 + Tc2].rearrange("(h p) t -> p h t", p=128)),
                incs=[(SD, 16)])
            xr_tok = SD.now()
            add_toks = []
            for o in range(HT):
                bank = o % 2
                for k in range(CT):
                    waits = []
                    if k == 0:
                        waits += pout_free[bank]
                        if o == 0:
                            waits.append(y2)
                    p.em("tensor", waits,
                         lambda e, o=o, k=k, bank=bank, yv=yv: e.matmul(
                             pout[bank][:],
                             Wo[:, (k * HT + o) * 128:(k * HT + o + 1) * 128],
                             yv[:, k * Tc2:(k + 1) * Tc2],
                             start=(k == 0), stop=(k == CT - 1)),
                         incs=[(TS, 1)] if k == CT - 1 else [])
                mo_tok = TS.now()
                p.em("vector", [mo_tok, xr_tok],
                     lambda e, o=o, bank=bank: e.tensor_add(
                         xrs[:, o * Tc2:(o + 1) * Tc2],
                         xrs[:, o * Tc2:(o + 1) * Tc2], pout[bank][:]),
                     incs=[(VS, 1)])
                add_toks.append(VS.now())
                pout_free[bank] = [add_toks[-1]]
            yb_free[par2] = [TS.now()]
            p.em("sync", [add_toks[-1]], lambda e, b2=b2, t02=t02: e.dma_start(
                out=x_out[b2, :, t02:t02 + Tc2].rearrange("(h p) t -> p h t", p=128),
                in_=xrs[:, :]), incs=[(SD, 16)])
            toks = [SD.now()]
            if final and xout is not None:
                p.em("sync", [], lambda e, b2=b2, t02=t02: e.dma_start(
                    out=xout[b2, :, t02:t02 + Tc2].rearrange("(h p) t -> p h t", p=128),
                    in_=xrs[:, :]), incs=[(SD, 16)])
                toks.append(SD.now())
            return toks

        for b in range(NB):
            p.em("vector", list(chain_free), lambda e: e.memset(carry[:, :], 0.0),
                 incs=[(VS, 1)])
            carry_tok = VS.now()
            for ch in range(NCH2):
                buf = ch % 2
                par = (b * NCH2 + ch) % 2
                t0 = ch * Tc2
                dti_v = dti[:, 0:n2]
                xcv_v = xcv[:, 0:n2]
                p.em("sync", dti_free[0], lambda e, b=b, t0=t0, dti_v=dti_v:
                     e.dma_start(out=dti_v,
                                 in_=dram["sp_dtin"][b, :, :, t0:t0 + Tc2]
                                 .rearrange("c p t -> p c t")),
                     incs=[(SD, 16)])
                dti_tok = SD.now()
                p.em("sync", xcv_free, lambda e, b=b, t0=t0, xcv_v=xcv_v:
                     e.dma_start(out=xcv_v,
                                 in_=dram["sp_xconv"][b, :, :, t0:t0 + Tc2]
                                 .rearrange("c p t -> p c t")),
                     incs=[(SD, 16)])
                xcv_tok = SD.now()
                p.em("sync", zs_free, lambda e, b=b, t0=t0: e.dma_start(
                    out=zsb[:, :],
                    in_=dram["sp_z"][b, :, :, t0:t0 + Tc2].rearrange("c p t -> p c t")),
                    incs=[(SD, 16)])
                zs_tok = SD.now()

                evac_tok = {}
                for o in range(CT):
                    bank = o % 4
                    for k in range(CT):
                        waits = []
                        if k == 0:
                            waits += bank_free[bank]
                            if o == 0:
                                waits += [dti_tok, w_tok]
                        p.em("tensor", waits,
                             lambda e, o=o, k=k, bank=bank, dti_v=dti_v: e.matmul(
                                 pmm[bank][:],
                                 Wdt[:, (k * CT + o) * 128:(k * CT + o + 1) * 128],
                                 dti_v[:, k * Tc2:(k + 1) * Tc2],
                                 start=(k == 0), stop=(k == CT - 1)),
                             incs=[(TS, 1)] if k == CT - 1 else [])
                    mm_tok = TS.now()
                    p.em("scalar", [mm_tok] + (chain_free if o == 0 else []),
                         lambda e, o=o, bank=bank: e.activation(
                             U[:, o * Tc2:(o + 1) * Tc2], pmm[bank][:],
                             AF.Identity, bias=g["bdt"][:, o:o + 1]),
                         incs=[(SS, 1)])
                    evac_tok[o] = SS.now()
                    bank_free[bank] = [evac_tok[o]]
                u_tok = evac_tok[CT - 1]
                dti_free[0] = [mm_tok]
                off = b * L + t0
                p.em("tensor", pg_free, lambda e, off=off: e.matmul(
                    pg[0][:], g["sel2"][:, 0:128], g["bgcg"][0:2, off:off + Tc2],
                    start=True, stop=True), incs=[(TS, 1)])
                p.em("tensor", [], lambda e, off=off: e.matmul(
                    pg[1][:], g["sel2"][:, 128:256], g["bgcg"][0:2, off:off + Tc2],
                    start=True, stop=True), incs=[(TS, 1)])
                gbc_tok = TS.now()

                if pend["v"] is not None:
                    xr_free = _emit_outproj(pend["v"])
                    pend["v"] = None

                p.em("scalar", [u_tok], lambda e: e.activation(
                    A[:, 0:n2], U[:, 0:n2], AF.Abs), incs=[(SS, 1)])
                p.em("scalar", [SS.now()], lambda e: e.activation(
                    A[:, 0:n2], A[:, 0:n2], AF.Exp, scale=-1.0), incs=[(SS, 1)])
                p.em("scalar", [SS.now()], lambda e: e.activation(
                    A[:, 0:n2], A[:, 0:n2], AF.Ln, bias=g["cbias"][:, 1:2]),
                    incs=[(SS, 1)])
                sp_tok = SS.now()
                p.em("vector", [sp_tok, u_tok], lambda e: e.scalar_tensor_tensor(
                    U[:, 0:n2], U[:, 0:n2], 0.0, A[:, 0:n2], ALU.max, ALU.add),
                    incs=[(VS, 1)])
                spu_tok = VS.now()
                p.em("scalar", [spu_tok], lambda e: e.activation(
                    A[:, 0:n2], U[:, 0:n2], AF.Sigmoid, scale=0.1), incs=[(SS, 1)])
                alpha_tok = SS.now()
                p.em("gpsimd", [alpha_tok], lambda e: e.tensor_scalar(
                    U[:, 0:n2], A[:, 0:n2], -1.0, 1.0, ALU.mult, ALU.add),
                    incs=[(GS, 1)])
                g1_tok = GS.now()
                for cix in range(CT):
                    p.em("vector", [gbc_tok, g1_tok] if cix == 0 else [],
                         lambda e, cix=cix: e.tensor_mul(
                             U[:, cix * Tc2:(cix + 1) * Tc2],
                             U[:, cix * Tc2:(cix + 1) * Tc2], pg[0][:]),
                         incs=[(VS, 1)] if cix == CT - 1 else [])
                gb_tok = VS.now()
                p.em("vector", [gb_tok, xcv_tok], lambda e, xcv_v=xcv_v: e.tensor_mul(
                    U[:, 0:n2], U[:, 0:n2], xcv_v[:, 0:n2]), incs=[(VS, 1)])
                gx_tok = VS.now()
                p.em("vector", [], lambda e: e.tensor_scalar(
                    A[:, 0:n2], A[:, 0:n2], 0.01, 0.99, ALU.max, ALU.min),
                    incs=[(VS, 1)])
                ac_tok = VS.now()
                for cix in range(CT):
                    w3 = [ac_tok, gx_tok] if cix == 0 else []
                    if ch == 0 and cix == 0:
                        w3.append(carry_tok)
                    p.em("vector", w3,
                         lambda e, cix=cix: e.tensor_tensor_scan(
                             Hs[:, cix * Tc2:(cix + 1) * Tc2],
                             A[:, cix * Tc2:(cix + 1) * Tc2],
                             U[:, cix * Tc2:(cix + 1) * Tc2],
                             carry[:, cix:cix + 1], ALU.mult, ALU.add),
                         incs=[(VS, 1)])
                    p.em("vector", [VS.now()],
                         lambda e, cix=cix: e.tensor_copy(
                             carry[:, cix:cix + 1],
                             Hs[:, (cix + 1) * Tc2 - 1:(cix + 1) * Tc2]),
                         incs=[(VS, 1)])
                for cix in range(CT):
                    p.em("vector", [],
                         lambda e, cix=cix: e.tensor_mul(
                             Hs[:, cix * Tc2:(cix + 1) * Tc2],
                             Hs[:, cix * Tc2:(cix + 1) * Tc2], pg[1][:]),
                         incs=[])
                for cix in range(CT):
                    p.em("vector", [],
                         lambda e, cix=cix, xcv_v=xcv_v: e.scalar_tensor_tensor(
                             U[:, cix * Tc2:(cix + 1) * Tc2],
                             xcv_v[:, cix * Tc2:(cix + 1) * Tc2],
                             g["dvec"][:, cix:cix + 1],
                             Hs[:, cix * Tc2:(cix + 1) * Tc2],
                             ALU.mult, ALU.add),
                         incs=[(VS, 1)] if cix == CT - 1 else [])
                gd_tok = VS.now()
                yv = YB[:, par * n2:(par + 1) * n2]
                p.em("gpsimd", [gd_tok, zs_tok] + yb_free[par],
                     lambda e, yv=yv: e.tensor_mul(
                         yv[:, 0:n2], U[:, 0:n2], zsb[:, 0:n2]), incs=[(GS, 1)])
                y_tok = GS.now()
                pg_free = [y_tok] if False else [gd_tok, y_tok]
                chain_free = [y_tok]
                zs_free = [y_tok]
                xcv_free = [y_tok]
                pend["v"] = (b, t0, par, y_tok)
            xr_free = _emit_outproj(pend["v"])
            pend["v"] = None

    p.join([TS.now(), VS.now(), SS.now(), GS.now(), SD.now(), GD.now(), CD.now()])



def phase_ffn(p, nc, c, x_in, x_out, final, xout, tag):
    TS, VS, SS, GS = c["TS"], c["VS"], c["SS"], c["GS"]
    SD, GD, CD = c["SD"], c["GD"], c["CD"]
    with ExitStack() as fst:
        EC = fst.enter_context
        combT = EC(nc.sbuf_tensor(tag + "_combT", [32, T], BF16))
        indE = EC(nc.sbuf_tensor(tag + "_indE", [32, NE * 128], BF16))
        identF = EC(nc.sbuf_tensor(tag + "_idF", [128, 128], F32))
        fg = dict(combT=combT, indE=indE, identF=identF)
        with ExitStack() as st2:
            iosc = st2.enter_context(nc.sbuf_tensor(tag + "_iosc", [128, NE * 128], F32))
            p.em("gpsimd", [], lambda e: e.iota(
                iosc[0:32, :], pattern=[[1, NE], [0, 128]], base=0,
                channel_multiplier=-1, allow_small_or_imprecise_dtypes=True),
                incs=[(GS, 1)])
            p.em("vector", [GS.now()], lambda e: e.tensor_single_scalar(
                indE[:], iosc[0:32, :], 0.0, ALU.is_equal), incs=[(VS, 1)])
            p.em("gpsimd", [VS.now()], lambda e: e.iota(
                iosc[:, 0:128], pattern=[[1, 128]], base=0,
                channel_multiplier=-1, allow_small_or_imprecise_dtypes=True),
                incs=[(GS, 1)])
            p.em("vector", [GS.now()], lambda e: e.tensor_single_scalar(
                identF[:], iosc[:, 0:128], 0.0, ALU.is_equal),
                incs=[(VS, 1)])
            p.join([VS.now(), GS.now()])
        _f0(p, nc, c, fg, x_in, tag)
        if DEBUG_COMBT and tag == "f1":
            p.em("sync", [c["SS"].now(), c["CD"].now()], lambda e: e.dma_start(
                out=c["dram"]["dbg_combT"][:, :], in_=combT[:, :]),
                incs=[(SD, 16)])
            p.em("sync", [], lambda e: e.dma_start(
                out=c["dram"]["dbg_xln"][:, :, :], in_=c["dram"]["sp_xln"][:, :, :]),
                incs=[(SD, 16)])
            p.em("sync", [], lambda e: e.dma_start(
                out=c["dram"]["dbg_sh"][:, :, :], in_=c["dram"]["sp_shared"][:, :, :]),
                incs=[(SD, 16)])
        p.join([TS.now(), VS.now(), SS.now(), GS.now(), SD.now(), GD.now(), CD.now()])
        for hh in range(2):
            _fhalf(p, nc, c, fg, x_in, x_out, final, xout, hh, tag)
            p.join([TS.now(), VS.now(), SS.now(), GS.now(),
                    SD.now(), GD.now(), CD.now()])


def _f0(p, nc, c, fg, x_in, tag):
    dram, g = c["dram"], c["g"]
    SD, GD, CD = c["SD"], c["GD"], c["CD"]
    TS, VS, SS, GS = c["TS"], c["VS"], c["SS"], c["GS"]
    Tc = 512
    NCH = T // Tc
    with ExitStack() as st:
        EC = st.enter_context
        Wg = EC(nc.sbuf_tensor(tag + "0_Wg", [128, HT * IS], BF16))
        Wu = EC(nc.sbuf_tensor(tag + "0_Wu", [128, HT * IS], BF16))
        Wd = EC(nc.sbuf_tensor(tag + "0_Wd", [128, (IS // 128) * H], BF16))
        xr = EC(nc.sbuf_tensor(tag + "0_xr", [128, 2 * HT * Tc], F32))
        xlnb = EC(nc.sbuf_tensor(tag + "0_xln", [128, HT * Tc], BF16))
        xbq = EC(nc.sbuf_tensor(tag + "0_xbq", [128, 2 * HT * Tc], BF16))
        rows = EC(nc.sbuf_tensor(tag + "0_rows", [1, 5 * Tc], F32))
        rows_b = EC(nc.sbuf_tensor(tag + "0_rows_b", [1, 2 * Tc], BF16))
        sg = EC(nc.sbuf_tensor(tag + "0_sg", [128, HT * Tc], BF16))
        su = EC(nc.sbuf_tensor(tag + "0_su", [128, HT * Tc], BF16))
        sh = EC(nc.sbuf_tensor(tag + "0_sh", [128, HT * Tc], BF16))
        sho = EC(nc.sbuf_tensor(tag + "0_sho", [128, HT * Tc], BF16))
        rsc = EC(nc.sbuf_tensor(tag + "0_rsc", [128, 320], F32))
        d32 = EC(nc.sbuf_tensor(tag + "0_d32", [128, HT * Tc], F32))
        xln32 = EC(nc.sbuf_tensor(tag + "0_x32", [128, HT * Tc], F32))
        Wr32 = EC(nc.sbuf_tensor(tag + "0_Wr32", [128, HT * NE], F32))
        pool = [EC(nc.psum_tensor(f"{tag}0_p{i}", [128, Tc], F32)) for i in range(4)]
        pst = [EC(nc.psum_tensor(f"{tag}0_s{i}", [1, Tc], F32)) for i in range(2)]
        pbc = [EC(nc.psum_tensor(f"{tag}0_b{i}", [128, Tc], F32)) for i in range(2)]

        p.em("gpsimd", [], lambda e: e.dma_start(
            out=Wg[:, :], in_=dram["Wg_T"].rearrange("k p f -> p k f")),
            incs=[(GD, 16)])
        p.em("gpsimd", [], lambda e: e.dma_start(
            out=Wu[:, :], in_=dram["Wu_T"].rearrange("k p f -> p k f")),
            incs=[(GD, 16)])
        p.em("gpsimd", [], lambda e: e.dma_start(
            out=Wd[:, :], in_=dram["Wd_T"].rearrange("k p f -> p k f")),
            incs=[(GD, 16)])
        p.em("sync", [], lambda e: e.dma_start(
            out=Wr32[:, :], in_=dram["Wr_T"].rearrange("k p f -> p k f")),
            incs=[(SD, 16)])
        w_tok = GD.now()
        wr_tok = SD.now()

        ebb = pool[0]
        p.em("tensor", [], lambda e: e.matmul(
            ebb[:, 0:NE], g["onesk_f"][:, 0:128], g["ebias_row"][:, 0:NE],
            start=True, stop=True), incs=[(TS, 1)])
        p.em("vector", [TS.now()], lambda e: e.tensor_copy(
            g["ebias_bc"][:, :], ebb[:, 0:NE]), incs=[(VS, 1)])
        ebias_tok = VS.now()
        p.pending["tensor"].append(ebias_tok)

        lnsc = dict(onesk=g["onesk"], onem=g["onem"], pm=pst[0], pv=pst[1],
                    pmb=pbc[0], prb=pbc[1], bc_free=[], stat_free=[])
        bank_free = {i: [] for i in range(4)}
        xr_free = {0: [], 1: []}
        sgsu_guard = []
        sho_guard = []
        comb_cols = 32

        for ch in range(NCH):
            buf = ch % 2
            gt0 = ch * Tc
            b = gt0 // L
            t0 = gt0 % L
            xr_v = xr[:, buf * HT * Tc:(buf + 1) * HT * Tc]
            p.em("sync", xr_free[buf], lambda e, b=b, t0=t0, xr_v=xr_v: e.dma_start(
                out=xr_v,
                in_=x_in[b, :, t0:t0 + Tc].rearrange("(h p) t -> p h t", p=128)),
                incs=[(SD, 16)])
            xr_tok = SD.now()
            lnsc.update(
                xb=xbq[:, 0:HT * Tc], xsq=xbq[:, HT * Tc:2 * HT * Tc],
                mrow=rows[:, 0:Tc], msq=rows[:, Tc:2 * Tc],
                vrow=rows[:, 2 * Tc:3 * Tc], lrow=rows[:, 3 * Tc:4 * Tc],
                rrow=rows[:, 4 * Tc:5 * Tc],
                mrow_b=rows_b[:, 0:Tc], rrow_b=rows_b[:, Tc:2 * Tc],
                dtile=xbq[:, HT * Tc:2 * HT * Tc],
                d32=d32, xln32_out=xln32, fp32_stats=True,
                xr_ready=[xr_tok],
            )
            xln_tok = emit_ln(p, c, xr_v, xlnb, Tc, lnsc)
            xr_free[buf] = [xln_tok]
            p.em("scalar", [xln_tok], lambda e, gt0=gt0: e.dma_start(
                out=dram["sp_xln"][:, :, gt0:gt0 + Tc].rearrange("h p t -> p h t"),
                in_=xlnb[:, :]), incs=[(CD, 16)])
            lnsc["out_free"] = [CD.now()]

            for tt in range(4):
                bank = tt % 4
                for k in range(HT):
                    w4 = []
                    if k == 0:
                        w4 += bank_free[bank]
                        if tt == 0:
                            w4 += [xln_tok, w_tok, wr_tok]
                    p.em("tensor", w4,
                         lambda e, tt=tt, k=k, bank=bank: e.matmul(
                             pool[bank][:, 0:NE],
                             xln32[:, k * Tc + tt * 128:k * Tc + tt * 128 + 128],
                             Wr32[:, k * NE:(k + 1) * NE],
                             start=(k == 0), stop=(k == HT - 1)),
                         incs=[(TS, 1)] if k == HT - 1 else [])
                rl_tok = TS.now()
                s0 = tt * 32
                rr = rsc[:, s0:s0 + 32]
                sm = rsc[:, 128 + s0:128 + s0 + 32]
                t8 = rsc[:, 256 + tt * 8:256 + tt * 8 + 8]
                nmx = rsc[:, 288 + tt:288 + tt + 1]
                ism = rsc[:, 296 + tt:296 + tt + 1]
                p.em("vector", [rl_tok],
                     lambda e, rr=rr, bank=bank: e.tensor_copy(rr, pool[bank][:, 0:NE]),
                     incs=[(VS, 1)])
                rr_tok = VS.now()
                bank_free[bank] = [rr_tok]
                p.em("vector", [rr_tok],
                     lambda e, rr=rr, nmx=nmx: e.reduce_max(
                         nmx, rr, axis=AX.X, negate=True), incs=[(VS, 1)])
                p.em("scalar", [VS.now()],
                     lambda e, rr=rr, sm=sm, nmx=nmx: e.activation(
                         sm, rr, AF.Exp, bias=nmx), incs=[(SS, 1)])
                p.em("vector", [SS.now()],
                     lambda e, sm=sm, ism=ism: e.reduce_sum(ism, sm, axis=AX.X),
                     incs=[(VS, 1)])
                p.em("vector", [VS.now()],
                     lambda e, ism=ism: e.reciprocal(ism, ism), incs=[(VS, 1)])
                p.em("vector", [VS.now()],
                     lambda e, sm=sm, ism=ism: e.tensor_scalar(
                         sm, sm, ism, None, ALU.mult), incs=[(VS, 1)])
                rw_tok = VS.now()
                p.em("vector", [rw_tok, ebias_tok],
                     lambda e, rr=rr, sm=sm: e.tensor_add(rr, sm, g["ebias_bc"][:, :]),
                     incs=[(VS, 1)])
                p.em("vector", [VS.now()],
                     lambda e, rr=rr, t8=t8: e.max(t8, rr), incs=[(VS, 1)])
                p.em("vector", [VS.now()],
                     lambda e, rr=rr, t8=t8: e.tensor_single_scalar(
                         rr, rr, t8[:, 5:6], ALU.is_ge), incs=[(VS, 1)])
                p.em("vector", [VS.now()],
                     lambda e, rr=rr, sm=sm: e.tensor_mul(sm, rr, sm), incs=[(VS, 1)])
                cm_tok = VS.now()
                p.em("vector", [cm_tok],
                     lambda e, sm=sm, ism=ism: e.reduce_sum(ism, sm, axis=AX.X),
                     incs=[(VS, 1)])
                p.em("vector", [VS.now()],
                     lambda e, ism=ism: e.tensor_scalar_max(ism, ism, 1e-12),
                     incs=[(VS, 1)])
                p.em("vector", [VS.now()],
                     lambda e, ism=ism: e.reciprocal(ism, ism), incs=[(VS, 1)])
                p.em("vector", [VS.now()],
                     lambda e, sm=sm, ism=ism: e.tensor_scalar(
                         sm, sm, ism, None, ALU.mult), incs=[(VS, 1)])
                comb_tok = VS.now()
                bank2 = (tt + 1) % 4
                p.em("tensor", [comb_tok] + bank_free[bank2],
                     lambda e, sm=sm, bank2=bank2: e.transpose(
                         pool[bank2][0:32, 0:128], sm, fg["identF"][:, :]),
                     incs=[(TS, 1)])
                tr_tok = TS.now()
                p.em("scalar", [tr_tok],
                     lambda e, gt0=gt0, tt=tt, bank2=bank2: e.copy(
                         fg["combT"][0:32, gt0 + tt * 128:gt0 + (tt + 1) * 128],
                         pool[bank2][0:32, 0:128]),
                     incs=[(SS, 1)])
                bank_free[bank2] = [SS.now()]

            ev_sg = {}
            for o in range(HT):
                bank = o % 4
                for k in range(HT):
                    w4 = []
                    if k == 0:
                        w4 += bank_free[bank]
                        if o == 0:
                            w4 += sgsu_guard
                    p.em("tensor", w4,
                         lambda e, o=o, k=k, bank=bank: e.matmul(
                             pool[bank][:],
                             Wg[:, (k * HT + o) * 128:(k * HT + o + 1) * 128],
                             xlnb[:, k * Tc:(k + 1) * Tc],
                             start=(k == 0), stop=(k == HT - 1)),
                         incs=[(TS, 1)] if k == HT - 1 else [])
                mm_tok = TS.now()
                p.em("scalar", [mm_tok],
                     lambda e, o=o, bank=bank: e.activation(
                         sg[:, o * Tc:(o + 1) * Tc], pool[bank][:], AF.Silu),
                     incs=[(SS, 1)])
                ev_sg[o] = SS.now()
                bank_free[bank] = [ev_sg[o]]
            sgsu_guard = []
            ev_su = {}
            for o in range(HT):
                bank = o % 4
                for k in range(HT):
                    w4 = bank_free[bank] if k == 0 else []
                    p.em("tensor", w4,
                         lambda e, o=o, k=k, bank=bank: e.matmul(
                             pool[bank][:],
                             Wu[:, (k * HT + o) * 128:(k * HT + o + 1) * 128],
                             xlnb[:, k * Tc:(k + 1) * Tc],
                             start=(k == 0), stop=(k == HT - 1)),
                         incs=[(TS, 1)] if k == HT - 1 else [])
                mm_tok = TS.now()
                p.em("scalar", [mm_tok],
                     lambda e, o=o, bank=bank: e.copy(
                         su[:, o * Tc:(o + 1) * Tc], pool[bank][:]),
                     incs=[(SS, 1)])
                ev_su[o] = SS.now()
                bank_free[bank] = [ev_su[o]]
            p.em("vector", [ev_sg[HT - 1], ev_su[HT - 1]],
                 lambda e: e.tensor_mul(sh[:, :], sg[:, :], su[:, :]), incs=[(VS, 1)])
            sh_tok = VS.now()
            sgsu_guard = [sh_tok]
            ev_sd = {}
            for o in range(HT):
                bank = o % 4
                for k in range(HT):
                    w4 = []
                    if k == 0:
                        w4 += bank_free[bank]
                        if o == 0:
                            w4 += [sh_tok] + sho_guard
                    p.em("tensor", w4,
                         lambda e, o=o, k=k, bank=bank: e.matmul(
                             pool[bank][:],
                             Wd[:, (k * HT + o) * 128:(k * HT + o + 1) * 128],
                             sh[:, k * Tc:(k + 1) * Tc],
                             start=(k == 0), stop=(k == HT - 1)),
                         incs=[(TS, 1)] if k == HT - 1 else [])
                mm_tok = TS.now()
                p.em("scalar", [mm_tok] + (sho_guard if o == 0 else []),
                     lambda e, o=o, bank=bank: e.copy(
                         sho[:, o * Tc:(o + 1) * Tc], pool[bank][:]),
                     incs=[(SS, 1)])
                ev_sd[o] = SS.now()
                bank_free[bank] = [ev_sd[o]]
            sho_guard = []
            p.em("scalar", [ev_sd[HT - 1]], lambda e, gt0=gt0: e.dma_start(
                out=dram["sp_shared"][:, :, gt0:gt0 + Tc].rearrange("h p t -> p h t"),
                in_=sho[:, :]), incs=[(CD, 16)])
            sho_guard = [CD.now()]


def _fhalf(p, nc, c, fg, x_in, x_out, final, xout, hh, tag):
    dram, g = c["dram"], c["g"]
    SD, GD, CD = c["SD"], c["GD"], c["CD"]
    TS, VS, SS, GS = c["TS"], c["VS"], c["SS"], c["GS"]
    Tc = 512
    NCHH = L // Tc
    gbase = hh * L
    b = hh
    with ExitStack() as st:
        EC = st.enter_context
        xlh = EC(nc.sbuf_tensor(tag + f"h{hh}_xln", [128, HT * L], BF16))
        acc = EC(nc.sbuf_tensor(tag + f"h{hh}_acc", [128, HT * L], F32))
        wgu = EC(nc.sbuf_tensor(tag + f"h{hh}_wgu", [128, 2 * HT * 2 * I_EXP], BF16))
        wdn = EC(nc.sbuf_tensor(tag + f"h{hh}_wdn", [128, 2 * 4 * H], BF16))
        sgt = EC(nc.sbuf_tensor(tag + f"h{hh}_sg", [128, 2 * 4 * Tc], BF16))
        hmt = EC(nc.sbuf_tensor(tag + f"h{hh}_hm", [128, 2 * 4 * Tc], BF16))
        xrf = EC(nc.sbuf_tensor(tag + f"h{hh}_xr", [128, HT * Tc], F32))
        pool = [EC(nc.psum_tensor(f"{tag}h{hh}_p{i}", [128, Tc], F32)) for i in range(4)]
        pacc = [EC(nc.psum_tensor(f"{tag}h{hh}_a{i}", [128, Tc], F32)) for i in range(2)]
        pcbc = EC(nc.psum_tensor(f"{tag}h{hh}_cb", [128, Tc], F32))

        p.em("sync", [], lambda e: e.dma_start(
            out=xlh[:, :],
            in_=dram["sp_xln"][:, :, gbase:gbase + L].rearrange("h p t -> p h t")),
            incs=[(SD, 16)])
        p.em("gpsimd", [], lambda e: e.dma_start(
            out=acc[:, :],
            in_=dram["sp_shared"][:, :, gbase:gbase + L].rearrange("h p t -> p h t")),
            incs=[(GD, 16)])
        ld_tok = SD.now()
        acc_init = GD.now()

        wready = {}
        def load_expert(e_ix, slot, waits):
            p.em("gpsimd", waits, lambda e, e_ix=e_ix, slot=slot: e.dma_start(
                out=wgu[:, slot * HT * 2 * I_EXP:(slot + 1) * HT * 2 * I_EXP],
                in_=dram["gup_T"][e_ix].rearrange("k p f -> p k f")),
                incs=[(GD, 16)])
            p.em("gpsimd", [], lambda e, e_ix=e_ix, slot=slot: e.dma_start(
                out=wdn[:, slot * 4 * H:(slot + 1) * 4 * H],
                in_=dram["down_T"][e_ix].rearrange("k p f -> p k f")),
                incs=[(GD, 16)])
            wready[e_ix] = GD.now()

        wfree = {0: [], 1: []}
        load_expert(0, 0, [])
        load_expert(1, 1, [])
        bank_free = {i: [] for i in range(4)}
        acc_bank_free = {i: [] for i in range(2)}
        sg_free = {0: [], 1: []}
        hm_free = {0: [], 1: []}
        NSTEP = NE * NCHH
        pcbc_free = {0: []}
        pend = None

        def _emit_down(state):
            e2, ch2, par2, hm2t = state
            t02 = ch2 * Tc
            slot2 = e2 % 2
            wd_v = wdn[:, slot2 * 4 * H:(slot2 + 1) * 4 * H]
            hm_v2 = hmt[:, par2 * 4 * Tc:(par2 + 1) * 4 * Tc]
            for oo in range(8):
                bk = oo % 2
                for k in range(4):
                    w4 = []
                    if k == 0:
                        w4 += acc_bank_free[bk]
                        if oo == 0:
                            w4 += [hm2t]
                    p.em("tensor", w4,
                         lambda e, oo=oo, k=k, bk=bk, wd_v=wd_v, hm_v2=hm_v2:
                         e.matmul(pacc[bk][:],
                                  wd_v[:, k * H + oo * 128:k * H + (oo + 1) * 128],
                                  hm_v2[:, k * Tc:(k + 1) * Tc],
                                  start=(k == 0), stop=(k == 3)),
                         incs=[(TS, 1)] if k == 3 else [])
                mm_tok = TS.now()
                a_sl = acc[:, oo * L + t02:oo * L + t02 + Tc]
                w5 = [mm_tok] + ([acc_init] if e2 == 0 else [])
                p.em("vector", w5,
                     lambda e, a_sl=a_sl, bk=bk: e.tensor_add(a_sl, a_sl, pacc[bk][:]),
                     incs=[(VS, 1)])
                acc_bank_free[bk] = [VS.now()]
            hm_free[par2] = [TS.now()]
            if ch2 == NCHH - 1:
                wfree[slot2] = [TS.now()]

        for s in range(NSTEP):
            e_ix, ch = divmod(s, NCHH)
            slot = e_ix % 2
            par = s % 2
            t0 = ch * Tc
            wg_v = wgu[:, slot * HT * 2 * I_EXP:(slot + 1) * HT * 2 * I_EXP]
            sg_v = sgt[:, par * 4 * Tc:(par + 1) * 4 * Tc]
            hm_v = hmt[:, par * 4 * Tc:(par + 1) * 4 * Tc]
            if ch == 1 and 1 <= e_ix and e_ix + 1 < NE:
                load_expert(e_ix + 1, 1 - slot, wfree[1 - slot])
            ev = {}
            for o in range(4):
                for k in range(HT):
                    w4 = []
                    if k == 0:
                        w4 += bank_free[o]
                        if o == 0 and ch == 0:
                            w4 += [wready[e_ix]]
                        if s == 0:
                            w4 += [ld_tok]
                    p.em("tensor", w4,
                         lambda e, o=o, k=k, wg_v=wg_v, t0=t0: e.matmul(
                             pool[o][:],
                             wg_v[:, k * 2 * I_EXP + o * 128:
                                  k * 2 * I_EXP + (o + 1) * 128],
                             xlh[:, k * L + t0:k * L + t0 + Tc],
                             start=(k == 0), stop=(k == HT - 1)),
                         incs=[(TS, 1)] if k == HT - 1 else [])
                ev[o] = TS.now()
            sg_toks = {}
            for o in range(4):
                p.em("scalar", [ev[o]] + (sg_free[par] if o == 0 else []),
                     lambda e, o=o, sg_v=sg_v: e.activation(
                         sg_v[:, o * Tc:(o + 1) * Tc], pool[o][:], AF.Silu),
                     incs=[(SS, 1)])
                sg_toks[o] = SS.now()
            evu = {}
            for o in range(4):
                for k in range(HT):
                    w4 = [sg_toks[o]] if k == 0 else []
                    p.em("tensor", w4,
                         lambda e, o=o, k=k, wg_v=wg_v, t0=t0: e.matmul(
                             pool[o][:],
                             wg_v[:, k * 2 * I_EXP + (o + 4) * 128:
                                  k * 2 * I_EXP + (o + 5) * 128],
                             xlh[:, k * L + t0:k * L + t0 + Tc],
                             start=(k == 0), stop=(k == HT - 1)),
                         incs=[(TS, 1)] if k == HT - 1 else [])
                evu[o] = TS.now()
            p.em("tensor", pcbc_free[0],
                 lambda e, e_ix=e_ix, t0=t0: e.matmul(
                     pcbc[:], fg["indE"][:, e_ix * 128:(e_ix + 1) * 128],
                     fg["combT"][0:32, gbase + t0:gbase + t0 + Tc],
                     start=True, stop=True), incs=[(TS, 1)])
            cbc_tok = TS.now()
            if pend is not None:
                _emit_down(pend)
            hm_toks = {}
            for o in range(4):
                p.em("vector", [evu[o]] + (hm_free[par] if o == 0 else []),
                     lambda e, o=o, sg_v=sg_v, hm_v=hm_v: e.tensor_mul(
                         hm_v[:, o * Tc:(o + 1) * Tc],
                         sg_v[:, o * Tc:(o + 1) * Tc], pool[o][:]),
                     incs=[(VS, 1)])
                hm_toks[o] = VS.now()
            sg_free[par] = [hm_toks[3]]
            for o in range(4):
                p.em("vector", [cbc_tok] if o == 0 else [],
                     lambda e, o=o, hm_v=hm_v: e.tensor_mul(
                         hm_v[:, o * Tc:(o + 1) * Tc],
                         hm_v[:, o * Tc:(o + 1) * Tc], pcbc[:]),
                     incs=[(VS, 1)])
            hm2_tok = VS.now()
            pcbc_free[0] = [hm2_tok]
            for o in range(4):
                bank_free[o] = [hm_toks[o]]
            bank_free[0] = [hm2_tok]
            pend = (e_ix, ch, par, hm2_tok)
        _emit_down(pend)

        if DEBUG_COMBT and tag == "f1" and hh == 0:
            p.em("sync", [VS.now()], lambda e: e.dma_start(
                out=dram["dbg_acc"][:, :], in_=acc[:, :]), incs=[(SD, 16)])

        if DEBUG_COMBT and tag == "f1" and hh == 0:
            p.em("sync", [VS.now()], lambda e: e.dma_start(
                out=dram["dbg_acc"][:, :], in_=acc[:, :]), incs=[(SD, 16)])
        xr_free2 = []
        for ch in range(NCHH):
            t0 = ch * Tc
            p.em("sync", xr_free2, lambda e, t0=t0: e.dma_start(
                out=xrf[:, :],
                in_=x_in[b, :, t0:t0 + Tc].rearrange("(h p) t -> p h t", p=128)),
                incs=[(SD, 16)])
            xr_tok = SD.now()
            add_toks = []
            for o in range(HT):
                p.em("vector", [xr_tok],
                     lambda e, o=o, t0=t0: e.tensor_add(
                         xrf[:, o * Tc:(o + 1) * Tc],
                         xrf[:, o * Tc:(o + 1) * Tc],
                         acc[:, o * L + t0:o * L + t0 + Tc]),
                     incs=[(VS, 1)])
                add_toks.append(VS.now())
            p.em("sync", [add_toks[-1]], lambda e, t0=t0: e.dma_start(
                out=x_out[b, :, t0:t0 + Tc].rearrange("(h p) t -> p h t", p=128),
                in_=xrf[:, :]), incs=[(SD, 16)])
            xr_free2 = [SD.now()]
            if final and xout is not None:
                p.em("sync", [], lambda e, t0=t0: e.dma_start(
                    out=xout[b, :, t0:t0 + Tc].rearrange("(h p) t -> p h t", p=128),
                    in_=xrf[:, :]), incs=[(SD, 16)])
                xr_free2.append(SD.now())


def phase_attn(p, nc, c, x_in, x_out, final, xout, tag):
    TS, VS, SS, GS = c["TS"], c["VS"], c["SS"], c["GS"]
    SD, GD, CD = c["SD"], c["GD"], c["CD"]
    for b in range(NB):
        with ExitStack() as ast_:
            EC = ast_.enter_context
            qb = EC(nc.sbuf_tensor(tag + f"{b}_q", [128, HT * L], BF16))
            kb = EC(nc.sbuf_tensor(tag + f"{b}_k", [128, HT * L], BF16))
            vT = EC(nc.sbuf_tensor(tag + f"{b}_vT", [128, (L // 128) * H], BF16))
            _attn_qkv(p, nc, c, x_in, b, qb, kb, vT, tag)
            p.join([TS.now(), VS.now(), SS.now(), GS.now(),
                    SD.now(), GD.now(), CD.now()])
            _attn_core(p, nc, c, x_in, x_out, final, xout, b, qb, kb, vT, tag)
            p.join([TS.now(), VS.now(), SS.now(), GS.now(),
                    SD.now(), GD.now(), CD.now()])


def _attn_qkv(p, nc, c, x_in, b, qb, kb, vT, tag):
    dram, g = c["dram"], c["g"]
    SD, GD, CD = c["SD"], c["GD"], c["CD"]
    TS, VS, SS, GS = c["TS"], c["VS"], c["SS"], c["GS"]
    Tc = 512
    NCH = L // Tc
    with ExitStack() as st:
        EC = st.enter_context
        Wq = EC(nc.sbuf_tensor(tag + f"{b}q_Wq", [128, HT * H], BF16))
        Wk = EC(nc.sbuf_tensor(tag + f"{b}q_Wk", [128, HT * H], BF16))
        Wv = EC(nc.sbuf_tensor(tag + f"{b}q_Wv", [128, HT * H], BF16))
        xr = EC(nc.sbuf_tensor(tag + f"{b}q_xr", [128, HT * Tc], F32))
        xlnb = EC(nc.sbuf_tensor(tag + f"{b}q_xln", [128, HT * Tc], BF16))
        xbq = EC(nc.sbuf_tensor(tag + f"{b}q_xbq", [128, 2 * HT * Tc], BF16))
        rows = EC(nc.sbuf_tensor(tag + f"{b}q_rows", [1, 5 * Tc], F32))
        rows_b = EC(nc.sbuf_tensor(tag + f"{b}q_rows_b", [1, 2 * Tc], BF16))
        pool = [EC(nc.psum_tensor(f"{tag}{b}q_p{i}", [128, Tc], F32)) for i in range(4)]
        pst = [EC(nc.psum_tensor(f"{tag}{b}q_s{i}", [1, Tc], F32)) for i in range(2)]
        pbc = [EC(nc.psum_tensor(f"{tag}{b}q_b{i}", [128, Tc], F32)) for i in range(2)]

        for nm, dst in (("Wq_T", Wq), ("Wk_T", Wk), ("Wv_T", Wv)):
            p.em("gpsimd", [], lambda e, nm=nm, dst=dst: e.dma_start(
                out=dst[:, :], in_=dram[nm].rearrange("k p f -> p k f")),
                incs=[(GD, 16)])
        w_tok = GD.now()

        lnsc = dict(onesk=g["onesk"], onem=g["onem"], pm=pst[0], pv=pst[1],
                    pmb=pbc[0], prb=pbc[1], bc_free=[], stat_free=[])
        bank_free = {i: [] for i in range(4)}
        xr_free = {0: [], 1: []}

        for ch in range(NCH):
            buf = 0
            t0 = ch * Tc
            xr_v = xr[:, :]
            p.em("sync", xr_free[buf], lambda e, t0=t0, xr_v=xr_v: e.dma_start(
                out=xr_v,
                in_=x_in[b, :, t0:t0 + Tc].rearrange("(h p) t -> p h t", p=128)),
                incs=[(SD, 16)])
            xr_tok = SD.now()
            lnsc.update(
                xb=xbq[:, 0:HT * Tc], xsq=xbq[:, HT * Tc:2 * HT * Tc],
                mrow=rows[:, 0:Tc], msq=rows[:, Tc:2 * Tc],
                vrow=rows[:, 2 * Tc:3 * Tc], lrow=rows[:, 3 * Tc:4 * Tc],
                rrow=rows[:, 4 * Tc:5 * Tc],
                mrow_b=rows_b[:, 0:Tc], rrow_b=rows_b[:, Tc:2 * Tc],
                dtile=xbq[:, HT * Tc:2 * HT * Tc],
                xr_ready=[xr_tok],
            )
            xln_tok = emit_ln(p, c, xr_v, xlnb, Tc, lnsc)
            xr_free[buf] = [xln_tok]

            for which, W, dst, scale in (("q", Wq, qb, 1.0 / 32.0),
                                         ("k", Wk, kb, None)):
                for o in range(HT):
                    bank = o % 4
                    for k in range(HT):
                        w4 = []
                        if k == 0:
                            w4 += bank_free[bank]
                            if o == 0 and which == "q":
                                w4 += [xln_tok, w_tok]
                        p.em("tensor", w4,
                             lambda e, o=o, k=k, bank=bank, W=W: e.matmul(
                                 pool[bank][:],
                                 W[:, (k * HT + o) * 128:(k * HT + o + 1) * 128],
                                 xlnb[:, k * Tc:(k + 1) * Tc],
                                 start=(k == 0), stop=(k == HT - 1)),
                             incs=[(TS, 1)] if k == HT - 1 else [])
                    mm_tok = TS.now()
                    if scale is not None:
                        p.em("scalar", [mm_tok],
                             lambda e, o=o, bank=bank, dst=dst, t0=t0, scale=scale:
                             e.activation(
                                 dst[:, o * L + t0:o * L + t0 + Tc], pool[bank][:],
                                 AF.Copy, scale=scale),
                             incs=[(SS, 1)])
                    else:
                        p.em("scalar", [mm_tok],
                             lambda e, o=o, bank=bank, dst=dst, t0=t0: e.copy(
                                 dst[:, o * L + t0:o * L + t0 + Tc], pool[bank][:]),
                             incs=[(SS, 1)])
                    bank_free[bank] = [SS.now()]
            for tt in range(4):
                kt = ch * 4 + tt
                for nn in range(2):
                    bank = (tt * 2 + nn) % 4
                    for k in range(HT):
                        w4 = bank_free[bank] if k == 0 else []
                        p.em("tensor", w4,
                             lambda e, tt=tt, nn=nn, k=k, bank=bank: e.matmul(
                                 pool[bank][:],
                                 xlnb[:, k * Tc + tt * 128:k * Tc + tt * 128 + 128],
                                 Wv[:, k * H + nn * 512:k * H + (nn + 1) * 512],
                                 start=(k == 0), stop=(k == HT - 1)),
                             incs=[(TS, 1)] if k == HT - 1 else [])
                    mm_tok = TS.now()
                    p.em("scalar", [mm_tok],
                         lambda e, kt=kt, nn=nn, bank=bank: e.copy(
                             vT[:, kt * H + nn * 512:kt * H + (nn + 1) * 512],
                             pool[bank][:]),
                         incs=[(SS, 1)])
                    bank_free[bank] = [SS.now()]


def _attn_core(p, nc, c, x_in, x_out, final, xout, b, qb, kb, vT, tag):
    dram, g = c["dram"], c["g"]
    SD, GD, CD = c["SD"], c["GD"], c["CD"]
    TS, VS, SS, GS = c["TS"], c["VS"], c["SS"], c["GS"]
    NKT = L // 128
    with ExitStack() as st:
        EC = st.enter_context
        Wo = EC(nc.sbuf_tensor(tag + f"{b}a_Wo", [128, HT * H], BF16))
        scr = EC(nc.sbuf_tensor(tag + f"{b}a_scr", [128, 4 * L], F32))
        abuf = EC(nc.sbuf_tensor(tag + f"{b}a_ab", [128, 4 * L], BF16))
        atT = EC(nc.sbuf_tensor(tag + f"{b}a_atT", [128, NKT * 512], BF16))
        ctx = EC(nc.sbuf_tensor(tag + f"{b}a_ctx", [128, HT * 512], BF16))
        xrs = EC(nc.sbuf_tensor(tag + f"{b}a_xr", [128, HT * 512], F32))
        srow = EC(nc.sbuf_tensor(tag + f"{b}a_srow", [128, 16], F32))
        pool = [EC(nc.psum_tensor(f"{tag}{b}a_p{i}", [128, 512], F32)) for i in range(3)]
        ptr = EC(nc.psum_tensor(f"{tag}{b}a_tr", [128, 128], BF16))
        pctx = [EC(nc.psum_tensor(f"{tag}{b}a_c{i}", [128, 512], F32)) for i in range(4)]

        p.em("gpsimd", [], lambda e: e.dma_start(
            out=Wo[:, :], in_=dram["W_outB_T"].rearrange("k p f -> p k f")),
            incs=[(GD, 16)])
        w_tok = GD.now()

        bank_free = {i: [] for i in range(3)}
        ptr_free = []
        pctx_free = {i: [] for i in range(4)}
        scr_free = []
        abuf_free = []
        atT_free = []
        ctx_free = []
        xr_free = []

        for gg in range(4):
            q0 = gg * 512
            sm_toks = []
            for q4 in range(4):
                qt0 = q0 + q4 * 128
                ev_toks = []
                for nn in range(4):
                    bank = (q4 * 4 + nn) % 3
                    for k in range(HT):
                        w4 = []
                        if k == 0:
                            w4 += bank_free[bank]
                            if q4 == 0 and nn == 0 and gg == 0:
                                w4 += [w_tok] + scr_free
                            if q4 == 0 and nn == 0:
                                w4 += scr_free
                        p.em("tensor", w4,
                             lambda e, q4=q4, nn=nn, k=k, bank=bank, qt0=qt0: e.matmul(
                                 pool[bank][:],
                                 qb[:, k * L + qt0:k * L + qt0 + 128],
                                 kb[:, k * L + nn * 512:k * L + (nn + 1) * 512],
                                 start=(k == 0), stop=(k == HT - 1)),
                             incs=[(TS, 1)] if k == HT - 1 else [])
                    mm_tok = TS.now()
                    p.em("scalar", [mm_tok],
                         lambda e, q4=q4, nn=nn, bank=bank: e.copy(
                             scr[:, q4 * L + nn * 512:q4 * L + (nn + 1) * 512],
                             pool[bank][:]),
                         incs=[(SS, 1)])
                    ev_toks.append(SS.now())
                    bank_free[bank] = [ev_toks[-1]]
                nmx = srow[:, q4 * 4:q4 * 4 + 1]
                ism = srow[:, q4 * 4 + 1:q4 * 4 + 2]
                p.em("vector", [ev_toks[-1]],
                     lambda e, q4=q4, nmx=nmx: e.reduce_max(
                         nmx, scr[:, q4 * L:(q4 + 1) * L], axis=AX.X, negate=True),
                     incs=[(VS, 1)])
                p.em("scalar", [VS.now()],
                     lambda e, q4=q4, nmx=nmx: e.activation(
                         scr[:, q4 * L:(q4 + 1) * L], scr[:, q4 * L:(q4 + 1) * L],
                         AF.Exp, bias=nmx),
                     incs=[(SS, 1)])
                p.em("vector", [SS.now()],
                     lambda e, q4=q4, ism=ism: e.reduce_sum(
                         ism, scr[:, q4 * L:(q4 + 1) * L], axis=AX.X), incs=[(VS, 1)])
                p.em("vector", [VS.now()],
                     lambda e, ism=ism: e.reciprocal(ism, ism), incs=[(VS, 1)])
                p.em("vector", [VS.now()] + (abuf_free if q4 == 0 else []),
                     lambda e, q4=q4, ism=ism: e.tensor_scalar(
                         abuf[:, q4 * L:(q4 + 1) * L], scr[:, q4 * L:(q4 + 1) * L],
                         ism, None, ALU.mult),
                     incs=[(VS, 1)])
                sm_toks.append(VS.now())
            abuf_free = []
            scr_free = [sm_toks[-1]]
            tr_toks = []
            for q4 in range(4):
                for j in range(NKT):
                    w4 = [sm_toks[q4]] if j == 0 else []
                    w4 += ptr_free
                    if q4 == 0 and j == 0:
                        w4 += atT_free
                    p.em("tensor", w4,
                         lambda e, q4=q4, j=j: e.transpose(
                             ptr[:], abuf[:, q4 * L + j * 128:q4 * L + (j + 1) * 128],
                             g["ident"][:, :]),
                         incs=[(TS, 1)])
                    t_tok = TS.now()
                    p.em("scalar", [t_tok],
                         lambda e, q4=q4, j=j: e.copy(
                             atT[:, j * 512 + q4 * 128:j * 512 + (q4 + 1) * 128],
                             ptr[:]),
                         incs=[(SS, 1)])
                    ptr_free = [SS.now()]
            atT_ready = SS.now()
            abuf_free = [TS.now()]
            ctx_toks = []
            for half in range(2):
                for o in range(4):
                    oo = half * 4 + o
                    for j in range(NKT):
                        w4 = []
                        if j == 0:
                            w4 += pctx_free[o]
                            if half == 0 and o == 0:
                                w4 += [atT_ready] + ctx_free
                        p.em("tensor", w4,
                             lambda e, oo=oo, j=j, o=o: e.matmul(
                                 pctx[o][:],
                                 vT[:, j * H + oo * 128:j * H + (oo + 1) * 128],
                                 atT[:, j * 512:(j + 1) * 512],
                                 start=(j == 0), stop=(j == NKT - 1)),
                             incs=[(TS, 1)] if j == NKT - 1 else [])
                    mm_tok = TS.now()
                    p.em("scalar", [mm_tok],
                         lambda e, oo=oo, o=o: e.copy(
                             ctx[:, oo * 512:(oo + 1) * 512], pctx[o][:]),
                         incs=[(SS, 1)])
                    pctx_free[o] = [SS.now()]
                    ctx_toks.append(SS.now())
            atT_free = [TS.now()]
            ctx_ready = ctx_toks[-1]
            p.em("sync", xr_free, lambda e, q0=q0: e.dma_start(
                out=xrs[:, :],
                in_=x_in[b, :, q0:q0 + 512].rearrange("(h p) t -> p h t", p=128)),
                incs=[(SD, 16)])
            xr_tok = SD.now()
            add_toks = []
            for o in range(HT):
                bank = o % 3
                for k in range(HT):
                    w4 = []
                    if k == 0:
                        w4 += bank_free[bank]
                        if o == 0:
                            w4 += [ctx_ready, w_tok]
                    p.em("tensor", w4,
                         lambda e, o=o, k=k, bank=bank: e.matmul(
                             pool[bank][:],
                             Wo[:, (k * HT + o) * 128:(k * HT + o + 1) * 128],
                             ctx[:, k * 512:(k + 1) * 512],
                             start=(k == 0), stop=(k == HT - 1)),
                         incs=[(TS, 1)] if k == HT - 1 else [])
                mo_tok = TS.now()
                p.em("vector", [mo_tok, xr_tok],
                     lambda e, o=o, bank=bank: e.tensor_add(
                         xrs[:, o * 512:(o + 1) * 512],
                         xrs[:, o * 512:(o + 1) * 512], pool[bank][:]),
                     incs=[(VS, 1)])
                add_toks.append(VS.now())
                bank_free[bank] = [add_toks[-1]]
            ctx_free = [TS.now()]
            p.em("sync", [add_toks[-1]], lambda e, q0=q0: e.dma_start(
                out=x_out[b, :, q0:q0 + 512].rearrange("(h p) t -> p h t", p=128),
                in_=xrs[:, :]), incs=[(SD, 16)])
            xr_free = [SD.now()]
            if final and xout is not None:
                p.em("sync", [], lambda e, q0=q0: e.dma_start(
                    out=xout[b, :, q0:q0 + 512].rearrange("(h p) t -> p h t", p=128),
                    in_=xrs[:, :]), incs=[(SD, 16)])
                xr_free.append(SD.now())



def _pack_pc(v):
    return np.ascontiguousarray(v.reshape(CT, 128).T)


def _prep_inputs(inputs):
    f = {}
    x = np.ascontiguousarray(inputs["L0_attn_a_x"])
    W_in = inputs["L0_attn_a_W_in"]
    f["W_in_T"] = np.ascontiguousarray(W_in.T).reshape(HT, 128, 2 * INNER)
    cw = inputs["L0_attn_a_conv_w"][:, 0, :]
    f["cw_p"] = np.ascontiguousarray(
        cw.reshape(CT, 128, 4).transpose(1, 0, 2).reshape(128, CT * 4))
    f["cb_p"] = _pack_pc(inputs["L0_attn_a_conv_b"])
    W_x = inputs["L0_attn_a_W_x"]
    f["W_BC_T"] = np.ascontiguousarray(W_x[:32].T).reshape(CT, 128, 32)
    f["W_dtin_T"] = np.ascontiguousarray(W_x[32:].T).reshape(CT, 128, INNER)
    f["W_dt_T"] = np.ascontiguousarray(inputs["L0_attn_a_W_dt"].T).reshape(CT, 128, INNER)
    f["bdt_p"] = _pack_pc(inputs["L0_attn_a_b_dt"])
    f["dv_p"] = _pack_pc(inputs["L0_attn_a_D"])
    f["W_outA_T"] = np.ascontiguousarray(inputs["L0_attn_a_W_out"].T).reshape(CT, 128, H)
    W_qkv = inputs["L0_attn_b_W_qkv"]
    f["Wq_T"] = np.ascontiguousarray(W_qkv[0::3].T).reshape(HT, 128, H)
    f["Wk_T"] = np.ascontiguousarray(W_qkv[1::3].T).reshape(HT, 128, H)
    f["Wv_T"] = np.ascontiguousarray(W_qkv[2::3].T).reshape(HT, 128, H)
    f["W_outB_T"] = np.ascontiguousarray(inputs["L0_attn_b_W_out"].T).reshape(HT, 128, H)
    f["Wr_T"] = np.ascontiguousarray(
        inputs["L0_ffn_router_weight"].T).reshape(HT, 128, NE)
    f["e_bias"] = np.ascontiguousarray(inputs["L0_ffn_e_score_bias"]).reshape(1, NE)
    f["Wg_T"] = np.ascontiguousarray(
        inputs["L0_ffn_shared_gate_w"].T).reshape(HT, 128, IS)
    f["Wu_T"] = np.ascontiguousarray(
        inputs["L0_ffn_shared_up_w"].T).reshape(HT, 128, IS)
    f["Wd_T"] = np.ascontiguousarray(
        inputs["L0_ffn_shared_down_w"].T).reshape(IS // 128, 128, H)
    f["gup_T"] = np.ascontiguousarray(
        inputs["L0_ffn_gate_up_proj"].transpose(0, 2, 1)).reshape(NE, HT, 128, 2 * I_EXP)
    f["down_T"] = np.ascontiguousarray(
        inputs["L0_ffn_down_proj"].transpose(0, 2, 1)).reshape(NE, 4, 128, H)

    shards = []
    for cid in range(8):
        d = dict(f)
        xs = x[cid * NB:(cid + 1) * NB]
        d["x0"] = np.ascontiguousarray(xs.transpose(0, 2, 1))
        shards.append(d)
    return shards


def kernel(**inputs):
    shards = _prep_inputs(inputs)
    nc = build_program(upto="all")
    res = run_bass_kernel_spmd(nc, shards, core_ids=list(range(8)))
    outs = []
    for cid in range(8):
        xo = res.results[cid]["xout"]
        outs.append(np.ascontiguousarray(np.asarray(xo).transpose(0, 2, 1)))
    return np.concatenate(outs, axis=0).astype(np.float32)


# version 10
# speedup vs baseline: 1.1081x; 1.1081x over previous
import numpy as np
import ml_dtypes
from contextlib import ExitStack

import concourse.bass as bass
import concourse.mybir as mybir
from concourse.bass_utils import run_bass_kernel_spmd

F32 = mybir.dt.float32
BF16 = mybir.dt.bfloat16
AF = mybir.ActivationFunctionType
ALU = mybir.AluOpType
AX = mybir.AxisListType

H = 1024
INNER = 2048
NB = 2
L = 2048
T = NB * L
NE = 32
I_EXP = 512
IS = 1024
EPS = 1e-5
HT = H // 128
CT = INNER // 128


class Ctr:
    def __init__(self, handle):
        self.h = handle
        self.n = 0

    def now(self):
        return (self.h, self.n)


class Prog:
    ENGINES = ("sync", "gpsimd", "vector", "scalar", "tensor")

    def __init__(self, nc):
        self.nc = nc
        self.q = {e: [] for e in self.ENGINES}
        self.pending = {e: [] for e in self.ENGINES}

    def em(self, eng, waits, f, incs=()):
        w = list(self.pending[eng]) + [x for x in waits if x]
        self.pending[eng] = []
        bumped = []
        for c, k in incs:
            c.n += k
            bumped.append((c.h, k))
        self.q[eng].append((tuple(w), f, tuple(bumped)))

    def join(self, tokens):
        for e in self.ENGINES:
            self.pending[e].extend(tokens)

    def run(self, block):
        hw = {e: {} for e in self.ENGINES}

        def make(eng_name):
            items = self.q[eng_name]
            marks = hw[eng_name]

            def body(engine):
                for waits, f, incs in items:
                    for (h, v) in waits:
                        if marks.get(h, -1) < v:
                            engine.wait_ge(h, v)
                            marks[h] = v
                    ins = f(engine)
                    for (h, k) in incs:
                        ins = ins.then_inc(h, k)
            return body

        block.sync(make("sync"))
        block.gpsimd(make("gpsimd"))
        block.vector(make("vector"))
        block.scalar(make("scalar"))
        block.tensor(make("tensor"))



def emit_ln(p, c, xr, x_ln_out, Tc, sc):
    SS, VS, TS = c["SS"], c["VS"], c["TS"]
    g = c["g"]
    n = HT * Tc
    fp32s = sc.get("fp32_stats", False)
    if fp32s:
        xb_tok = None
        p.em("scalar", sc["xr_ready"] + sc.get("xsq_free", []),
             lambda e: e.activation(sc["d32"][:, 0:n], xr[:, 0:n], AF.Square),
             incs=[(SS, 1)])
        xsq_tok = SS.now()
        for h in range(HT):
            p.em("tensor", (sc["xr_ready"] + sc["stat_free"]) if h == 0 else [],
                 lambda e, h=h: e.matmul(sc["pm"][:, 0:Tc], g["onem_f"][:, 0:1],
                                         xr[:, h * Tc:(h + 1) * Tc],
                                         start=(h == 0), stop=(h == HT - 1)),
                 incs=[(TS, 1)] if h == HT - 1 else [])
        pm_tok = TS.now()
        for h in range(HT):
            p.em("tensor", [xsq_tok] if h == 0 else [],
                 lambda e, h=h: e.matmul(sc["pv"][:, 0:Tc], g["onem_f"][:, 0:1],
                                         sc["d32"][:, h * Tc:(h + 1) * Tc],
                                         start=(h == 0), stop=(h == HT - 1)),
                 incs=[(TS, 1)] if h == HT - 1 else [])
        pv_tok = TS.now()
    else:
        p.em("scalar", sc["xr_ready"] + sc.get("xb_free", []),
             lambda e: e.copy(sc["xb"][:, 0:n], xr[:, 0:n]), incs=[(SS, 1)])
        xb_tok = SS.now()
        p.em("scalar", sc.get("xsq_free", []),
             lambda e: e.activation(sc["xsq"][:, 0:n], xr[:, 0:n], AF.Square),
             incs=[(SS, 1)])
        xsq_tok = SS.now()
        for h in range(HT):
            p.em("tensor", ([xb_tok] + sc["stat_free"]) if h == 0 else [],
                 lambda e, h=h: e.matmul(sc["pm"][:, 0:Tc], g["onem"][:, 0:1],
                                         sc["xb"][:, h * Tc:(h + 1) * Tc],
                                         start=(h == 0), stop=(h == HT - 1)),
                 incs=[(TS, 1)] if h == HT - 1 else [])
        pm_tok = TS.now()
        for h in range(HT):
            p.em("tensor", [xsq_tok] if h == 0 else [],
                 lambda e, h=h: e.matmul(sc["pv"][:, 0:Tc], g["onem"][:, 0:1],
                                         sc["xsq"][:, h * Tc:(h + 1) * Tc],
                                         start=(h == 0), stop=(h == HT - 1)),
                 incs=[(TS, 1)] if h == HT - 1 else [])
        pv_tok = TS.now()
    p.em("vector", [pm_tok], lambda e: e.tensor_copy(sc["mrow"][:, 0:Tc], sc["pm"][:, 0:Tc]),
         incs=[(VS, 1)])
    m_tok = VS.now()
    p.em("vector", [m_tok], lambda e: e.tensor_mul(
        sc["msq"][:, 0:Tc], sc["mrow"][:, 0:Tc], sc["mrow"][:, 0:Tc]), incs=[(VS, 1)])
    p.em("vector", [VS.now(), pv_tok], lambda e: e.tensor_sub(
        sc["vrow"][:, 0:Tc], sc["pv"][:, 0:Tc], sc["msq"][:, 0:Tc]), incs=[(VS, 1)])
    v_tok = VS.now()
    p.em("scalar", [v_tok], lambda e: e.activation(
        sc["lrow"][:, 0:Tc], sc["vrow"][:, 0:Tc], AF.Ln, bias=c["g"]["cbias"][0:1, 0:1]), incs=[(SS, 1)])
    p.em("scalar", [SS.now()], lambda e: e.activation(
        sc["rrow"][:, 0:Tc], sc["lrow"][:, 0:Tc], AF.Exp, scale=-0.5), incs=[(SS, 1)])
    r_tok = SS.now()
    sc["stat_free"] = [m_tok, v_tok]
    if fp32s:
        p.em("tensor", [m_tok] + sc["bc_free"], lambda e: e.matmul(
            sc["pmb"][:, 0:Tc], g["onesk_f"][:, 0:128], sc["mrow"][:, 0:Tc],
            start=True, stop=True), incs=[(TS, 1)])
        p.em("tensor", [r_tok], lambda e: e.matmul(
            sc["prb"][:, 0:Tc], g["onesk_f"][:, 0:128], sc["rrow"][:, 0:Tc],
            start=True, stop=True), incs=[(TS, 1)])
        bc_tok = TS.now()
    else:
        p.em("vector", [m_tok], lambda e: e.tensor_copy(
            sc["mrow_b"][:, 0:Tc], sc["mrow"][:, 0:Tc]), incs=[(VS, 1)])
        mb_tok = VS.now()
        p.em("scalar", [r_tok], lambda e: e.copy(
            sc["rrow_b"][:, 0:Tc], sc["rrow"][:, 0:Tc]), incs=[(SS, 1)])
        rb_tok = SS.now()
        p.em("tensor", [mb_tok] + sc["bc_free"], lambda e: e.matmul(
            sc["pmb"][:, 0:Tc], g["onesk"][:, 0:128], sc["mrow_b"][:, 0:Tc],
            start=True, stop=True), incs=[(TS, 1)])
        p.em("tensor", [rb_tok], lambda e: e.matmul(
            sc["prb"][:, 0:Tc], g["onesk"][:, 0:128], sc["rrow_b"][:, 0:Tc],
            start=True, stop=True), incs=[(TS, 1)])
        bc_tok = TS.now()
    x32 = sc.get("xln32_out")
    for h in range(HT):
        dt_v = sc["d32"] if x32 is not None else sc["dtile"]
        p.em("vector", [bc_tok] if h == 0 else [],
             lambda e, h=h, dt_v=dt_v: e.tensor_sub(dt_v[:, h * Tc:(h + 1) * Tc],
                                                    xr[:, h * Tc:(h + 1) * Tc],
                                                    sc["pmb"][:, 0:Tc]),
             incs=[])
        p.em("vector", sc.get("out_free", []) if h == 0 else [],
             lambda e, h=h, dt_v=dt_v: e.tensor_mul(x_ln_out[:, h * Tc:(h + 1) * Tc],
                                                    dt_v[:, h * Tc:(h + 1) * Tc],
                                                    sc["prb"][:, 0:Tc]),
             incs=[(VS, 1)])
        if x32 is not None:
            p.em("vector", [],
                 lambda e, h=h, dt_v=dt_v: e.tensor_mul(x32[:, h * Tc:(h + 1) * Tc],
                                                        dt_v[:, h * Tc:(h + 1) * Tc],
                                                        sc["prb"][:, 0:Tc]),
                 incs=[(VS, 1)])
    done = VS.now()
    sc["out_free"] = []
    sc["bc_free"] = [done]
    sc["xb_free"] = [pm_tok]
    sc["xsq_free"] = [pv_tok, done]
    return done



DEBUG_COMBT = False


def build_program(upto="all"):
    nc = bass.Bass("TRN2", target_bir_lowering=False)
    dram = {}

    def din(name, shape, dtype=F32):
        dram[name] = nc.dram_tensor(name, shape, dtype, kind="ExternalInput").ap()
        return dram[name]

    def dint(name, shape, dtype):
        dram[name] = nc.dram_tensor(name, shape, dtype, kind="Internal").ap()
        return dram[name]

    x0 = din("x0", [NB, H, L])
    din("W_in_T", [HT, 128, 2 * INNER])
    din("cw_p", [128, CT * 4])
    din("cb_p", [128, CT])
    din("W_BC_T", [CT, 128, 32])
    din("W_dtin_T", [CT, 128, INNER])
    din("W_dt_T", [CT, 128, INNER])
    din("bdt_p", [128, CT])
    din("dv_p", [128, CT])
    din("W_outA_T", [CT, 128, H])
    din("Wq_T", [HT, 128, H])
    din("Wk_T", [HT, 128, H])
    din("Wv_T", [HT, 128, H])
    din("W_outB_T", [HT, 128, H])
    din("Wr_T", [HT, 128, NE])
    din("e_bias", [1, NE])
    din("Wg_T", [HT, 128, IS])
    din("Wu_T", [HT, 128, IS])
    din("Wd_T", [IS // 128, 128, H])
    din("gup_T", [NE, HT, 128, 2 * I_EXP])
    din("down_T", [NE, 4, 128, H])

    xout = nc.dram_tensor("xout", [NB, H, L], F32, kind="ExternalOutput").ap()
    if DEBUG_COMBT:
        dram["dbg_combT"] = nc.dram_tensor("dbg_combT", [32, T], BF16,
                                           kind="ExternalOutput").ap()
        dram["dbg_xln"] = nc.dram_tensor("dbg_xln", [HT, 128, T], BF16,
                                         kind="ExternalOutput").ap()
        dram["dbg_sh"] = nc.dram_tensor("dbg_sh", [HT, 128, T], BF16,
                                        kind="ExternalOutput").ap()
        dram["dbg_acc"] = nc.dram_tensor("dbg_acc", [128, HT * L], F32,
                                         kind="ExternalOutput").ap()
        dram["dbg_hm"] = nc.dram_tensor("dbg_hm", [128, 4 * 512], BF16,
                                        kind="ExternalOutput").ap()
        dram["dbg_sg"] = nc.dram_tensor("dbg_sg", [128, 4 * 512], BF16,
                                        kind="ExternalOutput").ap()
    xa = dint("xa", [NB, H, L], F32)
    xb_h = dint("xb", [NB, H, L], F32)
    dint("sp_xconv", [NB, CT, 128, L], BF16)
    dint("sp_z", [NB, CT, 128, L], BF16)
    dint("sp_dtin", [NB, CT, 128, L], BF16)
    dint("sp_xln", [HT, 128, T], BF16)
    dint("sp_shared", [HT, 128, T], BF16)

    with ExitStack() as top:
        EC = top.enter_context
        sems = {}
        for nm in ("SD", "GD", "CD", "TS", "VS", "SS", "GS"):
            sems[nm] = Ctr(EC(nc.semaphore(nm)))
        p = Prog(nc)
        c = dict(p=p, nc=nc, dram=dram, sems=sems,
                 SD=sems["SD"], GD=sems["GD"], CD=sems["CD"], TS=sems["TS"],
                 VS=sems["VS"], SS=sems["SS"], GS=sems["GS"])

        g = {}
        g["onesk"] = EC(nc.sbuf_tensor("onesk", [1, 128], BF16))
        g["onem"] = EC(nc.sbuf_tensor("onem", [128, 1], BF16))
        g["selBC"] = EC(nc.sbuf_tensor("selBC", [32, 2], BF16))
        g["sel2"] = EC(nc.sbuf_tensor("sel2", [2, 256], BF16))
        g["ident"] = EC(nc.sbuf_tensor("ident", [128, 128], BF16))
        g["cw"] = EC(nc.sbuf_tensor("cw", [128, CT * 4], F32))
        g["cb"] = EC(nc.sbuf_tensor("cb", [128, CT], F32))
        g["bdt"] = EC(nc.sbuf_tensor("bdt", [128, CT], F32))
        g["dvec"] = EC(nc.sbuf_tensor("dvec", [128, CT], F32))
        g["ebias_bc"] = EC(nc.sbuf_tensor("ebias_bc", [128, NE], F32))
        g["onesk_f"] = EC(nc.sbuf_tensor("onesk_f", [1, 128], F32))
        g["cbias"] = EC(nc.sbuf_tensor("cbias", [128, 2], F32))
        g["onem_f"] = EC(nc.sbuf_tensor("onem_f", [128, 1], F32))
        g["ebias_row"] = EC(nc.sbuf_tensor("ebias_row", [1, NE], F32))
        c["g"] = g

        build_constants(p, c)

        phase_mamba(p, nc, c, x_in=x0, x_out=xa,
                    final=(upto == "mamba"), xout=xout)

        if upto in ("ffn1", "attn", "all"):
            phase_ffn(p, nc, c, x_in=xa, x_out=xb_h,
                      final=(upto == "ffn1"), xout=xout, tag="f1")
        if upto in ("attn", "all"):
            phase_attn(p, nc, c, x_in=xb_h, x_out=xa,
                       final=(upto == "attn"), xout=xout, tag="at")
        if upto == "all":
            phase_ffn(p, nc, c, x_in=xa, x_out=xout,
                      final=False, xout=None, tag="f2")

        SD, GD, CD = sems["SD"], sems["GD"], sems["CD"]
        p.em("sync", [SD.now(), GD.now(), CD.now()],
             lambda e: e.wait_ge(SD.h, SD.n))

        block = EC(nc.Block())
        p.run(block)
    return nc


def build_constants(p, c):
    nc, g, dram = c["nc"], c["g"], c["dram"]
    GS, VS, SD, GD = c["GS"], c["VS"], c["SD"], c["GD"]
    with ExitStack() as st:
        iosc = st.enter_context(nc.sbuf_tensor("iosc", [128, 512], F32))
        p.em("gpsimd", [], lambda e: e.iota(
            iosc[:, 0:128], pattern=[[1, 128]], base=0, channel_multiplier=-1,
            allow_small_or_imprecise_dtypes=True), incs=[(GS, 1)])
        p.em("vector", [GS.now()], lambda e: e.tensor_single_scalar(
            g["ident"][:], iosc[:, 0:128], 0.0, ALU.is_equal), incs=[(VS, 1)])
        p.em("vector", [], lambda e: e.memset(g["onesk"][:, :], 1.0), incs=[(VS, 1)])
        p.em("vector", [], lambda e: e.memset(g["onesk_f"][:, :], 1.0), incs=[(VS, 1)])
        p.em("vector", [], lambda e: e.memset(g["onem"][:, :], 1.0 / 1024.0),
             incs=[(VS, 1)])
        p.em("vector", [], lambda e: e.memset(g["cbias"][:, 0:1], EPS), incs=[(VS, 1)])
        p.em("vector", [], lambda e: e.memset(g["onem_f"][:, :], 1.0 / 1024.0),
             incs=[(VS, 1)])
        p.em("vector", [], lambda e: e.memset(g["cbias"][:, 1:2], 1.0), incs=[(VS, 1)])
        p.em("gpsimd", [], lambda e: e.iota(
            iosc[0:32, 128:130], pattern=[[0, 2]], base=0, channel_multiplier=1,
            allow_small_or_imprecise_dtypes=True), incs=[(GS, 1)])
        p.em("vector", [GS.now()], lambda e: e.tensor_single_scalar(
            iosc[0:32, 130:132], iosc[0:32, 128:130], 16.0, ALU.is_lt), incs=[(VS, 1)])
        tk2 = VS.now()
        p.em("vector", [tk2], lambda e: e.tensor_scalar(
            g["selBC"][0:32, 0:1], iosc[0:32, 130:131], 1.0 / 16.0, None, ALU.mult),
            incs=[(VS, 1)])
        p.em("vector", [tk2], lambda e: e.tensor_scalar(
            iosc[0:32, 132:133], iosc[0:32, 131:132], -1.0, 1.0, ALU.mult, ALU.add),
            incs=[(VS, 1)])
        p.em("vector", [VS.now()], lambda e: e.tensor_scalar(
            g["selBC"][0:32, 1:2], iosc[0:32, 132:133], 1.0 / 16.0, None, ALU.mult),
            incs=[(VS, 1)])
        p.em("gpsimd", [], lambda e: e.iota(
            iosc[0:2, 256:512], pattern=[[1, 2], [0, 128]], base=0,
            channel_multiplier=-1, allow_small_or_imprecise_dtypes=True),
            incs=[(GS, 1)])
        p.em("vector", [GS.now()], lambda e: e.tensor_single_scalar(
            g["sel2"][:], iosc[0:2, 256:512], 0.0, ALU.is_equal), incs=[(VS, 1)])
        for nm, dst in (("cw_p", g["cw"]), ("cb_p", g["cb"]), ("bdt_p", g["bdt"]),
                        ("dv_p", g["dvec"])):
            p.em("sync", [], lambda e, nm=nm, dst=dst: e.dma_start(
                out=dst[:, :], in_=dram[nm][:, :]), incs=[(SD, 16)])
        p.em("sync", [], lambda e: e.dma_start(
            out=g["ebias_row"][:, :], in_=dram["e_bias"][:, :]), incs=[(SD, 16)])
        p.join([VS.now(), GS.now(), SD.now()])



def phase_mamba(p, nc, c, x_in, x_out, final=False, xout=None):
    dram, g = c["dram"], c["g"]
    SD, GD, CD = c["SD"], c["GD"], c["CD"]
    TS, VS, SS, GS = c["TS"], c["VS"], c["SS"], c["GS"]
    Tc = 512
    NCH = L // Tc

    with ExitStack() as mst:
        g["bgcg"] = mst.enter_context(nc.sbuf_tensor("bgcg", [2, T], BF16))
        _phase_mamba_body(p, nc, c, x_in, x_out, final, xout, Tc, NCH)
    p.join([c["TS"].now(), c["VS"].now(), c["SS"].now(), c["GS"].now(),
            c["SD"].now(), c["GD"].now(), c["CD"].now()])


def _phase_mamba_body(p, nc, c, x_in, x_out, final, xout, Tc, NCH):
    TS, VS, SS, GS = c["TS"], c["VS"], c["SS"], c["GS"]
    SD, GD, CD = c["SD"], c["GD"], c["CD"]
    _m1(p, nc, c, x_in, Tc, NCH)
    p.join([TS.now(), VS.now(), SS.now(), GS.now(), SD.now(), GD.now(), CD.now()])
    _m2a(p, nc, c, Tc, NCH)
    p.join([TS.now(), VS.now(), SS.now(), GS.now(), SD.now(), GD.now(), CD.now()])
    _m2b(p, nc, c, x_in, x_out, final, xout)


def _m1(p, nc, c, x_in, Tc, NCH):
    dram, g = c["dram"], c["g"]
    SD, GD, CD = c["SD"], c["GD"], c["CD"]
    TS, VS, SS, GS = c["TS"], c["VS"], c["SS"], c["GS"]

    with ExitStack() as st:
        EC = st.enter_context
        W = EC(nc.sbuf_tensor("m1_W", [128, HT * 2 * INNER], BF16))
        xr = EC(nc.sbuf_tensor("m1_xr", [128, 2 * HT * Tc], F32))
        xlnb = EC(nc.sbuf_tensor("m1_xln", [128, HT * Tc], BF16))
        xbq = EC(nc.sbuf_tensor("m1_xbq", [128, 2 * HT * Tc], BF16))
        rows = EC(nc.sbuf_tensor("m1_rows", [1, 5 * Tc], F32))
        rows_b = EC(nc.sbuf_tensor("m1_rows_b", [1, 2 * Tc], BF16))
        xp = EC(nc.sbuf_tensor("m1_xp", [128, CT * (Tc + 3)], BF16))
        xcv = EC(nc.sbuf_tensor("m1_xcv", [128, CT * Tc], BF16))
        zs = EC(nc.sbuf_tensor("m1_zs", [128, CT * Tc], BF16))
        cacc = EC(nc.sbuf_tensor("m1_cacc", [128, 2 * Tc], F32))
        pmm = [EC(nc.psum_tensor(f"m1_p{i}", [128, Tc], F32)) for i in range(4)]
        pst = [EC(nc.psum_tensor(f"m1_s{i}", [1, Tc], F32)) for i in range(2)]
        pbc = [EC(nc.psum_tensor(f"m1_b{i}", [128, Tc], F32)) for i in range(2)]

        p.em("gpsimd", [], lambda e: e.dma_start(
            out=W[:, :], in_=dram["W_in_T"].rearrange("k p f -> p k f")),
            incs=[(GD, 16)])
        w_tok = GD.now()

        lnsc = dict(onesk=g["onesk"], onem=g["onem"], pm=pst[0], pv=pst[1],
                    pmb=pbc[0], prb=pbc[1], bc_free=[], stat_free=[])
        bank_free = {i: [] for i in range(4)}
        xr_free = {0: [], 1: []}
        xp_free = {cix: [] for cix in range(CT)}
        zs_guard = []
        xcv_guard = []

        for b in range(NB):
            p.em("vector", [], lambda e: e.memset(
                xp.rearrange("p (c t) -> p c t", c=CT)[:, :, 0:3], 0.0),
                incs=[(VS, 1)])
            halo0 = VS.now()
            for ch in range(NCH):
                buf = ch % 2
                t0 = ch * Tc
                xr_v = xr[:, buf * HT * Tc:(buf + 1) * HT * Tc]
                p.em("sync", xr_free[buf], lambda e, b=b, t0=t0, xr_v=xr_v: e.dma_start(
                    out=xr_v,
                    in_=x_in[b, :, t0:t0 + Tc].rearrange("(h p) t -> p h t", p=128)),
                    incs=[(SD, 16)])
                xr_tok = SD.now()

                lnsc.update(
                    xb=xbq[:, 0:HT * Tc], xsq=xbq[:, HT * Tc:2 * HT * Tc],
                    mrow=rows[:, 0:Tc], msq=rows[:, Tc:2 * Tc],
                    vrow=rows[:, 2 * Tc:3 * Tc], lrow=rows[:, 3 * Tc:4 * Tc],
                    rrow=rows[:, 4 * Tc:5 * Tc],
                    mrow_b=rows_b[:, 0:Tc], rrow_b=rows_b[:, Tc:2 * Tc],
                    dtile=xbq[:, HT * Tc:2 * HT * Tc],
                    xr_ready=[xr_tok],
                )
                xln_tok = emit_ln(p, c, xr_v, xlnb, Tc, lnsc)
                xr_free[buf] = [xln_tok]

                evac_tok = {}
                for o in range(32):
                    bank = o % 4
                    for k in range(HT):
                        waits = []
                        if k == 0:
                            waits += bank_free[bank]
                            if o == 0:
                                waits += [xln_tok, w_tok]
                        p.em("tensor", waits,
                             lambda e, o=o, k=k, bank=bank: e.matmul(
                                 pmm[bank][:],
                                 W[:, (k * 32 + o) * 128:(k * 32 + o + 1) * 128],
                                 xlnb[:, k * Tc:(k + 1) * Tc],
                                 start=(k == 0), stop=(k == HT - 1)),
                             incs=[(TS, 1)] if k == HT - 1 else [])
                    mm_tok = TS.now()
                    if o < CT:
                        dst = xp[:, o * (Tc + 3) + 3:o * (Tc + 3) + 3 + Tc]
                        w2 = [mm_tok] + xp_free[o]
                        if ch == 0 and o == 0:
                            w2.append(halo0)
                        p.em("scalar", w2,
                             lambda e, dst=dst, bank=bank: e.copy(dst, pmm[bank][:]),
                             incs=[(SS, 1)])
                    else:
                        cix = o - CT
                        p.em("scalar", [mm_tok] + zs_guard,
                             lambda e, cix=cix, bank=bank: e.activation(
                                 zs[:, cix * Tc:(cix + 1) * Tc], pmm[bank][:], AF.Silu),
                             incs=[(SS, 1)])
                        zs_guard = []
                    evac_tok[o] = SS.now()
                    bank_free[bank] = [evac_tok[o]]

                p.em("scalar", [evac_tok[31]], lambda e, b=b, t0=t0: e.dma_start(
                    out=dram["sp_z"][b, :, :, t0:t0 + Tc].rearrange("c p t -> p c t"),
                    in_=zs[:, :]), incs=[(CD, 16)])
                zst_tok = CD.now()

                conv_last = None
                acc_free = {0: [], 1: []}
                for cix in range(CT):
                    base = cix * (Tc + 3)
                    par = cix % 2
                    eng = "vector"
                    ECt = VS
                    acc = cacc[:, par * Tc:(par + 1) * Tc]
                    p.em(eng, [evac_tok[cix]] + acc_free[par],
                         lambda e, cix=cix, base=base, acc=acc: e.tensor_scalar(
                             acc, xp[:, base + 0:base + 0 + Tc],
                             g["cw"][:, cix * 4 + 0:cix * 4 + 1], None, ALU.mult),
                         incs=[])
                    for k in range(1, 4):
                        p.em(eng, [],
                             lambda e, cix=cix, k=k, base=base, acc=acc:
                             e.scalar_tensor_tensor(
                                 acc, xp[:, base + k:base + k + Tc],
                                 g["cw"][:, cix * 4 + k:cix * 4 + k + 1], acc,
                                 ALU.mult, ALU.add),
                             incs=[(ECt, 1)] if k == 3 else [])
                    cv_tok = ECt.now()
                    p.em("scalar", [cv_tok] + xcv_guard,
                         lambda e, cix=cix, acc=acc: e.activation(
                             xcv[:, cix * Tc:(cix + 1) * Tc], acc, AF.Silu,
                             bias=g["cb"][:, cix:cix + 1]),
                         incs=[(SS, 1)])
                    xcv_guard = []
                    conv_last = SS.now()
                    acc_free[par] = [conv_last]
                    if ch < NCH - 1:
                        p.em(eng, [],
                             lambda e, base=base: e.tensor_copy(
                                 xp[:, base:base + 3], xp[:, base + Tc:base + Tc + 3]),
                             incs=[(ECt, 1)])
                        xp_free[cix] = [ECt.now()]
                    else:
                        xp_free[cix] = [cv_tok]
                p.em("scalar", [conv_last], lambda e, b=b, t0=t0: e.dma_start(
                    out=dram["sp_xconv"][b, :, :, t0:t0 + Tc].rearrange("c p t -> p c t"),
                    in_=xcv[:, :]), incs=[(CD, 16)])
                zs_guard = [zst_tok]
                xcv_guard = [CD.now()]

    p.join([TS.now(), VS.now(), SS.now(), GS.now(), SD.now(), GD.now(), CD.now()])

def _m2a(p, nc, c, Tc, NCH):
    dram, g = c["dram"], c["g"]
    SD, GD, CD = c["SD"], c["GD"], c["CD"]
    TS, VS, SS, GS = c["TS"], c["VS"], c["SS"], c["GS"]

    with ExitStack() as st:
        EC = st.enter_context
        W = EC(nc.sbuf_tensor("m2a_W", [128, CT * INNER], BF16))
        Wbc = EC(nc.sbuf_tensor("m2a_Wbc", [128, CT * 32], BF16))
        xcv = EC(nc.sbuf_tensor("m2a_xcv", [128, 2 * CT * Tc], BF16))
        dti = EC(nc.sbuf_tensor("m2a_dti", [128, CT * Tc], BF16))
        bcv = EC(nc.sbuf_tensor("m2a_bcv", [32, Tc], BF16))
        pmm = [EC(nc.psum_tensor(f"m2a_p{i}", [128, Tc], F32)) for i in range(6)]
        pbcm = EC(nc.psum_tensor("m2a_bc", [32, Tc], F32))
        pmean = EC(nc.psum_tensor("m2a_mean", [2, Tc], F32))

        p.em("gpsimd", [], lambda e: e.dma_start(
            out=W[:, :], in_=dram["W_dtin_T"].rearrange("k p f -> p k f")),
            incs=[(GD, 16)])
        p.em("gpsimd", [], lambda e: e.dma_start(
            out=Wbc[:, :], in_=dram["W_BC_T"].rearrange("k p f -> p k f")),
            incs=[(GD, 16)])
        w_tok = GD.now()

        bank_free = {i: [] for i in range(6)}
        xcv_free = {0: [], 1: []}
        bc_free = []
        mean_free = []
        store_free = []

        for b in range(NB):
            for ch in range(NCH):
                buf = ch % 2
                t0 = ch * Tc
                xcv_v = xcv[:, buf * CT * Tc:(buf + 1) * CT * Tc]
                p.em("sync", xcv_free[buf], lambda e, b=b, t0=t0, xcv_v=xcv_v:
                     e.dma_start(
                         out=xcv_v,
                         in_=dram["sp_xconv"][b, :, :, t0:t0 + Tc]
                         .rearrange("c p t -> p c t")),
                     incs=[(SD, 16)])
                x_tok = SD.now()

                evac_tok = {}
                for o in range(CT):
                    bank = o % 6
                    for k in range(CT):
                        waits = []
                        if k == 0:
                            waits += bank_free[bank]
                            if o == 0:
                                waits += [x_tok, w_tok]
                        p.em("tensor", waits,
                             lambda e, o=o, k=k, bank=bank, xcv_v=xcv_v: e.matmul(
                                 pmm[bank][:],
                                 W[:, (k * CT + o) * 128:(k * CT + o + 1) * 128],
                                 xcv_v[:, k * Tc:(k + 1) * Tc],
                                 start=(k == 0), stop=(k == CT - 1)),
                             incs=[(TS, 1)] if k == CT - 1 else [])
                    mm_tok = TS.now()
                    p.em("scalar", [mm_tok] + (store_free if o == 0 else []),
                         lambda e, o=o, bank=bank: e.copy(
                             dti[:, o * Tc:(o + 1) * Tc], pmm[bank][:]),
                         incs=[(SS, 1)])
                    evac_tok[o] = SS.now()
                    bank_free[bank] = [evac_tok[o]]
                    if o == 0:
                        store_free = []
                for k in range(CT):
                    p.em("tensor", bc_free if k == 0 else [],
                         lambda e, k=k, xcv_v=xcv_v: e.matmul(
                             pbcm[:], Wbc[:, k * 32:(k + 1) * 32],
                             xcv_v[:, k * Tc:(k + 1) * Tc],
                             start=(k == 0), stop=(k == CT - 1)),
                         incs=[(TS, 1)] if k == CT - 1 else [])
                bcm_tok = TS.now()
                p.em("scalar", [bcm_tok], lambda e: e.copy(bcv[:, :], pbcm[:]),
                     incs=[(SS, 1)])
                bcv_tok = SS.now()
                bc_free = [bcv_tok]
                p.em("tensor", [bcv_tok] + mean_free, lambda e: e.matmul(
                    pmean[:], g["selBC"][:, :], bcv[:, :], start=True, stop=True),
                    incs=[(TS, 1)])
                mean_tok = TS.now()
                p.em("scalar", [mean_tok], lambda e, b=b, t0=t0: e.activation(
                    g["bgcg"][0:2, b * L + t0:b * L + t0 + Tc], pmean[:], AF.Sigmoid),
                    incs=[(SS, 1)])
                mean_free = [SS.now()]
                xcv_free[buf] = [mm_tok]
                p.em("scalar", [evac_tok[CT - 1]], lambda e, b=b, t0=t0: e.dma_start(
                    out=dram["sp_dtin"][b, :, :, t0:t0 + Tc].rearrange("c p t -> p c t"),
                    in_=dti[:, :]), incs=[(CD, 16)])
                store_free = [CD.now()]


def _m2b(p, nc, c, x_in, x_out, final, xout):
    dram, g = c["dram"], c["g"]
    SD, GD, CD = c["SD"], c["GD"], c["CD"]
    TS, VS, SS, GS = c["TS"], c["VS"], c["SS"], c["GS"]

    Tc2 = 256
    NCH2 = L // Tc2
    with ExitStack() as st:
        EC = st.enter_context
        Wdt = EC(nc.sbuf_tensor("m2b_Wdt", [128, CT * INNER], BF16))
        Wo = EC(nc.sbuf_tensor("m2b_Wo", [128, CT * H], BF16))
        dti = EC(nc.sbuf_tensor("m2b_dti", [128, CT * Tc2], BF16))
        xcv = EC(nc.sbuf_tensor("m2b_xcv", [128, CT * Tc2], BF16))
        zsb = EC(nc.sbuf_tensor("m2b_zs", [128, CT * Tc2], BF16))
        xrs = EC(nc.sbuf_tensor("m2b_xrs", [128, HT * Tc2], F32))
        U = EC(nc.sbuf_tensor("m2b_U", [128, CT * Tc2], F32))
        A = EC(nc.sbuf_tensor("m2b_A", [128, CT * Tc2], F32))
        Hs = EC(nc.sbuf_tensor("m2b_H", [128, CT * Tc2], F32))
        YB = EC(nc.sbuf_tensor("m2b_YB", [128, 2 * CT * Tc2], BF16))
        carry = EC(nc.sbuf_tensor("m2b_carry", [128, CT], F32))
        pmm = [EC(nc.psum_tensor(f"m2b_p{i}", [128, Tc2], F32)) for i in range(4)]
        pout = [EC(nc.psum_tensor(f"m2b_o{i}", [128, Tc2], F32)) for i in range(2)]
        pg = [EC(nc.psum_tensor(f"m2b_g{i}", [128, Tc2], F32)) for i in range(2)]

        p.em("gpsimd", [], lambda e: e.dma_start(
            out=Wdt[:, :], in_=dram["W_dt_T"].rearrange("k p f -> p k f")),
            incs=[(GD, 16)])
        p.em("gpsimd", [], lambda e: e.dma_start(
            out=Wo[:, :], in_=dram["W_outA_T"].rearrange("k p f -> p k f")),
            incs=[(GD, 16)])
        w_tok = GD.now()

        bank_free = {i: [] for i in range(4)}
        pout_free = {i: [] for i in range(2)}
        dti_free = {0: [], 1: []}
        xcv_free = []
        zs_free = []
        xr_free = []
        yb_free = {0: [], 1: []}
        chain_free = []
        pg_free = []
        n2 = CT * Tc2
        pendq = []

        def _emit_outproj(state):
            b2, t02, par2, y2 = state
            yv = YB[:, par2 * n2:(par2 + 1) * n2]
            p.em("sync", xr_free, lambda e, b2=b2, t02=t02: e.dma_start(
                out=xrs[:, :],
                in_=x_in[b2, :, t02:t02 + Tc2].rearrange("(h p) t -> p h t", p=128)),
                incs=[(SD, 16)])
            xr_tok = SD.now()
            add_toks = []
            for o in range(HT):
                bank = o % 2
                for k in range(CT):
                    waits = []
                    if k == 0:
                        waits += pout_free[bank]
                        if o == 0:
                            waits.append(y2)
                    p.em("tensor", waits,
                         lambda e, o=o, k=k, bank=bank, yv=yv: e.matmul(
                             pout[bank][:],
                             Wo[:, (k * HT + o) * 128:(k * HT + o + 1) * 128],
                             yv[:, k * Tc2:(k + 1) * Tc2],
                             start=(k == 0), stop=(k == CT - 1)),
                         incs=[(TS, 1)] if k == CT - 1 else [])
                mo_tok = TS.now()
                p.em("vector", [mo_tok, xr_tok],
                     lambda e, o=o, bank=bank: e.tensor_add(
                         xrs[:, o * Tc2:(o + 1) * Tc2],
                         xrs[:, o * Tc2:(o + 1) * Tc2], pout[bank][:]),
                     incs=[(VS, 1)])
                add_toks.append(VS.now())
                pout_free[bank] = [add_toks[-1]]
            yb_free[par2] = [TS.now()]
            p.em("sync", [add_toks[-1]], lambda e, b2=b2, t02=t02: e.dma_start(
                out=x_out[b2, :, t02:t02 + Tc2].rearrange("(h p) t -> p h t", p=128),
                in_=xrs[:, :]), incs=[(SD, 16)])
            toks = [SD.now()]
            if final and xout is not None:
                p.em("sync", [], lambda e, b2=b2, t02=t02: e.dma_start(
                    out=xout[b2, :, t02:t02 + Tc2].rearrange("(h p) t -> p h t", p=128),
                    in_=xrs[:, :]), incs=[(SD, 16)])
                toks.append(SD.now())
            return toks

        for b in range(NB):
            p.em("vector", list(chain_free), lambda e: e.memset(carry[:, :], 0.0),
                 incs=[(VS, 1)])
            carry_tok = VS.now()
            for ch in range(NCH2):
                buf = ch % 2
                par = (b * NCH2 + ch) % 2
                t0 = ch * Tc2
                dti_v = dti[:, 0:n2]
                xcv_v = xcv[:, 0:n2]
                p.em("sync", dti_free[0], lambda e, b=b, t0=t0, dti_v=dti_v:
                     e.dma_start(out=dti_v,
                                 in_=dram["sp_dtin"][b, :, :, t0:t0 + Tc2]
                                 .rearrange("c p t -> p c t")),
                     incs=[(SD, 16)])
                dti_tok = SD.now()
                p.em("sync", xcv_free, lambda e, b=b, t0=t0, xcv_v=xcv_v:
                     e.dma_start(out=xcv_v,
                                 in_=dram["sp_xconv"][b, :, :, t0:t0 + Tc2]
                                 .rearrange("c p t -> p c t")),
                     incs=[(SD, 16)])
                xcv_tok = SD.now()
                p.em("sync", zs_free, lambda e, b=b, t0=t0: e.dma_start(
                    out=zsb[:, :],
                    in_=dram["sp_z"][b, :, :, t0:t0 + Tc2].rearrange("c p t -> p c t")),
                    incs=[(SD, 16)])
                zs_tok = SD.now()

                evac_tok = {}
                for o in range(CT):
                    bank = o % 4
                    for k in range(CT):
                        waits = []
                        if k == 0:
                            waits += bank_free[bank]
                            if o == 0:
                                waits += [dti_tok, w_tok]
                        p.em("tensor", waits,
                             lambda e, o=o, k=k, bank=bank, dti_v=dti_v: e.matmul(
                                 pmm[bank][:],
                                 Wdt[:, (k * CT + o) * 128:(k * CT + o + 1) * 128],
                                 dti_v[:, k * Tc2:(k + 1) * Tc2],
                                 start=(k == 0), stop=(k == CT - 1)),
                             incs=[(TS, 1)] if k == CT - 1 else [])
                    mm_tok = TS.now()
                    p.em("scalar", [mm_tok] + (chain_free if o == 0 else []),
                         lambda e, o=o, bank=bank: e.activation(
                             U[:, o * Tc2:(o + 1) * Tc2], pmm[bank][:],
                             AF.Identity, bias=g["bdt"][:, o:o + 1]),
                         incs=[(SS, 1)])
                    evac_tok[o] = SS.now()
                    bank_free[bank] = [evac_tok[o]]
                u_tok = evac_tok[CT - 1]
                dti_free[0] = [mm_tok]
                off = b * L + t0
                p.em("tensor", pg_free, lambda e, off=off: e.matmul(
                    pg[0][:], g["sel2"][:, 0:128], g["bgcg"][0:2, off:off + Tc2],
                    start=True, stop=True), incs=[(TS, 1)])
                p.em("tensor", [], lambda e, off=off: e.matmul(
                    pg[1][:], g["sel2"][:, 128:256], g["bgcg"][0:2, off:off + Tc2],
                    start=True, stop=True), incs=[(TS, 1)])
                gbc_tok = TS.now()

                if len(pendq) >= 2:
                    xr_free = _emit_outproj(pendq.pop(0))

                p.em("scalar", [u_tok], lambda e: e.activation(
                    A[:, 0:n2], U[:, 0:n2], AF.Abs), incs=[(SS, 1)])
                p.em("scalar", [SS.now()], lambda e: e.activation(
                    A[:, 0:n2], A[:, 0:n2], AF.Exp, scale=-1.0), incs=[(SS, 1)])
                p.em("scalar", [SS.now()], lambda e: e.activation(
                    A[:, 0:n2], A[:, 0:n2], AF.Ln, bias=g["cbias"][:, 1:2]),
                    incs=[(SS, 1)])
                sp_tok = SS.now()
                p.em("vector", [sp_tok, u_tok], lambda e: e.scalar_tensor_tensor(
                    U[:, 0:n2], U[:, 0:n2], 0.0, A[:, 0:n2], ALU.max, ALU.add),
                    incs=[(VS, 1)])
                spu_tok = VS.now()
                p.em("scalar", [spu_tok], lambda e: e.activation(
                    A[:, 0:n2], U[:, 0:n2], AF.Sigmoid, scale=0.1), incs=[(SS, 1)])
                alpha_tok = SS.now()
                p.em("gpsimd", [alpha_tok], lambda e: e.tensor_scalar(
                    U[:, 0:n2], A[:, 0:n2], -1.0, 1.0, ALU.mult, ALU.add),
                    incs=[(GS, 1)])
                g1_tok = GS.now()
                for cix in range(CT):
                    p.em("vector", [gbc_tok, g1_tok] if cix == 0 else [],
                         lambda e, cix=cix: e.tensor_mul(
                             U[:, cix * Tc2:(cix + 1) * Tc2],
                             U[:, cix * Tc2:(cix + 1) * Tc2], pg[0][:]),
                         incs=[(VS, 1)] if cix == CT - 1 else [])
                gb_tok = VS.now()
                p.em("vector", [gb_tok, xcv_tok], lambda e, xcv_v=xcv_v: e.tensor_mul(
                    U[:, 0:n2], U[:, 0:n2], xcv_v[:, 0:n2]), incs=[(VS, 1)])
                gx_tok = VS.now()
                p.em("vector", [], lambda e: e.tensor_scalar(
                    A[:, 0:n2], A[:, 0:n2], 0.01, 0.99, ALU.max, ALU.min),
                    incs=[(VS, 1)])
                ac_tok = VS.now()
                for cix in range(CT):
                    w3 = [ac_tok, gx_tok] if cix == 0 else []
                    if ch == 0 and cix == 0:
                        w3.append(carry_tok)
                    p.em("vector", w3,
                         lambda e, cix=cix: e.tensor_tensor_scan(
                             Hs[:, cix * Tc2:(cix + 1) * Tc2],
                             A[:, cix * Tc2:(cix + 1) * Tc2],
                             U[:, cix * Tc2:(cix + 1) * Tc2],
                             carry[:, cix:cix + 1], ALU.mult, ALU.add),
                         incs=[(VS, 1)])
                    p.em("vector", [VS.now()],
                         lambda e, cix=cix: e.tensor_copy(
                             carry[:, cix:cix + 1],
                             Hs[:, (cix + 1) * Tc2 - 1:(cix + 1) * Tc2]),
                         incs=[(VS, 1)])
                for cix in range(CT):
                    p.em("vector", [],
                         lambda e, cix=cix: e.tensor_mul(
                             Hs[:, cix * Tc2:(cix + 1) * Tc2],
                             Hs[:, cix * Tc2:(cix + 1) * Tc2], pg[1][:]),
                         incs=[])
                for cix in range(CT):
                    p.em("vector", [],
                         lambda e, cix=cix, xcv_v=xcv_v: e.scalar_tensor_tensor(
                             U[:, cix * Tc2:(cix + 1) * Tc2],
                             xcv_v[:, cix * Tc2:(cix + 1) * Tc2],
                             g["dvec"][:, cix:cix + 1],
                             Hs[:, cix * Tc2:(cix + 1) * Tc2],
                             ALU.mult, ALU.add),
                         incs=[(VS, 1)] if cix == CT - 1 else [])
                gd_tok = VS.now()
                yv = YB[:, par * n2:(par + 1) * n2]
                p.em("gpsimd", [gd_tok, zs_tok] + yb_free[par],
                     lambda e, yv=yv: e.tensor_mul(
                         yv[:, 0:n2], U[:, 0:n2], zsb[:, 0:n2]), incs=[(GS, 1)])
                y_tok = GS.now()
                pg_free = [y_tok] if False else [gd_tok, y_tok]
                chain_free = [y_tok]
                zs_free = [y_tok]
                xcv_free = [y_tok]
                pendq.append((b, t0, par, y_tok))
            while pendq:
                xr_free = _emit_outproj(pendq.pop(0))

    p.join([TS.now(), VS.now(), SS.now(), GS.now(), SD.now(), GD.now(), CD.now()])



def phase_ffn(p, nc, c, x_in, x_out, final, xout, tag):
    TS, VS, SS, GS = c["TS"], c["VS"], c["SS"], c["GS"]
    SD, GD, CD = c["SD"], c["GD"], c["CD"]
    with ExitStack() as fst:
        EC = fst.enter_context
        combT = EC(nc.sbuf_tensor(tag + "_combT", [32, T], BF16))
        indE = EC(nc.sbuf_tensor(tag + "_indE", [32, NE * 128], BF16))
        identF = EC(nc.sbuf_tensor(tag + "_idF", [128, 128], F32))
        fg = dict(combT=combT, indE=indE, identF=identF)
        with ExitStack() as st2:
            iosc = st2.enter_context(nc.sbuf_tensor(tag + "_iosc", [128, NE * 128], F32))
            p.em("gpsimd", [], lambda e: e.iota(
                iosc[0:32, :], pattern=[[1, NE], [0, 128]], base=0,
                channel_multiplier=-1, allow_small_or_imprecise_dtypes=True),
                incs=[(GS, 1)])
            p.em("vector", [GS.now()], lambda e: e.tensor_single_scalar(
                indE[:], iosc[0:32, :], 0.0, ALU.is_equal), incs=[(VS, 1)])
            p.em("gpsimd", [VS.now()], lambda e: e.iota(
                iosc[:, 0:128], pattern=[[1, 128]], base=0,
                channel_multiplier=-1, allow_small_or_imprecise_dtypes=True),
                incs=[(GS, 1)])
            p.em("vector", [GS.now()], lambda e: e.tensor_single_scalar(
                identF[:], iosc[:, 0:128], 0.0, ALU.is_equal),
                incs=[(VS, 1)])
            p.join([VS.now(), GS.now()])
        _f0(p, nc, c, fg, x_in, tag)
        if DEBUG_COMBT and tag == "f1":
            p.em("sync", [c["SS"].now(), c["CD"].now()], lambda e: e.dma_start(
                out=c["dram"]["dbg_combT"][:, :], in_=combT[:, :]),
                incs=[(SD, 16)])
            p.em("sync", [], lambda e: e.dma_start(
                out=c["dram"]["dbg_xln"][:, :, :], in_=c["dram"]["sp_xln"][:, :, :]),
                incs=[(SD, 16)])
            p.em("sync", [], lambda e: e.dma_start(
                out=c["dram"]["dbg_sh"][:, :, :], in_=c["dram"]["sp_shared"][:, :, :]),
                incs=[(SD, 16)])
        p.join([TS.now(), VS.now(), SS.now(), GS.now(), SD.now(), GD.now(), CD.now()])
        for hh in range(2):
            _fhalf(p, nc, c, fg, x_in, x_out, final, xout, hh, tag)
            p.join([TS.now(), VS.now(), SS.now(), GS.now(),
                    SD.now(), GD.now(), CD.now()])


def _f0(p, nc, c, fg, x_in, tag):
    dram, g = c["dram"], c["g"]
    SD, GD, CD = c["SD"], c["GD"], c["CD"]
    TS, VS, SS, GS = c["TS"], c["VS"], c["SS"], c["GS"]
    Tc = 512
    NCH = T // Tc
    with ExitStack() as st:
        EC = st.enter_context
        Wg = EC(nc.sbuf_tensor(tag + "0_Wg", [128, HT * IS], BF16))
        Wu = EC(nc.sbuf_tensor(tag + "0_Wu", [128, HT * IS], BF16))
        Wd = EC(nc.sbuf_tensor(tag + "0_Wd", [128, (IS // 128) * H], BF16))
        xr = EC(nc.sbuf_tensor(tag + "0_xr", [128, 2 * HT * Tc], F32))
        xlnb = EC(nc.sbuf_tensor(tag + "0_xln", [128, HT * Tc], BF16))
        xbq = EC(nc.sbuf_tensor(tag + "0_xbq", [128, 2 * HT * Tc], BF16))
        rows = EC(nc.sbuf_tensor(tag + "0_rows", [1, 5 * Tc], F32))
        rows_b = EC(nc.sbuf_tensor(tag + "0_rows_b", [1, 2 * Tc], BF16))
        sg = EC(nc.sbuf_tensor(tag + "0_sg", [128, HT * Tc], BF16))
        su = EC(nc.sbuf_tensor(tag + "0_su", [128, HT * Tc], BF16))
        sh = EC(nc.sbuf_tensor(tag + "0_sh", [128, HT * Tc], BF16))
        sho = EC(nc.sbuf_tensor(tag + "0_sho", [128, HT * Tc], BF16))
        rsc = EC(nc.sbuf_tensor(tag + "0_rsc", [128, 320], F32))
        d32 = EC(nc.sbuf_tensor(tag + "0_d32", [128, HT * Tc], F32))
        xln32 = EC(nc.sbuf_tensor(tag + "0_x32", [128, HT * Tc], F32))
        Wr32 = EC(nc.sbuf_tensor(tag + "0_Wr32", [128, HT * NE], F32))
        pool = [EC(nc.psum_tensor(f"{tag}0_p{i}", [128, Tc], F32)) for i in range(4)]
        pst = [EC(nc.psum_tensor(f"{tag}0_s{i}", [1, Tc], F32)) for i in range(2)]
        pbc = [EC(nc.psum_tensor(f"{tag}0_b{i}", [128, Tc], F32)) for i in range(2)]

        p.em("gpsimd", [], lambda e: e.dma_start(
            out=Wg[:, :], in_=dram["Wg_T"].rearrange("k p f -> p k f")),
            incs=[(GD, 16)])
        p.em("gpsimd", [], lambda e: e.dma_start(
            out=Wu[:, :], in_=dram["Wu_T"].rearrange("k p f -> p k f")),
            incs=[(GD, 16)])
        p.em("gpsimd", [], lambda e: e.dma_start(
            out=Wd[:, :], in_=dram["Wd_T"].rearrange("k p f -> p k f")),
            incs=[(GD, 16)])
        p.em("sync", [], lambda e: e.dma_start(
            out=Wr32[:, :], in_=dram["Wr_T"].rearrange("k p f -> p k f")),
            incs=[(SD, 16)])
        w_tok = GD.now()
        wr_tok = SD.now()

        ebb = pool[0]
        p.em("tensor", [], lambda e: e.matmul(
            ebb[:, 0:NE], g["onesk_f"][:, 0:128], g["ebias_row"][:, 0:NE],
            start=True, stop=True), incs=[(TS, 1)])
        p.em("vector", [TS.now()], lambda e: e.tensor_copy(
            g["ebias_bc"][:, :], ebb[:, 0:NE]), incs=[(VS, 1)])
        ebias_tok = VS.now()
        p.pending["tensor"].append(ebias_tok)

        lnsc = dict(onesk=g["onesk"], onem=g["onem"], pm=pst[0], pv=pst[1],
                    pmb=pbc[0], prb=pbc[1], bc_free=[], stat_free=[])
        bank_free = {i: [] for i in range(4)}
        xr_free = {0: [], 1: []}
        sgsu_guard = []
        sho_guard = []
        comb_cols = 32

        for ch in range(NCH):
            buf = ch % 2
            gt0 = ch * Tc
            b = gt0 // L
            t0 = gt0 % L
            xr_v = xr[:, buf * HT * Tc:(buf + 1) * HT * Tc]
            p.em("sync", xr_free[buf], lambda e, b=b, t0=t0, xr_v=xr_v: e.dma_start(
                out=xr_v,
                in_=x_in[b, :, t0:t0 + Tc].rearrange("(h p) t -> p h t", p=128)),
                incs=[(SD, 16)])
            xr_tok = SD.now()
            lnsc.update(
                xb=xbq[:, 0:HT * Tc], xsq=xbq[:, HT * Tc:2 * HT * Tc],
                mrow=rows[:, 0:Tc], msq=rows[:, Tc:2 * Tc],
                vrow=rows[:, 2 * Tc:3 * Tc], lrow=rows[:, 3 * Tc:4 * Tc],
                rrow=rows[:, 4 * Tc:5 * Tc],
                mrow_b=rows_b[:, 0:Tc], rrow_b=rows_b[:, Tc:2 * Tc],
                dtile=xbq[:, HT * Tc:2 * HT * Tc],
                d32=d32, xln32_out=xln32, fp32_stats=True,
                xr_ready=[xr_tok],
            )
            xln_tok = emit_ln(p, c, xr_v, xlnb, Tc, lnsc)
            xr_free[buf] = [xln_tok]
            p.em("scalar", [xln_tok], lambda e, gt0=gt0: e.dma_start(
                out=dram["sp_xln"][:, :, gt0:gt0 + Tc].rearrange("h p t -> p h t"),
                in_=xlnb[:, :]), incs=[(CD, 16)])
            lnsc["out_free"] = [CD.now()]

            for tt in range(4):
                bank = tt % 4
                for k in range(HT):
                    w4 = []
                    if k == 0:
                        w4 += bank_free[bank]
                        if tt == 0:
                            w4 += [xln_tok, w_tok, wr_tok]
                    p.em("tensor", w4,
                         lambda e, tt=tt, k=k, bank=bank: e.matmul(
                             pool[bank][:, 0:NE],
                             xln32[:, k * Tc + tt * 128:k * Tc + tt * 128 + 128],
                             Wr32[:, k * NE:(k + 1) * NE],
                             start=(k == 0), stop=(k == HT - 1)),
                         incs=[(TS, 1)] if k == HT - 1 else [])
                rl_tok = TS.now()
                s0 = tt * 32
                rr = rsc[:, s0:s0 + 32]
                sm = rsc[:, 128 + s0:128 + s0 + 32]
                t8 = rsc[:, 256 + tt * 8:256 + tt * 8 + 8]
                nmx = rsc[:, 288 + tt:288 + tt + 1]
                ism = rsc[:, 296 + tt:296 + tt + 1]
                p.em("vector", [rl_tok],
                     lambda e, rr=rr, bank=bank: e.tensor_copy(rr, pool[bank][:, 0:NE]),
                     incs=[(VS, 1)])
                rr_tok = VS.now()
                bank_free[bank] = [rr_tok]
                p.em("vector", [rr_tok],
                     lambda e, rr=rr, nmx=nmx: e.reduce_max(
                         nmx, rr, axis=AX.X, negate=True), incs=[(VS, 1)])
                p.em("scalar", [VS.now()],
                     lambda e, rr=rr, sm=sm, nmx=nmx: e.activation(
                         sm, rr, AF.Exp, bias=nmx), incs=[(SS, 1)])
                p.em("vector", [SS.now()],
                     lambda e, sm=sm, ism=ism: e.reduce_sum(ism, sm, axis=AX.X),
                     incs=[(VS, 1)])
                p.em("vector", [VS.now()],
                     lambda e, ism=ism: e.reciprocal(ism, ism), incs=[(VS, 1)])
                p.em("vector", [VS.now()],
                     lambda e, sm=sm, ism=ism: e.tensor_scalar(
                         sm, sm, ism, None, ALU.mult), incs=[(VS, 1)])
                rw_tok = VS.now()
                p.em("vector", [rw_tok, ebias_tok],
                     lambda e, rr=rr, sm=sm: e.tensor_add(rr, sm, g["ebias_bc"][:, :]),
                     incs=[(VS, 1)])
                p.em("vector", [VS.now()],
                     lambda e, rr=rr, t8=t8: e.max(t8, rr), incs=[(VS, 1)])
                p.em("vector", [VS.now()],
                     lambda e, rr=rr, t8=t8: e.tensor_single_scalar(
                         rr, rr, t8[:, 5:6], ALU.is_ge), incs=[(VS, 1)])
                p.em("vector", [VS.now()],
                     lambda e, rr=rr, sm=sm: e.tensor_mul(sm, rr, sm), incs=[(VS, 1)])
                cm_tok = VS.now()
                p.em("vector", [cm_tok],
                     lambda e, sm=sm, ism=ism: e.reduce_sum(ism, sm, axis=AX.X),
                     incs=[(VS, 1)])
                p.em("vector", [VS.now()],
                     lambda e, ism=ism: e.tensor_scalar_max(ism, ism, 1e-12),
                     incs=[(VS, 1)])
                p.em("vector", [VS.now()],
                     lambda e, ism=ism: e.reciprocal(ism, ism), incs=[(VS, 1)])
                p.em("vector", [VS.now()],
                     lambda e, sm=sm, ism=ism: e.tensor_scalar(
                         sm, sm, ism, None, ALU.mult), incs=[(VS, 1)])
                comb_tok = VS.now()
                bank2 = (tt + 1) % 4
                p.em("tensor", [comb_tok] + bank_free[bank2],
                     lambda e, sm=sm, bank2=bank2: e.transpose(
                         pool[bank2][0:32, 0:128], sm, fg["identF"][:, :]),
                     incs=[(TS, 1)])
                tr_tok = TS.now()
                p.em("scalar", [tr_tok],
                     lambda e, gt0=gt0, tt=tt, bank2=bank2: e.copy(
                         fg["combT"][0:32, gt0 + tt * 128:gt0 + (tt + 1) * 128],
                         pool[bank2][0:32, 0:128]),
                     incs=[(SS, 1)])
                bank_free[bank2] = [SS.now()]

            ev_sg = {}
            for o in range(HT):
                bank = o % 4
                for k in range(HT):
                    w4 = []
                    if k == 0:
                        w4 += bank_free[bank]
                        if o == 0:
                            w4 += sgsu_guard
                    p.em("tensor", w4,
                         lambda e, o=o, k=k, bank=bank: e.matmul(
                             pool[bank][:],
                             Wg[:, (k * HT + o) * 128:(k * HT + o + 1) * 128],
                             xlnb[:, k * Tc:(k + 1) * Tc],
                             start=(k == 0), stop=(k == HT - 1)),
                         incs=[(TS, 1)] if k == HT - 1 else [])
                mm_tok = TS.now()
                p.em("scalar", [mm_tok],
                     lambda e, o=o, bank=bank: e.activation(
                         sg[:, o * Tc:(o + 1) * Tc], pool[bank][:], AF.Silu),
                     incs=[(SS, 1)])
                ev_sg[o] = SS.now()
                bank_free[bank] = [ev_sg[o]]
            sgsu_guard = []
            ev_su = {}
            for o in range(HT):
                bank = o % 4
                for k in range(HT):
                    w4 = bank_free[bank] if k == 0 else []
                    p.em("tensor", w4,
                         lambda e, o=o, k=k, bank=bank: e.matmul(
                             pool[bank][:],
                             Wu[:, (k * HT + o) * 128:(k * HT + o + 1) * 128],
                             xlnb[:, k * Tc:(k + 1) * Tc],
                             start=(k == 0), stop=(k == HT - 1)),
                         incs=[(TS, 1)] if k == HT - 1 else [])
                mm_tok = TS.now()
                p.em("scalar", [mm_tok],
                     lambda e, o=o, bank=bank: e.copy(
                         su[:, o * Tc:(o + 1) * Tc], pool[bank][:]),
                     incs=[(SS, 1)])
                ev_su[o] = SS.now()
                bank_free[bank] = [ev_su[o]]
            p.em("vector", [ev_sg[HT - 1], ev_su[HT - 1]],
                 lambda e: e.tensor_mul(sh[:, :], sg[:, :], su[:, :]), incs=[(VS, 1)])
            sh_tok = VS.now()
            sgsu_guard = [sh_tok]
            ev_sd = {}
            for o in range(HT):
                bank = o % 4
                for k in range(HT):
                    w4 = []
                    if k == 0:
                        w4 += bank_free[bank]
                        if o == 0:
                            w4 += [sh_tok] + sho_guard
                    p.em("tensor", w4,
                         lambda e, o=o, k=k, bank=bank: e.matmul(
                             pool[bank][:],
                             Wd[:, (k * HT + o) * 128:(k * HT + o + 1) * 128],
                             sh[:, k * Tc:(k + 1) * Tc],
                             start=(k == 0), stop=(k == HT - 1)),
                         incs=[(TS, 1)] if k == HT - 1 else [])
                mm_tok = TS.now()
                p.em("scalar", [mm_tok] + (sho_guard if o == 0 else []),
                     lambda e, o=o, bank=bank: e.copy(
                         sho[:, o * Tc:(o + 1) * Tc], pool[bank][:]),
                     incs=[(SS, 1)])
                ev_sd[o] = SS.now()
                bank_free[bank] = [ev_sd[o]]
            sho_guard = []
            p.em("scalar", [ev_sd[HT - 1]], lambda e, gt0=gt0: e.dma_start(
                out=dram["sp_shared"][:, :, gt0:gt0 + Tc].rearrange("h p t -> p h t"),
                in_=sho[:, :]), incs=[(CD, 16)])
            sho_guard = [CD.now()]


def _fhalf(p, nc, c, fg, x_in, x_out, final, xout, hh, tag):
    dram, g = c["dram"], c["g"]
    SD, GD, CD = c["SD"], c["GD"], c["CD"]
    TS, VS, SS, GS = c["TS"], c["VS"], c["SS"], c["GS"]
    Tc = 512
    NCHH = L // Tc
    gbase = hh * L
    b = hh
    with ExitStack() as st:
        EC = st.enter_context
        xlh = EC(nc.sbuf_tensor(tag + f"h{hh}_xln", [128, HT * L], BF16))
        acc = EC(nc.sbuf_tensor(tag + f"h{hh}_acc", [128, HT * L], F32))
        wgu = EC(nc.sbuf_tensor(tag + f"h{hh}_wgu", [128, 2 * HT * 2 * I_EXP], BF16))
        wdn = EC(nc.sbuf_tensor(tag + f"h{hh}_wdn", [128, 2 * 4 * H], BF16))
        sgt = EC(nc.sbuf_tensor(tag + f"h{hh}_sg", [128, 2 * 4 * Tc], BF16))
        hmt = EC(nc.sbuf_tensor(tag + f"h{hh}_hm", [128, 2 * 4 * Tc], BF16))
        xrf = EC(nc.sbuf_tensor(tag + f"h{hh}_xr", [128, HT * Tc], F32))
        pool = [EC(nc.psum_tensor(f"{tag}h{hh}_p{i}", [128, Tc], F32)) for i in range(4)]
        pacc = [EC(nc.psum_tensor(f"{tag}h{hh}_a{i}", [128, Tc], F32)) for i in range(2)]
        pcbc = EC(nc.psum_tensor(f"{tag}h{hh}_cb", [128, Tc], F32))

        p.em("sync", [], lambda e: e.dma_start(
            out=xlh[:, :],
            in_=dram["sp_xln"][:, :, gbase:gbase + L].rearrange("h p t -> p h t")),
            incs=[(SD, 16)])
        p.em("gpsimd", [], lambda e: e.dma_start(
            out=acc[:, :],
            in_=dram["sp_shared"][:, :, gbase:gbase + L].rearrange("h p t -> p h t")),
            incs=[(GD, 16)])
        ld_tok = SD.now()
        acc_init = GD.now()

        wready = {}
        def load_expert(e_ix, slot, waits):
            p.em("gpsimd", waits, lambda e, e_ix=e_ix, slot=slot: e.dma_start(
                out=wgu[:, slot * HT * 2 * I_EXP:(slot + 1) * HT * 2 * I_EXP],
                in_=dram["gup_T"][e_ix].rearrange("k p f -> p k f")),
                incs=[(GD, 16)])
            p.em("gpsimd", [], lambda e, e_ix=e_ix, slot=slot: e.dma_start(
                out=wdn[:, slot * 4 * H:(slot + 1) * 4 * H],
                in_=dram["down_T"][e_ix].rearrange("k p f -> p k f")),
                incs=[(GD, 16)])
            wready[e_ix] = GD.now()

        wfree = {0: [], 1: []}
        load_expert(0, 0, [])
        load_expert(1, 1, [])
        bank_free = {i: [] for i in range(4)}
        acc_bank_free = {i: [] for i in range(2)}
        sg_free = {0: [], 1: []}
        hm_free = {0: [], 1: []}
        NSTEP = NE * NCHH
        pcbc_free = {0: []}
        pend = None

        def _emit_down(state):
            e2, ch2, par2, hm2t = state
            t02 = ch2 * Tc
            slot2 = e2 % 2
            wd_v = wdn[:, slot2 * 4 * H:(slot2 + 1) * 4 * H]
            hm_v2 = hmt[:, par2 * 4 * Tc:(par2 + 1) * 4 * Tc]
            for oo in range(8):
                bk = oo % 2
                for k in range(4):
                    w4 = []
                    if k == 0:
                        w4 += acc_bank_free[bk]
                        if oo == 0:
                            w4 += [hm2t]
                    p.em("tensor", w4,
                         lambda e, oo=oo, k=k, bk=bk, wd_v=wd_v, hm_v2=hm_v2:
                         e.matmul(pacc[bk][:],
                                  wd_v[:, k * H + oo * 128:k * H + (oo + 1) * 128],
                                  hm_v2[:, k * Tc:(k + 1) * Tc],
                                  start=(k == 0), stop=(k == 3)),
                         incs=[(TS, 1)] if k == 3 else [])
                mm_tok = TS.now()
                a_sl = acc[:, oo * L + t02:oo * L + t02 + Tc]
                w5 = [mm_tok] + ([acc_init] if e2 == 0 else [])
                p.em("vector", w5,
                     lambda e, a_sl=a_sl, bk=bk: e.tensor_add(a_sl, a_sl, pacc[bk][:]),
                     incs=[(VS, 1)])
                acc_bank_free[bk] = [VS.now()]
            hm_free[par2] = [TS.now()]
            if ch2 == NCHH - 1:
                wfree[slot2] = [TS.now()]

        for s in range(NSTEP):
            e_ix, ch = divmod(s, NCHH)
            slot = e_ix % 2
            par = s % 2
            t0 = ch * Tc
            wg_v = wgu[:, slot * HT * 2 * I_EXP:(slot + 1) * HT * 2 * I_EXP]
            sg_v = sgt[:, par * 4 * Tc:(par + 1) * 4 * Tc]
            hm_v = hmt[:, par * 4 * Tc:(par + 1) * 4 * Tc]
            if ch == 1 and 1 <= e_ix and e_ix + 1 < NE:
                load_expert(e_ix + 1, 1 - slot, wfree[1 - slot])
            ev = {}
            for o in range(4):
                for k in range(HT):
                    w4 = []
                    if k == 0:
                        w4 += bank_free[o]
                        if o == 0 and ch == 0:
                            w4 += [wready[e_ix]]
                        if s == 0:
                            w4 += [ld_tok]
                    p.em("tensor", w4,
                         lambda e, o=o, k=k, wg_v=wg_v, t0=t0: e.matmul(
                             pool[o][:],
                             wg_v[:, k * 2 * I_EXP + o * 128:
                                  k * 2 * I_EXP + (o + 1) * 128],
                             xlh[:, k * L + t0:k * L + t0 + Tc],
                             start=(k == 0), stop=(k == HT - 1)),
                         incs=[(TS, 1)] if k == HT - 1 else [])
                ev[o] = TS.now()
            sg_toks = {}
            for o in range(4):
                p.em("scalar", [ev[o]] + (sg_free[par] if o == 0 else []),
                     lambda e, o=o, sg_v=sg_v: e.activation(
                         sg_v[:, o * Tc:(o + 1) * Tc], pool[o][:], AF.Silu),
                     incs=[(SS, 1)])
                sg_toks[o] = SS.now()
            evu = {}
            for o in range(4):
                for k in range(HT):
                    w4 = [sg_toks[o]] if k == 0 else []
                    p.em("tensor", w4,
                         lambda e, o=o, k=k, wg_v=wg_v, t0=t0: e.matmul(
                             pool[o][:],
                             wg_v[:, k * 2 * I_EXP + (o + 4) * 128:
                                  k * 2 * I_EXP + (o + 5) * 128],
                             xlh[:, k * L + t0:k * L + t0 + Tc],
                             start=(k == 0), stop=(k == HT - 1)),
                         incs=[(TS, 1)] if k == HT - 1 else [])
                evu[o] = TS.now()
            p.em("tensor", pcbc_free[0],
                 lambda e, e_ix=e_ix, t0=t0: e.matmul(
                     pcbc[:], fg["indE"][:, e_ix * 128:(e_ix + 1) * 128],
                     fg["combT"][0:32, gbase + t0:gbase + t0 + Tc],
                     start=True, stop=True), incs=[(TS, 1)])
            cbc_tok = TS.now()
            hm_toks = {}
            for o in range(4):
                p.em("vector", [evu[o]] + (hm_free[par] if o == 0 else []),
                     lambda e, o=o, sg_v=sg_v, hm_v=hm_v: e.tensor_mul(
                         hm_v[:, o * Tc:(o + 1) * Tc],
                         sg_v[:, o * Tc:(o + 1) * Tc], pool[o][:]),
                     incs=[(VS, 1)])
                hm_toks[o] = VS.now()
            sg_free[par] = [hm_toks[3]]
            for o in range(4):
                p.em("vector", [cbc_tok] if o == 0 else [],
                     lambda e, o=o, hm_v=hm_v: e.tensor_mul(
                         hm_v[:, o * Tc:(o + 1) * Tc],
                         hm_v[:, o * Tc:(o + 1) * Tc], pcbc[:]),
                     incs=[(VS, 1)])
            hm2_tok = VS.now()
            pcbc_free[0] = [hm2_tok]
            for o in range(4):
                bank_free[o] = [hm_toks[o]]
            bank_free[0] = [hm2_tok]
            if pend is not None:
                _emit_down(pend)
            pend = (e_ix, ch, par, hm2_tok)
        _emit_down(pend)

        if DEBUG_COMBT and tag == "f1" and hh == 0:
            p.em("sync", [VS.now()], lambda e: e.dma_start(
                out=dram["dbg_acc"][:, :], in_=acc[:, :]), incs=[(SD, 16)])

        if DEBUG_COMBT and tag == "f1" and hh == 0:
            p.em("sync", [VS.now()], lambda e: e.dma_start(
                out=dram["dbg_acc"][:, :], in_=acc[:, :]), incs=[(SD, 16)])
        xr_free2 = []
        for ch in range(NCHH):
            t0 = ch * Tc
            p.em("sync", xr_free2, lambda e, t0=t0: e.dma_start(
                out=xrf[:, :],
                in_=x_in[b, :, t0:t0 + Tc].rearrange("(h p) t -> p h t", p=128)),
                incs=[(SD, 16)])
            xr_tok = SD.now()
            add_toks = []
            for o in range(HT):
                p.em("vector", [xr_tok],
                     lambda e, o=o, t0=t0: e.tensor_add(
                         xrf[:, o * Tc:(o + 1) * Tc],
                         xrf[:, o * Tc:(o + 1) * Tc],
                         acc[:, o * L + t0:o * L + t0 + Tc]),
                     incs=[(VS, 1)])
                add_toks.append(VS.now())
            p.em("sync", [add_toks[-1]], lambda e, t0=t0: e.dma_start(
                out=x_out[b, :, t0:t0 + Tc].rearrange("(h p) t -> p h t", p=128),
                in_=xrf[:, :]), incs=[(SD, 16)])
            xr_free2 = [SD.now()]
            if final and xout is not None:
                p.em("sync", [], lambda e, t0=t0: e.dma_start(
                    out=xout[b, :, t0:t0 + Tc].rearrange("(h p) t -> p h t", p=128),
                    in_=xrf[:, :]), incs=[(SD, 16)])
                xr_free2.append(SD.now())


def phase_attn(p, nc, c, x_in, x_out, final, xout, tag):
    TS, VS, SS, GS = c["TS"], c["VS"], c["SS"], c["GS"]
    SD, GD, CD = c["SD"], c["GD"], c["CD"]
    for b in range(NB):
        with ExitStack() as ast_:
            EC = ast_.enter_context
            qb = EC(nc.sbuf_tensor(tag + f"{b}_q", [128, HT * L], BF16))
            kb = EC(nc.sbuf_tensor(tag + f"{b}_k", [128, HT * L], BF16))
            vT = EC(nc.sbuf_tensor(tag + f"{b}_vT", [128, (L // 128) * H], BF16))
            _attn_qkv(p, nc, c, x_in, b, qb, kb, vT, tag)
            p.join([TS.now(), VS.now(), SS.now(), GS.now(),
                    SD.now(), GD.now(), CD.now()])
            _attn_core(p, nc, c, x_in, x_out, final, xout, b, qb, kb, vT, tag)
            p.join([TS.now(), VS.now(), SS.now(), GS.now(),
                    SD.now(), GD.now(), CD.now()])


def _attn_qkv(p, nc, c, x_in, b, qb, kb, vT, tag):
    dram, g = c["dram"], c["g"]
    SD, GD, CD = c["SD"], c["GD"], c["CD"]
    TS, VS, SS, GS = c["TS"], c["VS"], c["SS"], c["GS"]
    Tc = 512
    NCH = L // Tc
    with ExitStack() as st:
        EC = st.enter_context
        Wq = EC(nc.sbuf_tensor(tag + f"{b}q_Wq", [128, HT * H], BF16))
        Wk = EC(nc.sbuf_tensor(tag + f"{b}q_Wk", [128, HT * H], BF16))
        Wv = EC(nc.sbuf_tensor(tag + f"{b}q_Wv", [128, HT * H], BF16))
        xr = EC(nc.sbuf_tensor(tag + f"{b}q_xr", [128, HT * Tc], F32))
        xlnb = EC(nc.sbuf_tensor(tag + f"{b}q_xln", [128, HT * Tc], BF16))
        xbq = EC(nc.sbuf_tensor(tag + f"{b}q_xbq", [128, 2 * HT * Tc], BF16))
        rows = EC(nc.sbuf_tensor(tag + f"{b}q_rows", [1, 5 * Tc], F32))
        rows_b = EC(nc.sbuf_tensor(tag + f"{b}q_rows_b", [1, 2 * Tc], BF16))
        pool = [EC(nc.psum_tensor(f"{tag}{b}q_p{i}", [128, Tc], F32)) for i in range(4)]
        pst = [EC(nc.psum_tensor(f"{tag}{b}q_s{i}", [1, Tc], F32)) for i in range(2)]
        pbc = [EC(nc.psum_tensor(f"{tag}{b}q_b{i}", [128, Tc], F32)) for i in range(2)]

        for nm, dst in (("Wq_T", Wq), ("Wk_T", Wk), ("Wv_T", Wv)):
            p.em("gpsimd", [], lambda e, nm=nm, dst=dst: e.dma_start(
                out=dst[:, :], in_=dram[nm].rearrange("k p f -> p k f")),
                incs=[(GD, 16)])
        w_tok = GD.now()

        lnsc = dict(onesk=g["onesk"], onem=g["onem"], pm=pst[0], pv=pst[1],
                    pmb=pbc[0], prb=pbc[1], bc_free=[], stat_free=[])
        bank_free = {i: [] for i in range(4)}
        xr_free = {0: [], 1: []}

        for ch in range(NCH):
            buf = 0
            t0 = ch * Tc
            xr_v = xr[:, :]
            p.em("sync", xr_free[buf], lambda e, t0=t0, xr_v=xr_v: e.dma_start(
                out=xr_v,
                in_=x_in[b, :, t0:t0 + Tc].rearrange("(h p) t -> p h t", p=128)),
                incs=[(SD, 16)])
            xr_tok = SD.now()
            lnsc.update(
                xb=xbq[:, 0:HT * Tc], xsq=xbq[:, HT * Tc:2 * HT * Tc],
                mrow=rows[:, 0:Tc], msq=rows[:, Tc:2 * Tc],
                vrow=rows[:, 2 * Tc:3 * Tc], lrow=rows[:, 3 * Tc:4 * Tc],
                rrow=rows[:, 4 * Tc:5 * Tc],
                mrow_b=rows_b[:, 0:Tc], rrow_b=rows_b[:, Tc:2 * Tc],
                dtile=xbq[:, HT * Tc:2 * HT * Tc],
                xr_ready=[xr_tok],
            )
            xln_tok = emit_ln(p, c, xr_v, xlnb, Tc, lnsc)
            xr_free[buf] = [xln_tok]

            for which, W, dst, scale in (("q", Wq, qb, 1.0 / 32.0),
                                         ("k", Wk, kb, None)):
                for o in range(HT):
                    bank = o % 4
                    for k in range(HT):
                        w4 = []
                        if k == 0:
                            w4 += bank_free[bank]
                            if o == 0 and which == "q":
                                w4 += [xln_tok, w_tok]
                        p.em("tensor", w4,
                             lambda e, o=o, k=k, bank=bank, W=W: e.matmul(
                                 pool[bank][:],
                                 W[:, (k * HT + o) * 128:(k * HT + o + 1) * 128],
                                 xlnb[:, k * Tc:(k + 1) * Tc],
                                 start=(k == 0), stop=(k == HT - 1)),
                             incs=[(TS, 1)] if k == HT - 1 else [])
                    mm_tok = TS.now()
                    if scale is not None:
                        p.em("scalar", [mm_tok],
                             lambda e, o=o, bank=bank, dst=dst, t0=t0, scale=scale:
                             e.activation(
                                 dst[:, o * L + t0:o * L + t0 + Tc], pool[bank][:],
                                 AF.Copy, scale=scale),
                             incs=[(SS, 1)])
                    else:
                        p.em("scalar", [mm_tok],
                             lambda e, o=o, bank=bank, dst=dst, t0=t0: e.copy(
                                 dst[:, o * L + t0:o * L + t0 + Tc], pool[bank][:]),
                             incs=[(SS, 1)])
                    bank_free[bank] = [SS.now()]
            for tt in range(4):
                kt = ch * 4 + tt
                for nn in range(2):
                    bank = (tt * 2 + nn) % 4
                    for k in range(HT):
                        w4 = bank_free[bank] if k == 0 else []
                        p.em("tensor", w4,
                             lambda e, tt=tt, nn=nn, k=k, bank=bank: e.matmul(
                                 pool[bank][:],
                                 xlnb[:, k * Tc + tt * 128:k * Tc + tt * 128 + 128],
                                 Wv[:, k * H + nn * 512:k * H + (nn + 1) * 512],
                                 start=(k == 0), stop=(k == HT - 1)),
                             incs=[(TS, 1)] if k == HT - 1 else [])
                    mm_tok = TS.now()
                    p.em("scalar", [mm_tok],
                         lambda e, kt=kt, nn=nn, bank=bank: e.copy(
                             vT[:, kt * H + nn * 512:kt * H + (nn + 1) * 512],
                             pool[bank][:]),
                         incs=[(SS, 1)])
                    bank_free[bank] = [SS.now()]


def _attn_core(p, nc, c, x_in, x_out, final, xout, b, qb, kb, vT, tag):
    dram, g = c["dram"], c["g"]
    SD, GD, CD = c["SD"], c["GD"], c["CD"]
    TS, VS, SS, GS = c["TS"], c["VS"], c["SS"], c["GS"]
    NKT = L // 128
    with ExitStack() as st:
        EC = st.enter_context
        Wo = EC(nc.sbuf_tensor(tag + f"{b}a_Wo", [128, HT * H], BF16))
        scr = EC(nc.sbuf_tensor(tag + f"{b}a_scr", [128, 4 * L], F32))
        abuf = EC(nc.sbuf_tensor(tag + f"{b}a_ab", [128, 4 * L], BF16))
        atT = EC(nc.sbuf_tensor(tag + f"{b}a_atT", [128, NKT * 512], BF16))
        ctx = EC(nc.sbuf_tensor(tag + f"{b}a_ctx", [128, HT * 512], BF16))
        xrs = EC(nc.sbuf_tensor(tag + f"{b}a_xr", [128, HT * 512], F32))
        srow = EC(nc.sbuf_tensor(tag + f"{b}a_srow", [128, 16], F32))
        pool = [EC(nc.psum_tensor(f"{tag}{b}a_p{i}", [128, 512], F32)) for i in range(3)]
        ptr = EC(nc.psum_tensor(f"{tag}{b}a_tr", [128, 128], BF16))
        pctx = [EC(nc.psum_tensor(f"{tag}{b}a_c{i}", [128, 512], F32)) for i in range(4)]

        p.em("gpsimd", [], lambda e: e.dma_start(
            out=Wo[:, :], in_=dram["W_outB_T"].rearrange("k p f -> p k f")),
            incs=[(GD, 16)])
        w_tok = GD.now()

        bank_free = {i: [] for i in range(3)}
        ptr_free = []
        pctx_free = {i: [] for i in range(4)}
        scr_free = []
        abuf_free = []
        atT_free = []
        ctx_free = []
        xr_free = []

        for gg in range(4):
            q0 = gg * 512
            sm_toks = []
            for q4 in range(4):
                qt0 = q0 + q4 * 128
                ev_toks = []
                for nn in range(4):
                    bank = (q4 * 4 + nn) % 3
                    for k in range(HT):
                        w4 = []
                        if k == 0:
                            w4 += bank_free[bank]
                            if q4 == 0 and nn == 0 and gg == 0:
                                w4 += [w_tok] + scr_free
                            if q4 == 0 and nn == 0:
                                w4 += scr_free
                        p.em("tensor", w4,
                             lambda e, q4=q4, nn=nn, k=k, bank=bank, qt0=qt0: e.matmul(
                                 pool[bank][:],
                                 qb[:, k * L + qt0:k * L + qt0 + 128],
                                 kb[:, k * L + nn * 512:k * L + (nn + 1) * 512],
                                 start=(k == 0), stop=(k == HT - 1)),
                             incs=[(TS, 1)] if k == HT - 1 else [])
                    mm_tok = TS.now()
                    p.em("scalar", [mm_tok],
                         lambda e, q4=q4, nn=nn, bank=bank: e.copy(
                             scr[:, q4 * L + nn * 512:q4 * L + (nn + 1) * 512],
                             pool[bank][:]),
                         incs=[(SS, 1)])
                    ev_toks.append(SS.now())
                    bank_free[bank] = [ev_toks[-1]]
                nmx = srow[:, q4 * 4:q4 * 4 + 1]
                ism = srow[:, q4 * 4 + 1:q4 * 4 + 2]
                p.em("vector", [ev_toks[-1]],
                     lambda e, q4=q4, nmx=nmx: e.reduce_max(
                         nmx, scr[:, q4 * L:(q4 + 1) * L], axis=AX.X, negate=True),
                     incs=[(VS, 1)])
                p.em("scalar", [VS.now()],
                     lambda e, q4=q4, nmx=nmx: e.activation(
                         scr[:, q4 * L:(q4 + 1) * L], scr[:, q4 * L:(q4 + 1) * L],
                         AF.Exp, bias=nmx),
                     incs=[(SS, 1)])
                p.em("vector", [SS.now()],
                     lambda e, q4=q4, ism=ism: e.reduce_sum(
                         ism, scr[:, q4 * L:(q4 + 1) * L], axis=AX.X), incs=[(VS, 1)])
                p.em("vector", [VS.now()],
                     lambda e, ism=ism: e.reciprocal(ism, ism), incs=[(VS, 1)])
                p.em("vector", [VS.now()] + (abuf_free if q4 == 0 else []),
                     lambda e, q4=q4, ism=ism: e.tensor_scalar(
                         abuf[:, q4 * L:(q4 + 1) * L], scr[:, q4 * L:(q4 + 1) * L],
                         ism, None, ALU.mult),
                     incs=[(VS, 1)])
                sm_toks.append(VS.now())
            abuf_free = []
            scr_free = [sm_toks[-1]]
            tr_toks = []
            for q4 in range(4):
                for j in range(NKT):
                    w4 = [sm_toks[q4]] if j == 0 else []
                    w4 += ptr_free
                    if q4 == 0 and j == 0:
                        w4 += atT_free
                    p.em("tensor", w4,
                         lambda e, q4=q4, j=j: e.transpose(
                             ptr[:], abuf[:, q4 * L + j * 128:q4 * L + (j + 1) * 128],
                             g["ident"][:, :]),
                         incs=[(TS, 1)])
                    t_tok = TS.now()
                    p.em("scalar", [t_tok],
                         lambda e, q4=q4, j=j: e.copy(
                             atT[:, j * 512 + q4 * 128:j * 512 + (q4 + 1) * 128],
                             ptr[:]),
                         incs=[(SS, 1)])
                    ptr_free = [SS.now()]
            atT_ready = SS.now()
            abuf_free = [TS.now()]
            ctx_toks = []
            for half in range(2):
                for o in range(4):
                    oo = half * 4 + o
                    for j in range(NKT):
                        w4 = []
                        if j == 0:
                            w4 += pctx_free[o]
                            if half == 0 and o == 0:
                                w4 += [atT_ready] + ctx_free
                        p.em("tensor", w4,
                             lambda e, oo=oo, j=j, o=o: e.matmul(
                                 pctx[o][:],
                                 vT[:, j * H + oo * 128:j * H + (oo + 1) * 128],
                                 atT[:, j * 512:(j + 1) * 512],
                                 start=(j == 0), stop=(j == NKT - 1)),
                             incs=[(TS, 1)] if j == NKT - 1 else [])
                    mm_tok = TS.now()
                    p.em("scalar", [mm_tok],
                         lambda e, oo=oo, o=o: e.copy(
                             ctx[:, oo * 512:(oo + 1) * 512], pctx[o][:]),
                         incs=[(SS, 1)])
                    pctx_free[o] = [SS.now()]
                    ctx_toks.append(SS.now())
            atT_free = [TS.now()]
            ctx_ready = ctx_toks[-1]
            p.em("sync", xr_free, lambda e, q0=q0: e.dma_start(
                out=xrs[:, :],
                in_=x_in[b, :, q0:q0 + 512].rearrange("(h p) t -> p h t", p=128)),
                incs=[(SD, 16)])
            xr_tok = SD.now()
            add_toks = []
            for o in range(HT):
                bank = o % 3
                for k in range(HT):
                    w4 = []
                    if k == 0:
                        w4 += bank_free[bank]
                        if o == 0:
                            w4 += [ctx_ready, w_tok]
                    p.em("tensor", w4,
                         lambda e, o=o, k=k, bank=bank: e.matmul(
                             pool[bank][:],
                             Wo[:, (k * HT + o) * 128:(k * HT + o + 1) * 128],
                             ctx[:, k * 512:(k + 1) * 512],
                             start=(k == 0), stop=(k == HT - 1)),
                         incs=[(TS, 1)] if k == HT - 1 else [])
                mo_tok = TS.now()
                p.em("vector", [mo_tok, xr_tok],
                     lambda e, o=o, bank=bank: e.tensor_add(
                         xrs[:, o * 512:(o + 1) * 512],
                         xrs[:, o * 512:(o + 1) * 512], pool[bank][:]),
                     incs=[(VS, 1)])
                add_toks.append(VS.now())
                bank_free[bank] = [add_toks[-1]]
            ctx_free = [TS.now()]
            p.em("sync", [add_toks[-1]], lambda e, q0=q0: e.dma_start(
                out=x_out[b, :, q0:q0 + 512].rearrange("(h p) t -> p h t", p=128),
                in_=xrs[:, :]), incs=[(SD, 16)])
            xr_free = [SD.now()]
            if final and xout is not None:
                p.em("sync", [], lambda e, q0=q0: e.dma_start(
                    out=xout[b, :, q0:q0 + 512].rearrange("(h p) t -> p h t", p=128),
                    in_=xrs[:, :]), incs=[(SD, 16)])
                xr_free.append(SD.now())



def _pack_pc(v):
    return np.ascontiguousarray(v.reshape(CT, 128).T)


def _prep_inputs(inputs):
    f = {}
    x = np.ascontiguousarray(inputs["L0_attn_a_x"])
    W_in = inputs["L0_attn_a_W_in"]
    f["W_in_T"] = np.ascontiguousarray(W_in.T).reshape(HT, 128, 2 * INNER)
    cw = inputs["L0_attn_a_conv_w"][:, 0, :]
    f["cw_p"] = np.ascontiguousarray(
        cw.reshape(CT, 128, 4).transpose(1, 0, 2).reshape(128, CT * 4))
    f["cb_p"] = _pack_pc(inputs["L0_attn_a_conv_b"])
    W_x = inputs["L0_attn_a_W_x"]
    f["W_BC_T"] = np.ascontiguousarray(W_x[:32].T).reshape(CT, 128, 32)
    f["W_dtin_T"] = np.ascontiguousarray(W_x[32:].T).reshape(CT, 128, INNER)
    f["W_dt_T"] = np.ascontiguousarray(inputs["L0_attn_a_W_dt"].T).reshape(CT, 128, INNER)
    f["bdt_p"] = _pack_pc(inputs["L0_attn_a_b_dt"])
    f["dv_p"] = _pack_pc(inputs["L0_attn_a_D"])
    f["W_outA_T"] = np.ascontiguousarray(inputs["L0_attn_a_W_out"].T).reshape(CT, 128, H)
    W_qkv = inputs["L0_attn_b_W_qkv"]
    f["Wq_T"] = np.ascontiguousarray(W_qkv[0::3].T).reshape(HT, 128, H)
    f["Wk_T"] = np.ascontiguousarray(W_qkv[1::3].T).reshape(HT, 128, H)
    f["Wv_T"] = np.ascontiguousarray(W_qkv[2::3].T).reshape(HT, 128, H)
    f["W_outB_T"] = np.ascontiguousarray(inputs["L0_attn_b_W_out"].T).reshape(HT, 128, H)
    f["Wr_T"] = np.ascontiguousarray(
        inputs["L0_ffn_router_weight"].T).reshape(HT, 128, NE)
    f["e_bias"] = np.ascontiguousarray(inputs["L0_ffn_e_score_bias"]).reshape(1, NE)
    f["Wg_T"] = np.ascontiguousarray(
        inputs["L0_ffn_shared_gate_w"].T).reshape(HT, 128, IS)
    f["Wu_T"] = np.ascontiguousarray(
        inputs["L0_ffn_shared_up_w"].T).reshape(HT, 128, IS)
    f["Wd_T"] = np.ascontiguousarray(
        inputs["L0_ffn_shared_down_w"].T).reshape(IS // 128, 128, H)
    f["gup_T"] = np.ascontiguousarray(
        inputs["L0_ffn_gate_up_proj"].transpose(0, 2, 1)).reshape(NE, HT, 128, 2 * I_EXP)
    f["down_T"] = np.ascontiguousarray(
        inputs["L0_ffn_down_proj"].transpose(0, 2, 1)).reshape(NE, 4, 128, H)

    shards = []
    for cid in range(8):
        d = dict(f)
        xs = x[cid * NB:(cid + 1) * NB]
        d["x0"] = np.ascontiguousarray(xs.transpose(0, 2, 1))
        shards.append(d)
    return shards


def kernel(**inputs):
    shards = _prep_inputs(inputs)
    nc = build_program(upto="all")
    res = run_bass_kernel_spmd(nc, shards, core_ids=list(range(8)))
    outs = []
    for cid in range(8):
        xo = res.results[cid]["xout"]
        outs.append(np.ascontiguousarray(np.asarray(xo).transpose(0, 2, 1)))
    return np.concatenate(outs, axis=0).astype(np.float32)


# version 13 (final)
# speedup vs baseline: 1.1593x; 1.0462x over previous
import numpy as np
import ml_dtypes
from contextlib import ExitStack

import concourse.bass as bass
import concourse.mybir as mybir
from concourse.bass_utils import run_bass_kernel_spmd

F32 = mybir.dt.float32
BF16 = mybir.dt.bfloat16
AF = mybir.ActivationFunctionType
ALU = mybir.AluOpType
AX = mybir.AxisListType

H = 1024
INNER = 2048
NB = 2
L = 2048
T = NB * L
NE = 32
I_EXP = 512
IS = 1024
EPS = 1e-5
HT = H // 128
CT = INNER // 128


class Ctr:
    def __init__(self, handle):
        self.h = handle
        self.n = 0

    def now(self):
        return (self.h, self.n)


class Prog:
    ENGINES = ("sync", "gpsimd", "vector", "scalar", "tensor")

    def __init__(self, nc):
        self.nc = nc
        self.q = {e: [] for e in self.ENGINES}
        self.pending = {e: [] for e in self.ENGINES}

    def em(self, eng, waits, f, incs=()):
        w = list(self.pending[eng]) + [x for x in waits if x]
        self.pending[eng] = []
        bumped = []
        for c, k in incs:
            c.n += k
            bumped.append((c.h, k))
        self.q[eng].append((tuple(w), f, tuple(bumped)))

    def join(self, tokens):
        for e in self.ENGINES:
            self.pending[e].extend(tokens)

    def run(self, block):
        hw = {e: {} for e in self.ENGINES}

        def make(eng_name):
            items = self.q[eng_name]
            marks = hw[eng_name]

            def body(engine):
                for waits, f, incs in items:
                    for (h, v) in waits:
                        if marks.get(h, -1) < v:
                            engine.wait_ge(h, v)
                            marks[h] = v
                    ins = f(engine)
                    for (h, k) in incs:
                        ins = ins.then_inc(h, k)
            return body

        block.sync(make("sync"))
        block.gpsimd(make("gpsimd"))
        block.vector(make("vector"))
        block.scalar(make("scalar"))
        block.tensor(make("tensor"))



def emit_ln(p, c, xr, x_ln_out, Tc, sc):
    SS, VS, TS = c["SS"], c["VS"], c["TS"]
    g = c["g"]
    n = HT * Tc
    fp32s = sc.get("fp32_stats", False)
    if fp32s:
        xb_tok = None
        p.em("scalar", sc["xr_ready"] + sc.get("xsq_free", []),
             lambda e: e.activation(sc["d32"][:, 0:n], xr[:, 0:n], AF.Square),
             incs=[(SS, 1)])
        xsq_tok = SS.now()
        for h in range(HT):
            p.em("tensor", (sc["xr_ready"] + sc["stat_free"]) if h == 0 else [],
                 lambda e, h=h: e.matmul(sc["pm"][:, 0:Tc], g["onem_f"][:, 0:1],
                                         xr[:, h * Tc:(h + 1) * Tc],
                                         start=(h == 0), stop=(h == HT - 1)),
                 incs=[(TS, 1)] if h == HT - 1 else [])
        pm_tok = TS.now()
        for h in range(HT):
            p.em("tensor", [xsq_tok] if h == 0 else [],
                 lambda e, h=h: e.matmul(sc["pv"][:, 0:Tc], g["onem_f"][:, 0:1],
                                         sc["d32"][:, h * Tc:(h + 1) * Tc],
                                         start=(h == 0), stop=(h == HT - 1)),
                 incs=[(TS, 1)] if h == HT - 1 else [])
        pv_tok = TS.now()
    else:
        p.em("scalar", sc["xr_ready"] + sc.get("xb_free", []),
             lambda e: e.copy(sc["xb"][:, 0:n], xr[:, 0:n]), incs=[(SS, 1)])
        xb_tok = SS.now()
        p.em("scalar", sc.get("xsq_free", []),
             lambda e: e.activation(sc["xsq"][:, 0:n], xr[:, 0:n], AF.Square),
             incs=[(SS, 1)])
        xsq_tok = SS.now()
        for h in range(HT):
            p.em("tensor", ([xb_tok] + sc["stat_free"]) if h == 0 else [],
                 lambda e, h=h: e.matmul(sc["pm"][:, 0:Tc], g["onem"][:, 0:1],
                                         sc["xb"][:, h * Tc:(h + 1) * Tc],
                                         start=(h == 0), stop=(h == HT - 1)),
                 incs=[(TS, 1)] if h == HT - 1 else [])
        pm_tok = TS.now()
        for h in range(HT):
            p.em("tensor", [xsq_tok] if h == 0 else [],
                 lambda e, h=h: e.matmul(sc["pv"][:, 0:Tc], g["onem"][:, 0:1],
                                         sc["xsq"][:, h * Tc:(h + 1) * Tc],
                                         start=(h == 0), stop=(h == HT - 1)),
                 incs=[(TS, 1)] if h == HT - 1 else [])
        pv_tok = TS.now()
    p.em("vector", [pm_tok], lambda e: e.tensor_copy(sc["mrow"][:, 0:Tc], sc["pm"][:, 0:Tc]),
         incs=[(VS, 1)])
    m_tok = VS.now()
    p.em("vector", [m_tok], lambda e: e.tensor_mul(
        sc["msq"][:, 0:Tc], sc["mrow"][:, 0:Tc], sc["mrow"][:, 0:Tc]), incs=[(VS, 1)])
    p.em("vector", [VS.now(), pv_tok], lambda e: e.tensor_sub(
        sc["vrow"][:, 0:Tc], sc["pv"][:, 0:Tc], sc["msq"][:, 0:Tc]), incs=[(VS, 1)])
    v_tok = VS.now()
    p.em("scalar", [v_tok], lambda e: e.activation(
        sc["lrow"][:, 0:Tc], sc["vrow"][:, 0:Tc], AF.Ln, bias=c["g"]["cbias"][0:1, 0:1]), incs=[(SS, 1)])
    p.em("scalar", [SS.now()], lambda e: e.activation(
        sc["rrow"][:, 0:Tc], sc["lrow"][:, 0:Tc], AF.Exp, scale=-0.5), incs=[(SS, 1)])
    r_tok = SS.now()
    sc["stat_free"] = [m_tok, v_tok]
    if fp32s:
        p.em("tensor", [m_tok] + sc["bc_free"], lambda e: e.matmul(
            sc["pmb"][:, 0:Tc], g["onesk_f"][:, 0:128], sc["mrow"][:, 0:Tc],
            start=True, stop=True), incs=[(TS, 1)])
        p.em("tensor", [r_tok], lambda e: e.matmul(
            sc["prb"][:, 0:Tc], g["onesk_f"][:, 0:128], sc["rrow"][:, 0:Tc],
            start=True, stop=True), incs=[(TS, 1)])
        bc_tok = TS.now()
    else:
        p.em("vector", [m_tok], lambda e: e.tensor_copy(
            sc["mrow_b"][:, 0:Tc], sc["mrow"][:, 0:Tc]), incs=[(VS, 1)])
        mb_tok = VS.now()
        p.em("scalar", [r_tok], lambda e: e.copy(
            sc["rrow_b"][:, 0:Tc], sc["rrow"][:, 0:Tc]), incs=[(SS, 1)])
        rb_tok = SS.now()
        p.em("tensor", [mb_tok] + sc["bc_free"], lambda e: e.matmul(
            sc["pmb"][:, 0:Tc], g["onesk"][:, 0:128], sc["mrow_b"][:, 0:Tc],
            start=True, stop=True), incs=[(TS, 1)])
        p.em("tensor", [rb_tok], lambda e: e.matmul(
            sc["prb"][:, 0:Tc], g["onesk"][:, 0:128], sc["rrow_b"][:, 0:Tc],
            start=True, stop=True), incs=[(TS, 1)])
        bc_tok = TS.now()
    x32 = sc.get("xln32_out")
    for h in range(HT):
        dt_v = sc["d32"] if x32 is not None else sc["dtile"]
        p.em("vector", [bc_tok] if h == 0 else [],
             lambda e, h=h, dt_v=dt_v: e.tensor_sub(dt_v[:, h * Tc:(h + 1) * Tc],
                                                    xr[:, h * Tc:(h + 1) * Tc],
                                                    sc["pmb"][:, 0:Tc]),
             incs=[])
        p.em("vector", sc.get("out_free", []) if h == 0 else [],
             lambda e, h=h, dt_v=dt_v: e.tensor_mul(x_ln_out[:, h * Tc:(h + 1) * Tc],
                                                    dt_v[:, h * Tc:(h + 1) * Tc],
                                                    sc["prb"][:, 0:Tc]),
             incs=[(VS, 1)])
        if x32 is not None:
            p.em("vector", [],
                 lambda e, h=h, dt_v=dt_v: e.tensor_mul(x32[:, h * Tc:(h + 1) * Tc],
                                                        dt_v[:, h * Tc:(h + 1) * Tc],
                                                        sc["prb"][:, 0:Tc]),
                 incs=[(VS, 1)])
    done = VS.now()
    sc["out_free"] = []
    sc["bc_free"] = [done]
    sc["xb_free"] = [pm_tok]
    sc["xsq_free"] = [pv_tok, done]
    return done



DEBUG_COMBT = False


def build_program(upto="all"):
    nc = bass.Bass("TRN2", target_bir_lowering=False)
    dram = {}

    def din(name, shape, dtype=F32):
        dram[name] = nc.dram_tensor(name, shape, dtype, kind="ExternalInput").ap()
        return dram[name]

    def dint(name, shape, dtype):
        dram[name] = nc.dram_tensor(name, shape, dtype, kind="Internal").ap()
        return dram[name]

    x0 = din("x0", [NB, H, L])
    din("W_in_T", [HT, 128, 2 * INNER])
    din("cw_p", [128, CT * 4])
    din("cb_p", [128, CT])
    din("W_BC_T", [CT, 128, 32])
    din("W_dtin_T", [CT, 128, INNER])
    din("W_dt_T", [CT, 128, INNER])
    din("bdt_p", [128, CT])
    din("dv_p", [128, CT])
    din("W_outA_T", [CT, 128, H])
    din("Wq_T", [HT, 128, H])
    din("Wk_T", [HT, 128, H])
    din("Wv_T", [HT, 128, H])
    din("W_outB_T", [HT, 128, H])
    din("Wr_T", [HT, 128, NE])
    din("e_bias", [1, NE])
    din("Wg_T", [HT, 128, IS])
    din("Wu_T", [HT, 128, IS])
    din("Wd_T", [IS // 128, 128, H])
    din("gup_T", [NE, HT, 128, 2 * I_EXP])
    din("down_T", [NE, 4, 128, H])

    xout = nc.dram_tensor("xout", [NB, H, L], F32, kind="ExternalOutput").ap()
    if DEBUG_COMBT:
        dram["dbg_combT"] = nc.dram_tensor("dbg_combT", [32, T], BF16,
                                           kind="ExternalOutput").ap()
        dram["dbg_xln"] = nc.dram_tensor("dbg_xln", [HT, 128, T], BF16,
                                         kind="ExternalOutput").ap()
        dram["dbg_sh"] = nc.dram_tensor("dbg_sh", [HT, 128, T], BF16,
                                        kind="ExternalOutput").ap()
        dram["dbg_acc"] = nc.dram_tensor("dbg_acc", [128, HT * L], F32,
                                         kind="ExternalOutput").ap()
        dram["dbg_hm"] = nc.dram_tensor("dbg_hm", [128, 4 * 512], BF16,
                                        kind="ExternalOutput").ap()
        dram["dbg_sg"] = nc.dram_tensor("dbg_sg", [128, 4 * 512], BF16,
                                        kind="ExternalOutput").ap()
    xa = dint("xa", [NB, H, L], F32)
    xb_h = dint("xb", [NB, H, L], F32)
    dint("sp_xconv", [NB, CT, 128, L], BF16)
    dint("sp_z", [NB, CT, 128, L], BF16)
    dint("sp_dtin", [NB, CT, 128, L], BF16)
    dint("sp_xln", [HT, 128, T], BF16)
    dint("sp_shared", [HT, 128, T], BF16)

    with ExitStack() as top:
        EC = top.enter_context
        sems = {}
        for nm in ("SD", "GD", "CD", "TS", "VS", "SS", "GS"):
            sems[nm] = Ctr(EC(nc.semaphore(nm)))
        p = Prog(nc)
        c = dict(p=p, nc=nc, dram=dram, sems=sems,
                 SD=sems["SD"], GD=sems["GD"], CD=sems["CD"], TS=sems["TS"],
                 VS=sems["VS"], SS=sems["SS"], GS=sems["GS"])

        g = {}
        g["onesk"] = EC(nc.sbuf_tensor("onesk", [1, 128], BF16))
        g["onem"] = EC(nc.sbuf_tensor("onem", [128, 1], BF16))
        g["selBC"] = EC(nc.sbuf_tensor("selBC", [32, 2], BF16))
        g["sel2"] = EC(nc.sbuf_tensor("sel2", [2, 256], BF16))
        g["ident"] = EC(nc.sbuf_tensor("ident", [128, 128], BF16))
        g["cw"] = EC(nc.sbuf_tensor("cw", [128, CT * 4], F32))
        g["cb"] = EC(nc.sbuf_tensor("cb", [128, CT], F32))
        g["bdt"] = EC(nc.sbuf_tensor("bdt", [128, CT], F32))
        g["dvec"] = EC(nc.sbuf_tensor("dvec", [128, CT], F32))
        g["ebias_bc"] = EC(nc.sbuf_tensor("ebias_bc", [128, NE], F32))
        g["onesk_f"] = EC(nc.sbuf_tensor("onesk_f", [1, 128], F32))
        g["cbias"] = EC(nc.sbuf_tensor("cbias", [128, 2], F32))
        g["onem_f"] = EC(nc.sbuf_tensor("onem_f", [128, 1], F32))
        g["ebias_row"] = EC(nc.sbuf_tensor("ebias_row", [1, NE], F32))
        c["g"] = g

        build_constants(p, c)

        phase_mamba(p, nc, c, x_in=x0, x_out=xa,
                    final=(upto == "mamba"), xout=xout)

        if upto in ("ffn1", "attn", "all"):
            phase_ffn(p, nc, c, x_in=xa, x_out=xb_h,
                      final=(upto == "ffn1"), xout=xout, tag="f1")
        if upto in ("attn", "all"):
            phase_attn(p, nc, c, x_in=xb_h, x_out=xa,
                       final=(upto == "attn"), xout=xout, tag="at")
        if upto == "all":
            phase_ffn(p, nc, c, x_in=xa, x_out=xout,
                      final=False, xout=None, tag="f2")

        SD, GD, CD = sems["SD"], sems["GD"], sems["CD"]
        p.em("sync", [SD.now(), GD.now(), CD.now()],
             lambda e: e.wait_ge(SD.h, SD.n))

        block = EC(nc.Block())
        p.run(block)
    return nc


def build_constants(p, c):
    nc, g, dram = c["nc"], c["g"], c["dram"]
    GS, VS, SD, GD = c["GS"], c["VS"], c["SD"], c["GD"]
    with ExitStack() as st:
        iosc = st.enter_context(nc.sbuf_tensor("iosc", [128, 512], F32))
        p.em("gpsimd", [], lambda e: e.iota(
            iosc[:, 0:128], pattern=[[1, 128]], base=0, channel_multiplier=-1,
            allow_small_or_imprecise_dtypes=True), incs=[(GS, 1)])
        p.em("vector", [GS.now()], lambda e: e.tensor_single_scalar(
            g["ident"][:], iosc[:, 0:128], 0.0, ALU.is_equal), incs=[(VS, 1)])
        p.em("vector", [], lambda e: e.memset(g["onesk"][:, :], 1.0), incs=[(VS, 1)])
        p.em("vector", [], lambda e: e.memset(g["onesk_f"][:, :], 1.0), incs=[(VS, 1)])
        p.em("vector", [], lambda e: e.memset(g["onem"][:, :], 1.0 / 1024.0),
             incs=[(VS, 1)])
        p.em("vector", [], lambda e: e.memset(g["cbias"][:, 0:1], EPS), incs=[(VS, 1)])
        p.em("vector", [], lambda e: e.memset(g["onem_f"][:, :], 1.0 / 1024.0),
             incs=[(VS, 1)])
        p.em("vector", [], lambda e: e.memset(g["cbias"][:, 1:2], 1.0), incs=[(VS, 1)])
        p.em("gpsimd", [], lambda e: e.iota(
            iosc[0:32, 128:130], pattern=[[0, 2]], base=0, channel_multiplier=1,
            allow_small_or_imprecise_dtypes=True), incs=[(GS, 1)])
        p.em("vector", [GS.now()], lambda e: e.tensor_single_scalar(
            iosc[0:32, 130:132], iosc[0:32, 128:130], 16.0, ALU.is_lt), incs=[(VS, 1)])
        tk2 = VS.now()
        p.em("vector", [tk2], lambda e: e.tensor_scalar(
            g["selBC"][0:32, 0:1], iosc[0:32, 130:131], 1.0 / 16.0, None, ALU.mult),
            incs=[(VS, 1)])
        p.em("vector", [tk2], lambda e: e.tensor_scalar(
            iosc[0:32, 132:133], iosc[0:32, 131:132], -1.0, 1.0, ALU.mult, ALU.add),
            incs=[(VS, 1)])
        p.em("vector", [VS.now()], lambda e: e.tensor_scalar(
            g["selBC"][0:32, 1:2], iosc[0:32, 132:133], 1.0 / 16.0, None, ALU.mult),
            incs=[(VS, 1)])
        p.em("gpsimd", [], lambda e: e.iota(
            iosc[0:2, 256:512], pattern=[[1, 2], [0, 128]], base=0,
            channel_multiplier=-1, allow_small_or_imprecise_dtypes=True),
            incs=[(GS, 1)])
        p.em("vector", [GS.now()], lambda e: e.tensor_single_scalar(
            g["sel2"][:], iosc[0:2, 256:512], 0.0, ALU.is_equal), incs=[(VS, 1)])
        for nm, dst in (("cw_p", g["cw"]), ("cb_p", g["cb"]), ("bdt_p", g["bdt"]),
                        ("dv_p", g["dvec"])):
            p.em("sync", [], lambda e, nm=nm, dst=dst: e.dma_start(
                out=dst[:, :], in_=dram[nm][:, :]), incs=[(SD, 16)])
        p.em("sync", [], lambda e: e.dma_start(
            out=g["ebias_row"][:, :], in_=dram["e_bias"][:, :]), incs=[(SD, 16)])
        p.join([VS.now(), GS.now(), SD.now()])



def phase_mamba(p, nc, c, x_in, x_out, final=False, xout=None):
    dram, g = c["dram"], c["g"]
    SD, GD, CD = c["SD"], c["GD"], c["CD"]
    TS, VS, SS, GS = c["TS"], c["VS"], c["SS"], c["GS"]
    Tc = 512
    NCH = L // Tc

    with ExitStack() as mst:
        g["bgcg"] = mst.enter_context(nc.sbuf_tensor("bgcg", [2, T], BF16))
        _phase_mamba_body(p, nc, c, x_in, x_out, final, xout, Tc, NCH)
    p.join([c["TS"].now(), c["VS"].now(), c["SS"].now(), c["GS"].now(),
            c["SD"].now(), c["GD"].now(), c["CD"].now()])


def _phase_mamba_body(p, nc, c, x_in, x_out, final, xout, Tc, NCH):
    TS, VS, SS, GS = c["TS"], c["VS"], c["SS"], c["GS"]
    SD, GD, CD = c["SD"], c["GD"], c["CD"]
    _m1(p, nc, c, x_in, Tc, NCH)
    p.join([TS.now(), VS.now(), SS.now(), GS.now(), SD.now(), GD.now(), CD.now()])
    _m2a(p, nc, c, Tc, NCH)
    p.join([TS.now(), VS.now(), SS.now(), GS.now(), SD.now(), GD.now(), CD.now()])
    _m2b(p, nc, c, x_in, x_out, final, xout)


def _m1(p, nc, c, x_in, Tc, NCH):
    dram, g = c["dram"], c["g"]
    SD, GD, CD = c["SD"], c["GD"], c["CD"]
    TS, VS, SS, GS = c["TS"], c["VS"], c["SS"], c["GS"]

    with ExitStack() as st:
        EC = st.enter_context
        W = EC(nc.sbuf_tensor("m1_W", [128, HT * 2 * INNER], BF16))
        xr = EC(nc.sbuf_tensor("m1_xr", [128, HT * Tc], F32))
        xlnb = EC(nc.sbuf_tensor("m1_xln", [128, HT * Tc], BF16))
        xbq = EC(nc.sbuf_tensor("m1_xbq", [128, 2 * HT * Tc], BF16))
        rows = EC(nc.sbuf_tensor("m1_rows", [1, 5 * Tc], F32))
        rows_b = EC(nc.sbuf_tensor("m1_rows_b", [1, 2 * Tc], BF16))
        xp = EC(nc.sbuf_tensor("m1_xp", [128, 2 * CT * (Tc + 3)], BF16))
        xcv = EC(nc.sbuf_tensor("m1_xcv", [128, CT * Tc], BF16))
        zs = EC(nc.sbuf_tensor("m1_zs", [128, CT * Tc], BF16))
        cacc = EC(nc.sbuf_tensor("m1_cacc", [128, 2 * Tc], F32))
        pmm = [EC(nc.psum_tensor(f"m1_p{i}", [128, Tc], F32)) for i in range(4)]
        pst = [EC(nc.psum_tensor(f"m1_s{i}", [1, Tc], F32)) for i in range(2)]
        pbc = [EC(nc.psum_tensor(f"m1_b{i}", [128, Tc], F32)) for i in range(2)]

        p.em("gpsimd", [], lambda e: e.dma_start(
            out=W[:, :], in_=dram["W_in_T"].rearrange("k p f -> p k f")),
            incs=[(GD, 16)])
        w_tok = GD.now()

        lnsc = dict(onesk=g["onesk"], onem=g["onem"], pm=pst[0], pv=pst[1],
                    pmb=pbc[0], prb=pbc[1], bc_free=[], stat_free=[])
        bank_free = {i: [] for i in range(4)}
        xr_free = {0: [], 1: []}
        xp_free = {cix: [] for cix in range(CT)}
        zs_guard = []
        xcv_guard = []

        XPS = CT * (Tc + 3)
        acc_free = {0: [], 1: []}
        pendc = {"v": None}

        def _emit_conv(state):
            nonlocal xcv_guard
            b2, ch2, t02, gp2, evtoks = state
            xpv = xp[:, gp2 * XPS:(gp2 + 1) * XPS]
            xpn = xp[:, (1 - gp2) * XPS:(2 - gp2) * XPS]
            conv_last = None
            for cix in range(CT):
                base = cix * (Tc + 3)
                par = cix % 2
                acc = cacc[:, par * Tc:(par + 1) * Tc]
                p.em("vector", [evtoks[cix]] + acc_free[par],
                     lambda e, cix=cix, base=base, acc=acc, xpv=xpv: e.tensor_scalar(
                         acc, xpv[:, base + 0:base + 0 + Tc],
                         g["cw"][:, cix * 4 + 0:cix * 4 + 1], None, ALU.mult),
                     incs=[])
                for k in range(1, 4):
                    p.em("vector", [],
                         lambda e, cix=cix, k=k, base=base, acc=acc, xpv=xpv:
                         e.scalar_tensor_tensor(
                             acc, xpv[:, base + k:base + k + Tc],
                             g["cw"][:, cix * 4 + k:cix * 4 + k + 1], acc,
                             ALU.mult, ALU.add),
                         incs=[(VS, 1)] if k == 3 else [])
                cv_tok = VS.now()
                p.em("scalar", [cv_tok] + xcv_guard,
                     lambda e, cix=cix, acc=acc: e.activation(
                         xcv[:, cix * Tc:(cix + 1) * Tc], acc, AF.Silu,
                         bias=g["cb"][:, cix:cix + 1]),
                     incs=[(SS, 1)])
                xcv_guard = []
                conv_last = SS.now()
                acc_free[par] = [conv_last]
                if ch2 < NCH - 1:
                    p.em("vector", [],
                         lambda e, base=base, xpv=xpv, xpn=xpn: e.tensor_copy(
                             xpn[:, base:base + 3], xpv[:, base + Tc:base + Tc + 3]),
                         incs=[(VS, 1)])
                xp_free[gp2] = [VS.now()]
            p.em("scalar", [conv_last], lambda e, b2=b2, t02=t02: e.dma_start(
                out=dram["sp_xconv"][b2, :, :, t02:t02 + Tc]
                .rearrange("c p t -> p c t"),
                in_=xcv[:, :]), incs=[(CD, 16)])
            xcv_guard = [CD.now()]

        for b in range(NB):
            for ch in range(NCH):
                buf = ch % 2
                gp = (b * NCH + ch) % 2
                t0 = ch * Tc
                if ch == 0:
                    p.em("vector", xp_free[gp], lambda e, gp=gp: e.memset(
                        xp[:, gp * XPS:(gp + 1) * XPS]
                        .rearrange("p (c t) -> p c t", c=CT)[:, :, 0:3], 0.0),
                        incs=[(VS, 1)])
                    xp_free[gp] = [VS.now()]
                xr_v = xr[:, :]
                p.em("sync", xr_free[0], lambda e, b=b, t0=t0, xr_v=xr_v: e.dma_start(
                    out=xr_v,
                    in_=x_in[b, :, t0:t0 + Tc].rearrange("(h p) t -> p h t", p=128)),
                    incs=[(SD, 16)])
                xr_tok = SD.now()

                lnsc.update(
                    xb=xbq[:, 0:HT * Tc], xsq=xbq[:, HT * Tc:2 * HT * Tc],
                    mrow=rows[:, 0:Tc], msq=rows[:, Tc:2 * Tc],
                    vrow=rows[:, 2 * Tc:3 * Tc], lrow=rows[:, 3 * Tc:4 * Tc],
                    rrow=rows[:, 4 * Tc:5 * Tc],
                    mrow_b=rows_b[:, 0:Tc], rrow_b=rows_b[:, Tc:2 * Tc],
                    dtile=xbq[:, HT * Tc:2 * HT * Tc],
                    xr_ready=[xr_tok],
                )
                xln_tok = emit_ln(p, c, xr_v, xlnb, Tc, lnsc)
                xr_free[0] = [xln_tok]

                evac_tok = {}
                for o in range(32):
                    bank = o % 4
                    for k in range(HT):
                        waits = []
                        if k == 0:
                            waits += bank_free[bank]
                            if o == 0:
                                waits += [xln_tok, w_tok]
                        p.em("tensor", waits,
                             lambda e, o=o, k=k, bank=bank: e.matmul(
                                 pmm[bank][:],
                                 W[:, (k * 32 + o) * 128:(k * 32 + o + 1) * 128],
                                 xlnb[:, k * Tc:(k + 1) * Tc],
                                 start=(k == 0), stop=(k == HT - 1)),
                             incs=[(TS, 1)] if k == HT - 1 else [])
                    mm_tok = TS.now()
                    if o < CT:
                        dst = xp[:, gp * XPS + o * (Tc + 3) + 3:
                                 gp * XPS + o * (Tc + 3) + 3 + Tc]
                        w2 = [mm_tok] + xp_free[gp]
                        p.em("scalar", w2,
                             lambda e, dst=dst, bank=bank: e.copy(dst, pmm[bank][:]),
                             incs=[(SS, 1)])
                    else:
                        cix = o - CT
                        p.em("scalar", [mm_tok] + zs_guard,
                             lambda e, cix=cix, bank=bank: e.activation(
                                 zs[:, cix * Tc:(cix + 1) * Tc], pmm[bank][:], AF.Silu),
                             incs=[(SS, 1)])
                        zs_guard = []
                    evac_tok[o] = SS.now()
                    bank_free[bank] = [evac_tok[o]]

                p.em("scalar", [evac_tok[31]], lambda e, b=b, t0=t0: e.dma_start(
                    out=dram["sp_z"][b, :, :, t0:t0 + Tc].rearrange("c p t -> p c t"),
                    in_=zs[:, :]), incs=[(CD, 16)])
                zst_tok = CD.now()
                zs_guard = [zst_tok]

                if pendc["v"] is not None:
                    _emit_conv(pendc["v"])
                pendc["v"] = (b, ch, t0, gp, evac_tok)
        _emit_conv(pendc["v"])
        pendc["v"] = None

    p.join([TS.now(), VS.now(), SS.now(), GS.now(), SD.now(), GD.now(), CD.now()])

def _m2a(p, nc, c, Tc, NCH):
    dram, g = c["dram"], c["g"]
    SD, GD, CD = c["SD"], c["GD"], c["CD"]
    TS, VS, SS, GS = c["TS"], c["VS"], c["SS"], c["GS"]

    with ExitStack() as st:
        EC = st.enter_context
        W = EC(nc.sbuf_tensor("m2a_W", [128, CT * INNER], BF16))
        Wbc = EC(nc.sbuf_tensor("m2a_Wbc", [128, CT * 32], BF16))
        xcv = EC(nc.sbuf_tensor("m2a_xcv", [128, 2 * CT * Tc], BF16))
        dti = EC(nc.sbuf_tensor("m2a_dti", [128, CT * Tc], BF16))
        bcv = EC(nc.sbuf_tensor("m2a_bcv", [32, Tc], BF16))
        pmm = [EC(nc.psum_tensor(f"m2a_p{i}", [128, Tc], F32)) for i in range(6)]
        pbcm = EC(nc.psum_tensor("m2a_bc", [32, Tc], F32))
        pmean = EC(nc.psum_tensor("m2a_mean", [2, Tc], F32))

        p.em("gpsimd", [], lambda e: e.dma_start(
            out=W[:, :], in_=dram["W_dtin_T"].rearrange("k p f -> p k f")),
            incs=[(GD, 16)])
        p.em("gpsimd", [], lambda e: e.dma_start(
            out=Wbc[:, :], in_=dram["W_BC_T"].rearrange("k p f -> p k f")),
            incs=[(GD, 16)])
        w_tok = GD.now()

        bank_free = {i: [] for i in range(6)}
        xcv_free = {0: [], 1: []}
        bc_free = []
        mean_free = []
        store_free = []

        for b in range(NB):
            for ch in range(NCH):
                buf = ch % 2
                t0 = ch * Tc
                xcv_v = xcv[:, buf * CT * Tc:(buf + 1) * CT * Tc]
                p.em("sync", xcv_free[buf], lambda e, b=b, t0=t0, xcv_v=xcv_v:
                     e.dma_start(
                         out=xcv_v,
                         in_=dram["sp_xconv"][b, :, :, t0:t0 + Tc]
                         .rearrange("c p t -> p c t")),
                     incs=[(SD, 16)])
                x_tok = SD.now()

                evac_tok = {}
                for o in range(CT):
                    bank = o % 6
                    for k in range(CT):
                        waits = []
                        if k == 0:
                            waits += bank_free[bank]
                            if o == 0:
                                waits += [x_tok, w_tok]
                        p.em("tensor", waits,
                             lambda e, o=o, k=k, bank=bank, xcv_v=xcv_v: e.matmul(
                                 pmm[bank][:],
                                 W[:, (k * CT + o) * 128:(k * CT + o + 1) * 128],
                                 xcv_v[:, k * Tc:(k + 1) * Tc],
                                 start=(k == 0), stop=(k == CT - 1)),
                             incs=[(TS, 1)] if k == CT - 1 else [])
                    mm_tok = TS.now()
                    p.em("scalar", [mm_tok] + (store_free if o == 0 else []),
                         lambda e, o=o, bank=bank: e.copy(
                             dti[:, o * Tc:(o + 1) * Tc], pmm[bank][:]),
                         incs=[(SS, 1)])
                    evac_tok[o] = SS.now()
                    bank_free[bank] = [evac_tok[o]]
                    if o == 0:
                        store_free = []
                for k in range(CT):
                    p.em("tensor", bc_free if k == 0 else [],
                         lambda e, k=k, xcv_v=xcv_v: e.matmul(
                             pbcm[:], Wbc[:, k * 32:(k + 1) * 32],
                             xcv_v[:, k * Tc:(k + 1) * Tc],
                             start=(k == 0), stop=(k == CT - 1)),
                         incs=[(TS, 1)] if k == CT - 1 else [])
                bcm_tok = TS.now()
                p.em("scalar", [bcm_tok], lambda e: e.copy(bcv[:, :], pbcm[:]),
                     incs=[(SS, 1)])
                bcv_tok = SS.now()
                bc_free = [bcv_tok]
                p.em("tensor", [bcv_tok] + mean_free, lambda e: e.matmul(
                    pmean[:], g["selBC"][:, :], bcv[:, :], start=True, stop=True),
                    incs=[(TS, 1)])
                mean_tok = TS.now()
                p.em("scalar", [mean_tok], lambda e, b=b, t0=t0: e.activation(
                    g["bgcg"][0:2, b * L + t0:b * L + t0 + Tc], pmean[:], AF.Sigmoid),
                    incs=[(SS, 1)])
                mean_free = [SS.now()]
                xcv_free[buf] = [mm_tok]
                p.em("scalar", [evac_tok[CT - 1]], lambda e, b=b, t0=t0: e.dma_start(
                    out=dram["sp_dtin"][b, :, :, t0:t0 + Tc].rearrange("c p t -> p c t"),
                    in_=dti[:, :]), incs=[(CD, 16)])
                store_free = [CD.now()]


def _m2b(p, nc, c, x_in, x_out, final, xout):
    dram, g = c["dram"], c["g"]
    SD, GD, CD = c["SD"], c["GD"], c["CD"]
    TS, VS, SS, GS = c["TS"], c["VS"], c["SS"], c["GS"]

    Tc2 = 256
    NCH2 = L // Tc2
    with ExitStack() as st:
        EC = st.enter_context
        Wdt = EC(nc.sbuf_tensor("m2b_Wdt", [128, CT * INNER], BF16))
        Wo = EC(nc.sbuf_tensor("m2b_Wo", [128, CT * H], BF16))
        dti = EC(nc.sbuf_tensor("m2b_dti", [128, CT * Tc2], BF16))
        xcv = EC(nc.sbuf_tensor("m2b_xcv", [128, CT * Tc2], BF16))
        zsb = EC(nc.sbuf_tensor("m2b_zs", [128, CT * Tc2], BF16))
        xrs = EC(nc.sbuf_tensor("m2b_xrs", [128, HT * Tc2], F32))
        U = EC(nc.sbuf_tensor("m2b_U", [128, CT * Tc2], F32))
        A = EC(nc.sbuf_tensor("m2b_A", [128, CT * Tc2], F32))
        Hs = EC(nc.sbuf_tensor("m2b_H", [128, CT * Tc2], F32))
        YB = EC(nc.sbuf_tensor("m2b_YB", [128, 2 * CT * Tc2], BF16))
        carry = EC(nc.sbuf_tensor("m2b_carry", [128, CT], F32))
        pmm = [EC(nc.psum_tensor(f"m2b_p{i}", [128, Tc2], F32)) for i in range(4)]
        pout = [EC(nc.psum_tensor(f"m2b_o{i}", [128, Tc2], F32)) for i in range(2)]
        pg = [EC(nc.psum_tensor(f"m2b_g{i}", [128, Tc2], F32)) for i in range(2)]

        p.em("gpsimd", [], lambda e: e.dma_start(
            out=Wdt[:, :], in_=dram["W_dt_T"].rearrange("k p f -> p k f")),
            incs=[(GD, 16)])
        p.em("gpsimd", [], lambda e: e.dma_start(
            out=Wo[:, :], in_=dram["W_outA_T"].rearrange("k p f -> p k f")),
            incs=[(GD, 16)])
        w_tok = GD.now()

        bank_free = {i: [] for i in range(4)}
        pout_free = {i: [] for i in range(2)}
        dti_free = {0: [], 1: []}
        xcv_free = []
        zs_free = []
        xr_free = []
        yb_free = {0: [], 1: []}
        chain_free_h = {0: [], 1: []}
        pg_free = []
        n2 = CT * Tc2
        pendq = []

        def _emit_outproj(state):
            b2, t02, par2, y2 = state
            yv = YB[:, par2 * n2:(par2 + 1) * n2]
            p.em("sync", xr_free, lambda e, b2=b2, t02=t02: e.dma_start(
                out=xrs[:, :],
                in_=x_in[b2, :, t02:t02 + Tc2].rearrange("(h p) t -> p h t", p=128)),
                incs=[(SD, 16)])
            xr_tok = SD.now()
            add_toks = []
            for o in range(HT):
                bank = o % 2
                for k in range(CT):
                    waits = []
                    if k == 0:
                        waits += pout_free[bank]
                        if o == 0:
                            waits.append(y2)
                    p.em("tensor", waits,
                         lambda e, o=o, k=k, bank=bank, yv=yv: e.matmul(
                             pout[bank][:],
                             Wo[:, (k * HT + o) * 128:(k * HT + o + 1) * 128],
                             yv[:, k * Tc2:(k + 1) * Tc2],
                             start=(k == 0), stop=(k == CT - 1)),
                         incs=[(TS, 1)] if k == CT - 1 else [])
                mo_tok = TS.now()
                p.em("vector", [mo_tok, xr_tok],
                     lambda e, o=o, bank=bank: e.tensor_add(
                         xrs[:, o * Tc2:(o + 1) * Tc2],
                         xrs[:, o * Tc2:(o + 1) * Tc2], pout[bank][:]),
                     incs=[(VS, 1)])
                add_toks.append(VS.now())
                pout_free[bank] = [add_toks[-1]]
            yb_free[par2] = [TS.now()]
            p.em("sync", [add_toks[-1]], lambda e, b2=b2, t02=t02: e.dma_start(
                out=x_out[b2, :, t02:t02 + Tc2].rearrange("(h p) t -> p h t", p=128),
                in_=xrs[:, :]), incs=[(SD, 16)])
            toks = [SD.now()]
            if final and xout is not None:
                p.em("sync", [], lambda e, b2=b2, t02=t02: e.dma_start(
                    out=xout[b2, :, t02:t02 + Tc2].rearrange("(h p) t -> p h t", p=128),
                    in_=xrs[:, :]), incs=[(SD, 16)])
                toks.append(SD.now())
            return toks

        for b in range(NB):
            p.em("vector", list(chain_free_h[0]) + list(chain_free_h[1]),
                 lambda e: e.memset(carry[:, :], 0.0),
                 incs=[(VS, 1)])
            carry_tok = VS.now()
            for ch in range(NCH2):
                buf = ch % 2
                par = (b * NCH2 + ch) % 2
                t0 = ch * Tc2
                dti_v = dti[:, 0:n2]
                xcv_v = xcv[:, 0:n2]
                p.em("sync", dti_free[0], lambda e, b=b, t0=t0, dti_v=dti_v:
                     e.dma_start(out=dti_v,
                                 in_=dram["sp_dtin"][b, :, :, t0:t0 + Tc2]
                                 .rearrange("c p t -> p c t")),
                     incs=[(SD, 16)])
                dti_tok = SD.now()
                p.em("sync", xcv_free, lambda e, b=b, t0=t0, xcv_v=xcv_v:
                     e.dma_start(out=xcv_v,
                                 in_=dram["sp_xconv"][b, :, :, t0:t0 + Tc2]
                                 .rearrange("c p t -> p c t")),
                     incs=[(SD, 16)])
                xcv_tok = SD.now()
                p.em("sync", zs_free, lambda e, b=b, t0=t0: e.dma_start(
                    out=zsb[:, :],
                    in_=dram["sp_z"][b, :, :, t0:t0 + Tc2].rearrange("c p t -> p c t")),
                    incs=[(SD, 16)])
                zs_tok = SD.now()

                evac_tok = {}
                for o in range(CT):
                    bank = o % 4
                    for k in range(CT):
                        waits = []
                        if k == 0:
                            waits += bank_free[bank]
                            if o == 0:
                                waits += [dti_tok, w_tok]
                        p.em("tensor", waits,
                             lambda e, o=o, k=k, bank=bank, dti_v=dti_v: e.matmul(
                                 pmm[bank][:],
                                 Wdt[:, (k * CT + o) * 128:(k * CT + o + 1) * 128],
                                 dti_v[:, k * Tc2:(k + 1) * Tc2],
                                 start=(k == 0), stop=(k == CT - 1)),
                             incs=[(TS, 1)] if k == CT - 1 else [])
                    mm_tok = TS.now()
                    cguard = chain_free_h[0] if o == 0 else (
                        chain_free_h[1] if o == 8 else [])
                    p.em("scalar", [mm_tok] + cguard,
                         lambda e, o=o, bank=bank: e.activation(
                             U[:, o * Tc2:(o + 1) * Tc2], pmm[bank][:],
                             AF.Identity, bias=g["bdt"][:, o:o + 1]),
                         incs=[(SS, 1)])
                    evac_tok[o] = SS.now()
                    bank_free[bank] = [evac_tok[o]]
                u_tok = evac_tok[CT - 1]
                dti_free[0] = [mm_tok]
                off = b * L + t0
                p.em("tensor", pg_free, lambda e, off=off: e.matmul(
                    pg[0][:], g["sel2"][:, 0:128], g["bgcg"][0:2, off:off + Tc2],
                    start=True, stop=True), incs=[(TS, 1)])
                p.em("tensor", [], lambda e, off=off: e.matmul(
                    pg[1][:], g["sel2"][:, 128:256], g["bgcg"][0:2, off:off + Tc2],
                    start=True, stop=True), incs=[(TS, 1)])
                gbc_tok = TS.now()

                if len(pendq) >= 2:
                    xr_free = _emit_outproj(pendq.pop(0))

                n2h = n2 // 2
                HC = [range(0, 8), range(8, 16)]
                sl = [slice(0, n2h), slice(n2h, n2)]
                u_h = [evac_tok[7], evac_tok[15]]
                tok = {}
                for hc in range(2):
                    p.em("scalar", [u_h[hc]], lambda e, s=sl[hc]: e.activation(
                        A[:, s], U[:, s], AF.Abs), incs=[(SS, 1)])
                    tok[("abs", hc)] = SS.now()
                for hc in range(2):
                    p.em("scalar", [], lambda e, s=sl[hc]: e.activation(
                        A[:, s], A[:, s], AF.Exp, scale=-1.0), incs=[(SS, 1)])
                for hc in range(2):
                    p.em("scalar", [], lambda e, s=sl[hc]: e.activation(
                        A[:, s], A[:, s], AF.Ln, bias=g["cbias"][:, 1:2]),
                        incs=[(SS, 1)])
                    tok[("ln", hc)] = SS.now()
                for hc in range(2):
                    p.em("vector", [tok[("ln", hc)], u_h[hc]],
                         lambda e, s=sl[hc]: e.scalar_tensor_tensor(
                             U[:, s], U[:, s], 0.0, A[:, s], ALU.max, ALU.add),
                         incs=[(VS, 1)])
                    tok[("spu", hc)] = VS.now()
                for hc in range(2):
                    p.em("scalar", [tok[("spu", hc)]],
                         lambda e, s=sl[hc]: e.activation(
                             A[:, s], U[:, s], AF.Sigmoid, scale=0.1),
                         incs=[(SS, 1)])
                    tok[("sig", hc)] = SS.now()
                for hc in range(2):
                    p.em("gpsimd", [tok[("sig", hc)]],
                         lambda e, s=sl[hc]: e.tensor_scalar(
                             U[:, s], A[:, s], -1.0, 1.0, ALU.mult, ALU.add),
                         incs=[(GS, 1)])
                    tok[("g1", hc)] = GS.now()
                for hc in range(2):
                    for i, cix in enumerate(HC[hc]):
                        p.em("vector",
                             [gbc_tok, tok[("g1", hc)]] if i == 0 else [],
                             lambda e, cix=cix: e.tensor_mul(
                                 U[:, cix * Tc2:(cix + 1) * Tc2],
                                 U[:, cix * Tc2:(cix + 1) * Tc2], pg[0][:]),
                             incs=[(VS, 1)] if i == 7 else [])
                    tok[("gb", hc)] = VS.now()
                for hc in range(2):
                    p.em("vector", [tok[("gb", hc)], xcv_tok],
                         lambda e, s=sl[hc], xcv_v=xcv_v: e.tensor_mul(
                             U[:, s], U[:, s], xcv_v[:, s]), incs=[(VS, 1)])
                    tok[("gx", hc)] = VS.now()
                for hc in range(2):
                    p.em("vector", [], lambda e, s=sl[hc]: e.tensor_scalar(
                        A[:, s], A[:, s], 0.01, 0.99, ALU.max, ALU.min),
                        incs=[(VS, 1)])
                    tok[("ac", hc)] = VS.now()
                for hc in range(2):
                    for i, cix in enumerate(HC[hc]):
                        w3 = [tok[("ac", hc)], tok[("gx", hc)]] if i == 0 else []
                        if ch == 0 and cix == 0:
                            w3.append(carry_tok)
                        p.em("vector", w3,
                             lambda e, cix=cix: e.tensor_tensor_scan(
                                 Hs[:, cix * Tc2:(cix + 1) * Tc2],
                                 A[:, cix * Tc2:(cix + 1) * Tc2],
                                 U[:, cix * Tc2:(cix + 1) * Tc2],
                                 carry[:, cix:cix + 1], ALU.mult, ALU.add),
                             incs=[(VS, 1)])
                        p.em("vector", [VS.now()],
                             lambda e, cix=cix: e.tensor_copy(
                                 carry[:, cix:cix + 1],
                                 Hs[:, (cix + 1) * Tc2 - 1:(cix + 1) * Tc2]),
                             incs=[(VS, 1)])
                for hc in range(2):
                    for cix in HC[hc]:
                        p.em("vector", [],
                             lambda e, cix=cix: e.tensor_mul(
                                 Hs[:, cix * Tc2:(cix + 1) * Tc2],
                                 Hs[:, cix * Tc2:(cix + 1) * Tc2], pg[1][:]),
                             incs=[])
                    for i, cix in enumerate(HC[hc]):
                        p.em("vector", [],
                             lambda e, cix=cix, xcv_v=xcv_v: e.scalar_tensor_tensor(
                                 U[:, cix * Tc2:(cix + 1) * Tc2],
                                 xcv_v[:, cix * Tc2:(cix + 1) * Tc2],
                                 g["dvec"][:, cix:cix + 1],
                                 Hs[:, cix * Tc2:(cix + 1) * Tc2],
                                 ALU.mult, ALU.add),
                             incs=[(VS, 1)] if i == 7 else [])
                    tok[("gd", hc)] = VS.now()
                    yv = YB[:, par * n2:(par + 1) * n2]
                    p.em("gpsimd",
                         [tok[("gd", hc)], zs_tok] +
                         (yb_free[par] + chain_free for _ in ())
                         if False else
                         [tok[("gd", hc)], zs_tok] +
                         (yb_free[par] if hc == 0 else []),
                         lambda e, s=sl[hc], yv=yv: e.tensor_mul(
                             yv[:, s], U[:, s], zsb[:, s]), incs=[(GS, 1)])
                    tok[("y", hc)] = GS.now()
                y_tok = tok[("y", 1)]
                gd_tok = tok[("gd", 1)]
                pg_free = [tok[("gd", 0)], gd_tok, y_tok]
                chain_free_h[0] = [tok[("y", 0)]]
                chain_free_h[1] = [tok[("y", 1)]]
                zs_free = [y_tok]
                xcv_free = [y_tok]
                pendq.append((b, t0, par, y_tok))
            while pendq:
                xr_free = _emit_outproj(pendq.pop(0))

    p.join([TS.now(), VS.now(), SS.now(), GS.now(), SD.now(), GD.now(), CD.now()])



def phase_ffn(p, nc, c, x_in, x_out, final, xout, tag):
    TS, VS, SS, GS = c["TS"], c["VS"], c["SS"], c["GS"]
    SD, GD, CD = c["SD"], c["GD"], c["CD"]
    with ExitStack() as fst:
        EC = fst.enter_context
        combT = EC(nc.sbuf_tensor(tag + "_combT", [32, T], BF16))
        indE = EC(nc.sbuf_tensor(tag + "_indE", [32, NE * 128], BF16))
        identF = EC(nc.sbuf_tensor(tag + "_idF", [128, 128], F32))
        fg = dict(combT=combT, indE=indE, identF=identF)
        with ExitStack() as st2:
            iosc = st2.enter_context(nc.sbuf_tensor(tag + "_iosc", [128, NE * 128], F32))
            p.em("gpsimd", [], lambda e: e.iota(
                iosc[0:32, :], pattern=[[1, NE], [0, 128]], base=0,
                channel_multiplier=-1, allow_small_or_imprecise_dtypes=True),
                incs=[(GS, 1)])
            p.em("vector", [GS.now()], lambda e: e.tensor_single_scalar(
                indE[:], iosc[0:32, :], 0.0, ALU.is_equal), incs=[(VS, 1)])
            p.em("gpsimd", [VS.now()], lambda e: e.iota(
                iosc[:, 0:128], pattern=[[1, 128]], base=0,
                channel_multiplier=-1, allow_small_or_imprecise_dtypes=True),
                incs=[(GS, 1)])
            p.em("vector", [GS.now()], lambda e: e.tensor_single_scalar(
                identF[:], iosc[:, 0:128], 0.0, ALU.is_equal),
                incs=[(VS, 1)])
            p.join([VS.now(), GS.now()])
        _f0(p, nc, c, fg, x_in, tag)
        if DEBUG_COMBT and tag == "f1":
            p.em("sync", [c["SS"].now(), c["CD"].now()], lambda e: e.dma_start(
                out=c["dram"]["dbg_combT"][:, :], in_=combT[:, :]),
                incs=[(SD, 16)])
            p.em("sync", [], lambda e: e.dma_start(
                out=c["dram"]["dbg_xln"][:, :, :], in_=c["dram"]["sp_xln"][:, :, :]),
                incs=[(SD, 16)])
            p.em("sync", [], lambda e: e.dma_start(
                out=c["dram"]["dbg_sh"][:, :, :], in_=c["dram"]["sp_shared"][:, :, :]),
                incs=[(SD, 16)])
        p.join([TS.now(), VS.now(), SS.now(), GS.now(), SD.now(), GD.now(), CD.now()])
        for hh in range(2):
            _fhalf(p, nc, c, fg, x_in, x_out, final, xout, hh, tag)
            p.join([TS.now(), VS.now(), SS.now(), GS.now(),
                    SD.now(), GD.now(), CD.now()])


def _f0(p, nc, c, fg, x_in, tag):
    dram, g = c["dram"], c["g"]
    SD, GD, CD = c["SD"], c["GD"], c["CD"]
    TS, VS, SS, GS = c["TS"], c["VS"], c["SS"], c["GS"]
    Tc = 512
    NCH = T // Tc
    with ExitStack() as st:
        EC = st.enter_context
        Wg = EC(nc.sbuf_tensor(tag + "0_Wg", [128, HT * IS], BF16))
        Wu = EC(nc.sbuf_tensor(tag + "0_Wu", [128, HT * IS], BF16))
        Wd = EC(nc.sbuf_tensor(tag + "0_Wd", [128, (IS // 128) * H], BF16))
        xr = EC(nc.sbuf_tensor(tag + "0_xr", [128, 2 * HT * Tc], F32))
        xlnb = EC(nc.sbuf_tensor(tag + "0_xln", [128, HT * Tc], BF16))
        xbq = EC(nc.sbuf_tensor(tag + "0_xbq", [128, 2 * HT * Tc], BF16))
        rows = EC(nc.sbuf_tensor(tag + "0_rows", [1, 5 * Tc], F32))
        rows_b = EC(nc.sbuf_tensor(tag + "0_rows_b", [1, 2 * Tc], BF16))
        sg = EC(nc.sbuf_tensor(tag + "0_sg", [128, HT * Tc], BF16))
        su = EC(nc.sbuf_tensor(tag + "0_su", [128, HT * Tc], BF16))
        sh = EC(nc.sbuf_tensor(tag + "0_sh", [128, HT * Tc], BF16))
        sho = EC(nc.sbuf_tensor(tag + "0_sho", [128, HT * Tc], BF16))
        rsc = EC(nc.sbuf_tensor(tag + "0_rsc", [128, 320], F32))
        d32 = EC(nc.sbuf_tensor(tag + "0_d32", [128, HT * Tc], F32))
        xln32 = EC(nc.sbuf_tensor(tag + "0_x32", [128, HT * Tc], F32))
        Wr32 = EC(nc.sbuf_tensor(tag + "0_Wr32", [128, HT * NE], F32))
        pool = [EC(nc.psum_tensor(f"{tag}0_p{i}", [128, Tc], F32)) for i in range(4)]
        pst = [EC(nc.psum_tensor(f"{tag}0_s{i}", [1, Tc], F32)) for i in range(2)]
        pbc = [EC(nc.psum_tensor(f"{tag}0_b{i}", [128, Tc], F32)) for i in range(2)]

        p.em("gpsimd", [], lambda e: e.dma_start(
            out=Wg[:, :], in_=dram["Wg_T"].rearrange("k p f -> p k f")),
            incs=[(GD, 16)])
        p.em("gpsimd", [], lambda e: e.dma_start(
            out=Wu[:, :], in_=dram["Wu_T"].rearrange("k p f -> p k f")),
            incs=[(GD, 16)])
        p.em("gpsimd", [], lambda e: e.dma_start(
            out=Wd[:, :], in_=dram["Wd_T"].rearrange("k p f -> p k f")),
            incs=[(GD, 16)])
        p.em("sync", [], lambda e: e.dma_start(
            out=Wr32[:, :], in_=dram["Wr_T"].rearrange("k p f -> p k f")),
            incs=[(SD, 16)])
        w_tok = GD.now()
        wr_tok = SD.now()

        ebb = pool[0]
        p.em("tensor", [], lambda e: e.matmul(
            ebb[:, 0:NE], g["onesk_f"][:, 0:128], g["ebias_row"][:, 0:NE],
            start=True, stop=True), incs=[(TS, 1)])
        p.em("vector", [TS.now()], lambda e: e.tensor_copy(
            g["ebias_bc"][:, :], ebb[:, 0:NE]), incs=[(VS, 1)])
        ebias_tok = VS.now()
        p.pending["tensor"].append(ebias_tok)

        lnsc = dict(onesk=g["onesk"], onem=g["onem"], pm=pst[0], pv=pst[1],
                    pmb=pbc[0], prb=pbc[1], bc_free=[], stat_free=[])
        bank_free = {i: [] for i in range(4)}
        xr_free = {0: [], 1: []}
        sgsu_guard = []
        sho_guard = []
        comb_cols = 32

        for ch in range(NCH):
            buf = ch % 2
            gt0 = ch * Tc
            b = gt0 // L
            t0 = gt0 % L
            xr_v = xr[:, buf * HT * Tc:(buf + 1) * HT * Tc]
            p.em("sync", xr_free[buf], lambda e, b=b, t0=t0, xr_v=xr_v: e.dma_start(
                out=xr_v,
                in_=x_in[b, :, t0:t0 + Tc].rearrange("(h p) t -> p h t", p=128)),
                incs=[(SD, 16)])
            xr_tok = SD.now()
            lnsc.update(
                xb=xbq[:, 0:HT * Tc], xsq=xbq[:, HT * Tc:2 * HT * Tc],
                mrow=rows[:, 0:Tc], msq=rows[:, Tc:2 * Tc],
                vrow=rows[:, 2 * Tc:3 * Tc], lrow=rows[:, 3 * Tc:4 * Tc],
                rrow=rows[:, 4 * Tc:5 * Tc],
                mrow_b=rows_b[:, 0:Tc], rrow_b=rows_b[:, Tc:2 * Tc],
                dtile=xbq[:, HT * Tc:2 * HT * Tc],
                d32=d32, xln32_out=xln32, fp32_stats=True,
                xr_ready=[xr_tok],
            )
            xln_tok = emit_ln(p, c, xr_v, xlnb, Tc, lnsc)
            xr_free[buf] = [xln_tok]
            p.em("scalar", [xln_tok], lambda e, gt0=gt0: e.dma_start(
                out=dram["sp_xln"][:, :, gt0:gt0 + Tc].rearrange("h p t -> p h t"),
                in_=xlnb[:, :]), incs=[(CD, 16)])
            lnsc["out_free"] = [CD.now()]

            for tt in range(4):
                bank = tt % 4
                for k in range(HT):
                    w4 = []
                    if k == 0:
                        w4 += bank_free[bank]
                        if tt == 0:
                            w4 += [xln_tok, w_tok, wr_tok]
                    p.em("tensor", w4,
                         lambda e, tt=tt, k=k, bank=bank: e.matmul(
                             pool[bank][:, 0:NE],
                             xln32[:, k * Tc + tt * 128:k * Tc + tt * 128 + 128],
                             Wr32[:, k * NE:(k + 1) * NE],
                             start=(k == 0), stop=(k == HT - 1)),
                         incs=[(TS, 1)] if k == HT - 1 else [])
                rl_tok = TS.now()
                s0 = tt * 32
                rr = rsc[:, s0:s0 + 32]
                sm = rsc[:, 128 + s0:128 + s0 + 32]
                t8 = rsc[:, 256 + tt * 8:256 + tt * 8 + 8]
                nmx = rsc[:, 288 + tt:288 + tt + 1]
                ism = rsc[:, 296 + tt:296 + tt + 1]
                p.em("vector", [rl_tok],
                     lambda e, rr=rr, bank=bank: e.tensor_copy(rr, pool[bank][:, 0:NE]),
                     incs=[(VS, 1)])
                rr_tok = VS.now()
                bank_free[bank] = [rr_tok]
                p.em("vector", [rr_tok],
                     lambda e, rr=rr, nmx=nmx: e.reduce_max(
                         nmx, rr, axis=AX.X, negate=True), incs=[(VS, 1)])
                p.em("scalar", [VS.now()],
                     lambda e, rr=rr, sm=sm, nmx=nmx: e.activation(
                         sm, rr, AF.Exp, bias=nmx), incs=[(SS, 1)])
                p.em("vector", [SS.now()],
                     lambda e, sm=sm, ism=ism: e.reduce_sum(ism, sm, axis=AX.X),
                     incs=[(VS, 1)])
                p.em("vector", [VS.now()],
                     lambda e, ism=ism: e.reciprocal(ism, ism), incs=[(VS, 1)])
                p.em("vector", [VS.now()],
                     lambda e, sm=sm, ism=ism: e.tensor_scalar(
                         sm, sm, ism, None, ALU.mult), incs=[(VS, 1)])
                rw_tok = VS.now()
                p.em("vector", [rw_tok, ebias_tok],
                     lambda e, rr=rr, sm=sm: e.tensor_add(rr, sm, g["ebias_bc"][:, :]),
                     incs=[(VS, 1)])
                p.em("vector", [VS.now()],
                     lambda e, rr=rr, t8=t8: e.max(t8, rr), incs=[(VS, 1)])
                p.em("vector", [VS.now()],
                     lambda e, rr=rr, t8=t8: e.tensor_single_scalar(
                         rr, rr, t8[:, 5:6], ALU.is_ge), incs=[(VS, 1)])
                p.em("vector", [VS.now()],
                     lambda e, rr=rr, sm=sm: e.tensor_mul(sm, rr, sm), incs=[(VS, 1)])
                cm_tok = VS.now()
                p.em("vector", [cm_tok],
                     lambda e, sm=sm, ism=ism: e.reduce_sum(ism, sm, axis=AX.X),
                     incs=[(VS, 1)])
                p.em("vector", [VS.now()],
                     lambda e, ism=ism: e.tensor_scalar_max(ism, ism, 1e-12),
                     incs=[(VS, 1)])
                p.em("vector", [VS.now()],
                     lambda e, ism=ism: e.reciprocal(ism, ism), incs=[(VS, 1)])
                p.em("vector", [VS.now()],
                     lambda e, sm=sm, ism=ism: e.tensor_scalar(
                         sm, sm, ism, None, ALU.mult), incs=[(VS, 1)])
                comb_tok = VS.now()
                bank2 = (tt + 1) % 4
                p.em("tensor", [comb_tok] + bank_free[bank2],
                     lambda e, sm=sm, bank2=bank2: e.transpose(
                         pool[bank2][0:32, 0:128], sm, fg["identF"][:, :]),
                     incs=[(TS, 1)])
                tr_tok = TS.now()
                p.em("scalar", [tr_tok],
                     lambda e, gt0=gt0, tt=tt, bank2=bank2: e.copy(
                         fg["combT"][0:32, gt0 + tt * 128:gt0 + (tt + 1) * 128],
                         pool[bank2][0:32, 0:128]),
                     incs=[(SS, 1)])
                bank_free[bank2] = [SS.now()]

            ev_sg = {}
            for o in range(HT):
                bank = o % 4
                for k in range(HT):
                    w4 = []
                    if k == 0:
                        w4 += bank_free[bank]
                        if o == 0:
                            w4 += sgsu_guard
                    p.em("tensor", w4,
                         lambda e, o=o, k=k, bank=bank: e.matmul(
                             pool[bank][:],
                             Wg[:, (k * HT + o) * 128:(k * HT + o + 1) * 128],
                             xlnb[:, k * Tc:(k + 1) * Tc],
                             start=(k == 0), stop=(k == HT - 1)),
                         incs=[(TS, 1)] if k == HT - 1 else [])
                mm_tok = TS.now()
                p.em("scalar", [mm_tok],
                     lambda e, o=o, bank=bank: e.activation(
                         sg[:, o * Tc:(o + 1) * Tc], pool[bank][:], AF.Silu),
                     incs=[(SS, 1)])
                ev_sg[o] = SS.now()
                bank_free[bank] = [ev_sg[o]]
            sgsu_guard = []
            ev_su = {}
            for o in range(HT):
                bank = o % 4
                for k in range(HT):
                    w4 = bank_free[bank] if k == 0 else []
                    p.em("tensor", w4,
                         lambda e, o=o, k=k, bank=bank: e.matmul(
                             pool[bank][:],
                             Wu[:, (k * HT + o) * 128:(k * HT + o + 1) * 128],
                             xlnb[:, k * Tc:(k + 1) * Tc],
                             start=(k == 0), stop=(k == HT - 1)),
                         incs=[(TS, 1)] if k == HT - 1 else [])
                mm_tok = TS.now()
                p.em("scalar", [mm_tok],
                     lambda e, o=o, bank=bank: e.copy(
                         su[:, o * Tc:(o + 1) * Tc], pool[bank][:]),
                     incs=[(SS, 1)])
                ev_su[o] = SS.now()
                bank_free[bank] = [ev_su[o]]
            p.em("vector", [ev_sg[HT - 1], ev_su[HT - 1]],
                 lambda e: e.tensor_mul(sh[:, :], sg[:, :], su[:, :]), incs=[(VS, 1)])
            sh_tok = VS.now()
            sgsu_guard = [sh_tok]
            ev_sd = {}
            for o in range(HT):
                bank = o % 4
                for k in range(HT):
                    w4 = []
                    if k == 0:
                        w4 += bank_free[bank]
                        if o == 0:
                            w4 += [sh_tok] + sho_guard
                    p.em("tensor", w4,
                         lambda e, o=o, k=k, bank=bank: e.matmul(
                             pool[bank][:],
                             Wd[:, (k * HT + o) * 128:(k * HT + o + 1) * 128],
                             sh[:, k * Tc:(k + 1) * Tc],
                             start=(k == 0), stop=(k == HT - 1)),
                         incs=[(TS, 1)] if k == HT - 1 else [])
                mm_tok = TS.now()
                p.em("scalar", [mm_tok] + (sho_guard if o == 0 else []),
                     lambda e, o=o, bank=bank: e.copy(
                         sho[:, o * Tc:(o + 1) * Tc], pool[bank][:]),
                     incs=[(SS, 1)])
                ev_sd[o] = SS.now()
                bank_free[bank] = [ev_sd[o]]
            sho_guard = []
            p.em("scalar", [ev_sd[HT - 1]], lambda e, gt0=gt0: e.dma_start(
                out=dram["sp_shared"][:, :, gt0:gt0 + Tc].rearrange("h p t -> p h t"),
                in_=sho[:, :]), incs=[(CD, 16)])
            sho_guard = [CD.now()]


def _fhalf(p, nc, c, fg, x_in, x_out, final, xout, hh, tag):
    dram, g = c["dram"], c["g"]
    SD, GD, CD = c["SD"], c["GD"], c["CD"]
    TS, VS, SS, GS = c["TS"], c["VS"], c["SS"], c["GS"]
    Tc = 512
    NCHH = L // Tc
    gbase = hh * L
    b = hh
    with ExitStack() as st:
        EC = st.enter_context
        xlh = EC(nc.sbuf_tensor(tag + f"h{hh}_xln", [128, HT * L], BF16))
        acc = EC(nc.sbuf_tensor(tag + f"h{hh}_acc", [128, HT * L], F32))
        wgu = EC(nc.sbuf_tensor(tag + f"h{hh}_wgu", [128, 2 * HT * 2 * I_EXP], BF16))
        wdn = EC(nc.sbuf_tensor(tag + f"h{hh}_wdn", [128, 2 * 4 * H], BF16))
        sgt = EC(nc.sbuf_tensor(tag + f"h{hh}_sg", [128, 2 * 4 * Tc], BF16))
        hmt = EC(nc.sbuf_tensor(tag + f"h{hh}_hm", [128, 2 * 4 * Tc], BF16))
        xrf = EC(nc.sbuf_tensor(tag + f"h{hh}_xr", [128, HT * Tc], F32))
        pool = [EC(nc.psum_tensor(f"{tag}h{hh}_p{i}", [128, Tc], F32)) for i in range(4)]
        pacc = [EC(nc.psum_tensor(f"{tag}h{hh}_a{i}", [128, Tc], F32)) for i in range(2)]
        pcbc = EC(nc.psum_tensor(f"{tag}h{hh}_cb", [128, Tc], F32))

        p.em("sync", [], lambda e: e.dma_start(
            out=xlh[:, :],
            in_=dram["sp_xln"][:, :, gbase:gbase + L].rearrange("h p t -> p h t")),
            incs=[(SD, 16)])
        ld_tok = SD.now()

        wready = {}
        def load_expert(e_ix, slot, waits):
            p.em("gpsimd", waits, lambda e, e_ix=e_ix, slot=slot: e.dma_start(
                out=wgu[:, slot * HT * 2 * I_EXP:(slot + 1) * HT * 2 * I_EXP],
                in_=dram["gup_T"][e_ix].rearrange("k p f -> p k f")),
                incs=[(GD, 16)])
            p.em("gpsimd", [], lambda e, e_ix=e_ix, slot=slot: e.dma_start(
                out=wdn[:, slot * 4 * H:(slot + 1) * 4 * H],
                in_=dram["down_T"][e_ix].rearrange("k p f -> p k f")),
                incs=[(GD, 16)])
            wready[e_ix] = GD.now()

        wfree = {0: [], 1: []}
        load_expert(0, 0, [])
        p.em("gpsimd", [], lambda e: e.dma_start(
            out=acc[:, :],
            in_=dram["sp_shared"][:, :, gbase:gbase + L].rearrange("h p t -> p h t")),
            incs=[(GD, 16)])
        acc_init = GD.now()
        load_expert(1, 1, [])
        bank_free = {i: [] for i in range(4)}
        acc_bank_free = {i: [] for i in range(2)}
        sg_free = {0: [], 1: []}
        hm_free = {0: [], 1: []}
        NSTEP = NE * NCHH
        pcbc_free = {0: []}
        pend = None

        def _emit_down(state):
            e2, ch2, par2, hm2t = state
            t02 = ch2 * Tc
            slot2 = e2 % 2
            wd_v = wdn[:, slot2 * 4 * H:(slot2 + 1) * 4 * H]
            hm_v2 = hmt[:, par2 * 4 * Tc:(par2 + 1) * 4 * Tc]
            for oo in range(8):
                bk = oo % 2
                for k in range(4):
                    w4 = []
                    if k == 0:
                        w4 += acc_bank_free[bk]
                        if oo == 0:
                            w4 += [hm2t]
                    p.em("tensor", w4,
                         lambda e, oo=oo, k=k, bk=bk, wd_v=wd_v, hm_v2=hm_v2:
                         e.matmul(pacc[bk][:],
                                  wd_v[:, k * H + oo * 128:k * H + (oo + 1) * 128],
                                  hm_v2[:, k * Tc:(k + 1) * Tc],
                                  start=(k == 0), stop=(k == 3)),
                         incs=[(TS, 1)] if k == 3 else [])
                mm_tok = TS.now()
                a_sl = acc[:, oo * L + t02:oo * L + t02 + Tc]
                w5 = [mm_tok] + ([acc_init] if e2 == 0 else [])
                p.em("vector", w5,
                     lambda e, a_sl=a_sl, bk=bk: e.tensor_add(a_sl, a_sl, pacc[bk][:]),
                     incs=[(VS, 1)])
                acc_bank_free[bk] = [VS.now()]
            hm_free[par2] = [TS.now()]
            if ch2 == NCHH - 1:
                wfree[slot2] = [TS.now()]

        for s in range(NSTEP):
            e_ix, ch = divmod(s, NCHH)
            slot = e_ix % 2
            par = s % 2
            t0 = ch * Tc
            wg_v = wgu[:, slot * HT * 2 * I_EXP:(slot + 1) * HT * 2 * I_EXP]
            sg_v = sgt[:, par * 4 * Tc:(par + 1) * 4 * Tc]
            hm_v = hmt[:, par * 4 * Tc:(par + 1) * 4 * Tc]
            if ch == 1 and 1 <= e_ix and e_ix + 1 < NE:
                load_expert(e_ix + 1, 1 - slot, wfree[1 - slot])
            ev = {}
            for o in range(4):
                for k in range(HT):
                    w4 = []
                    if k == 0:
                        w4 += bank_free[o]
                        if o == 0 and ch == 0:
                            w4 += [wready[e_ix]]
                        if s == 0:
                            w4 += [ld_tok]
                    p.em("tensor", w4,
                         lambda e, o=o, k=k, wg_v=wg_v, t0=t0: e.matmul(
                             pool[o][:],
                             wg_v[:, k * 2 * I_EXP + o * 128:
                                  k * 2 * I_EXP + (o + 1) * 128],
                             xlh[:, k * L + t0:k * L + t0 + Tc],
                             start=(k == 0), stop=(k == HT - 1)),
                         incs=[(TS, 1)] if k == HT - 1 else [])
                ev[o] = TS.now()
            sg_toks = {}
            for o in range(4):
                p.em("scalar", [ev[o]] + (sg_free[par] if o == 0 else []),
                     lambda e, o=o, sg_v=sg_v: e.activation(
                         sg_v[:, o * Tc:(o + 1) * Tc], pool[o][:], AF.Silu),
                     incs=[(SS, 1)])
                sg_toks[o] = SS.now()
            evu = {}
            for o in range(4):
                for k in range(HT):
                    w4 = [sg_toks[o]] if k == 0 else []
                    p.em("tensor", w4,
                         lambda e, o=o, k=k, wg_v=wg_v, t0=t0: e.matmul(
                             pool[o][:],
                             wg_v[:, k * 2 * I_EXP + (o + 4) * 128:
                                  k * 2 * I_EXP + (o + 5) * 128],
                             xlh[:, k * L + t0:k * L + t0 + Tc],
                             start=(k == 0), stop=(k == HT - 1)),
                         incs=[(TS, 1)] if k == HT - 1 else [])
                evu[o] = TS.now()
            p.em("tensor", pcbc_free[0],
                 lambda e, e_ix=e_ix, t0=t0: e.matmul(
                     pcbc[:], fg["indE"][:, e_ix * 128:(e_ix + 1) * 128],
                     fg["combT"][0:32, gbase + t0:gbase + t0 + Tc],
                     start=True, stop=True), incs=[(TS, 1)])
            cbc_tok = TS.now()
            hm_toks = {}
            for o in range(4):
                p.em("vector", [evu[o]] + (hm_free[par] if o == 0 else []),
                     lambda e, o=o, sg_v=sg_v, hm_v=hm_v: e.tensor_mul(
                         hm_v[:, o * Tc:(o + 1) * Tc],
                         sg_v[:, o * Tc:(o + 1) * Tc], pool[o][:]),
                     incs=[(VS, 1)])
                hm_toks[o] = VS.now()
            sg_free[par] = [hm_toks[3]]
            for o in range(4):
                p.em("vector", [cbc_tok] if o == 0 else [],
                     lambda e, o=o, hm_v=hm_v: e.tensor_mul(
                         hm_v[:, o * Tc:(o + 1) * Tc],
                         hm_v[:, o * Tc:(o + 1) * Tc], pcbc[:]),
                     incs=[(VS, 1)])
            hm2_tok = VS.now()
            pcbc_free[0] = [hm2_tok]
            for o in range(4):
                bank_free[o] = [hm_toks[o]]
            bank_free[0] = [hm2_tok]
            if pend is not None:
                _emit_down(pend)
            pend = (e_ix, ch, par, hm2_tok)
        _emit_down(pend)

        if DEBUG_COMBT and tag == "f1" and hh == 0:
            p.em("sync", [VS.now()], lambda e: e.dma_start(
                out=dram["dbg_acc"][:, :], in_=acc[:, :]), incs=[(SD, 16)])

        if DEBUG_COMBT and tag == "f1" and hh == 0:
            p.em("sync", [VS.now()], lambda e: e.dma_start(
                out=dram["dbg_acc"][:, :], in_=acc[:, :]), incs=[(SD, 16)])
        xr_free2 = []
        for ch in range(NCHH):
            t0 = ch * Tc
            p.em("sync", xr_free2, lambda e, t0=t0: e.dma_start(
                out=xrf[:, :],
                in_=x_in[b, :, t0:t0 + Tc].rearrange("(h p) t -> p h t", p=128)),
                incs=[(SD, 16)])
            xr_tok = SD.now()
            add_toks = []
            for o in range(HT):
                p.em("vector", [xr_tok],
                     lambda e, o=o, t0=t0: e.tensor_add(
                         xrf[:, o * Tc:(o + 1) * Tc],
                         xrf[:, o * Tc:(o + 1) * Tc],
                         acc[:, o * L + t0:o * L + t0 + Tc]),
                     incs=[(VS, 1)])
                add_toks.append(VS.now())
            p.em("sync", [add_toks[-1]], lambda e, t0=t0: e.dma_start(
                out=x_out[b, :, t0:t0 + Tc].rearrange("(h p) t -> p h t", p=128),
                in_=xrf[:, :]), incs=[(SD, 16)])
            xr_free2 = [SD.now()]
            if final and xout is not None:
                p.em("sync", [], lambda e, t0=t0: e.dma_start(
                    out=xout[b, :, t0:t0 + Tc].rearrange("(h p) t -> p h t", p=128),
                    in_=xrf[:, :]), incs=[(SD, 16)])
                xr_free2.append(SD.now())


def phase_attn(p, nc, c, x_in, x_out, final, xout, tag):
    TS, VS, SS, GS = c["TS"], c["VS"], c["SS"], c["GS"]
    SD, GD, CD = c["SD"], c["GD"], c["CD"]
    for b in range(NB):
        with ExitStack() as ast_:
            EC = ast_.enter_context
            qb = EC(nc.sbuf_tensor(tag + f"{b}_q", [128, HT * L], BF16))
            kb = EC(nc.sbuf_tensor(tag + f"{b}_k", [128, HT * L], BF16))
            vT = EC(nc.sbuf_tensor(tag + f"{b}_vT", [128, (L // 128) * H], BF16))
            _attn_qkv(p, nc, c, x_in, b, qb, kb, vT, tag)
            p.join([TS.now(), VS.now(), SS.now(), GS.now(),
                    SD.now(), GD.now(), CD.now()])
            _attn_core(p, nc, c, x_in, x_out, final, xout, b, qb, kb, vT, tag)
            p.join([TS.now(), VS.now(), SS.now(), GS.now(),
                    SD.now(), GD.now(), CD.now()])


def _attn_qkv(p, nc, c, x_in, b, qb, kb, vT, tag):
    dram, g = c["dram"], c["g"]
    SD, GD, CD = c["SD"], c["GD"], c["CD"]
    TS, VS, SS, GS = c["TS"], c["VS"], c["SS"], c["GS"]
    Tc = 512
    NCH = L // Tc
    with ExitStack() as st:
        EC = st.enter_context
        Wq = EC(nc.sbuf_tensor(tag + f"{b}q_Wq", [128, HT * H], BF16))
        Wk = EC(nc.sbuf_tensor(tag + f"{b}q_Wk", [128, HT * H], BF16))
        Wv = EC(nc.sbuf_tensor(tag + f"{b}q_Wv", [128, HT * H], BF16))
        xr = EC(nc.sbuf_tensor(tag + f"{b}q_xr", [128, HT * Tc], F32))
        xlnb = EC(nc.sbuf_tensor(tag + f"{b}q_xln", [128, HT * Tc], BF16))
        xbq = EC(nc.sbuf_tensor(tag + f"{b}q_xbq", [128, 2 * HT * Tc], BF16))
        rows = EC(nc.sbuf_tensor(tag + f"{b}q_rows", [1, 5 * Tc], F32))
        rows_b = EC(nc.sbuf_tensor(tag + f"{b}q_rows_b", [1, 2 * Tc], BF16))
        pool = [EC(nc.psum_tensor(f"{tag}{b}q_p{i}", [128, Tc], F32)) for i in range(4)]
        pst = [EC(nc.psum_tensor(f"{tag}{b}q_s{i}", [1, Tc], F32)) for i in range(2)]
        pbc = [EC(nc.psum_tensor(f"{tag}{b}q_b{i}", [128, Tc], F32)) for i in range(2)]

        for nm, dst in (("Wq_T", Wq), ("Wk_T", Wk), ("Wv_T", Wv)):
            p.em("gpsimd", [], lambda e, nm=nm, dst=dst: e.dma_start(
                out=dst[:, :], in_=dram[nm].rearrange("k p f -> p k f")),
                incs=[(GD, 16)])
        w_tok = GD.now()

        lnsc = dict(onesk=g["onesk"], onem=g["onem"], pm=pst[0], pv=pst[1],
                    pmb=pbc[0], prb=pbc[1], bc_free=[], stat_free=[])
        bank_free = {i: [] for i in range(4)}
        xr_free = {0: [], 1: []}

        for ch in range(NCH):
            buf = 0
            t0 = ch * Tc
            xr_v = xr[:, :]
            p.em("sync", xr_free[buf], lambda e, t0=t0, xr_v=xr_v: e.dma_start(
                out=xr_v,
                in_=x_in[b, :, t0:t0 + Tc].rearrange("(h p) t -> p h t", p=128)),
                incs=[(SD, 16)])
            xr_tok = SD.now()
            lnsc.update(
                xb=xbq[:, 0:HT * Tc], xsq=xbq[:, HT * Tc:2 * HT * Tc],
                mrow=rows[:, 0:Tc], msq=rows[:, Tc:2 * Tc],
                vrow=rows[:, 2 * Tc:3 * Tc], lrow=rows[:, 3 * Tc:4 * Tc],
                rrow=rows[:, 4 * Tc:5 * Tc],
                mrow_b=rows_b[:, 0:Tc], rrow_b=rows_b[:, Tc:2 * Tc],
                dtile=xbq[:, HT * Tc:2 * HT * Tc],
                xr_ready=[xr_tok],
            )
            xln_tok = emit_ln(p, c, xr_v, xlnb, Tc, lnsc)
            xr_free[buf] = [xln_tok]

            for which, W, dst, scale in (("q", Wq, qb, 1.0 / 32.0),
                                         ("k", Wk, kb, None)):
                for o in range(HT):
                    bank = o % 4
                    for k in range(HT):
                        w4 = []
                        if k == 0:
                            w4 += bank_free[bank]
                            if o == 0 and which == "q":
                                w4 += [xln_tok, w_tok]
                        p.em("tensor", w4,
                             lambda e, o=o, k=k, bank=bank, W=W: e.matmul(
                                 pool[bank][:],
                                 W[:, (k * HT + o) * 128:(k * HT + o + 1) * 128],
                                 xlnb[:, k * Tc:(k + 1) * Tc],
                                 start=(k == 0), stop=(k == HT - 1)),
                             incs=[(TS, 1)] if k == HT - 1 else [])
                    mm_tok = TS.now()
                    if scale is not None:
                        p.em("scalar", [mm_tok],
                             lambda e, o=o, bank=bank, dst=dst, t0=t0, scale=scale:
                             e.activation(
                                 dst[:, o * L + t0:o * L + t0 + Tc], pool[bank][:],
                                 AF.Copy, scale=scale),
                             incs=[(SS, 1)])
                    else:
                        p.em("scalar", [mm_tok],
                             lambda e, o=o, bank=bank, dst=dst, t0=t0: e.copy(
                                 dst[:, o * L + t0:o * L + t0 + Tc], pool[bank][:]),
                             incs=[(SS, 1)])
                    bank_free[bank] = [SS.now()]
            for tt in range(4):
                kt = ch * 4 + tt
                for nn in range(2):
                    bank = (tt * 2 + nn) % 4
                    for k in range(HT):
                        w4 = bank_free[bank] if k == 0 else []
                        p.em("tensor", w4,
                             lambda e, tt=tt, nn=nn, k=k, bank=bank: e.matmul(
                                 pool[bank][:],
                                 xlnb[:, k * Tc + tt * 128:k * Tc + tt * 128 + 128],
                                 Wv[:, k * H + nn * 512:k * H + (nn + 1) * 512],
                                 start=(k == 0), stop=(k == HT - 1)),
                             incs=[(TS, 1)] if k == HT - 1 else [])
                    mm_tok = TS.now()
                    p.em("scalar", [mm_tok],
                         lambda e, kt=kt, nn=nn, bank=bank: e.copy(
                             vT[:, kt * H + nn * 512:kt * H + (nn + 1) * 512],
                             pool[bank][:]),
                         incs=[(SS, 1)])
                    bank_free[bank] = [SS.now()]


def _attn_core(p, nc, c, x_in, x_out, final, xout, b, qb, kb, vT, tag):
    dram, g = c["dram"], c["g"]
    SD, GD, CD = c["SD"], c["GD"], c["CD"]
    TS, VS, SS, GS = c["TS"], c["VS"], c["SS"], c["GS"]
    NKT = L // 128
    with ExitStack() as st:
        EC = st.enter_context
        Wo = EC(nc.sbuf_tensor(tag + f"{b}a_Wo", [128, HT * H], BF16))
        scr = EC(nc.sbuf_tensor(tag + f"{b}a_scr", [128, 4 * L], F32))
        abuf = EC(nc.sbuf_tensor(tag + f"{b}a_ab", [128, 4 * L], BF16))
        atT = EC(nc.sbuf_tensor(tag + f"{b}a_atT", [128, NKT * 512], BF16))
        ctx = EC(nc.sbuf_tensor(tag + f"{b}a_ctx", [128, HT * 512], BF16))
        xrs = EC(nc.sbuf_tensor(tag + f"{b}a_xr", [128, HT * 512], F32))
        srow = EC(nc.sbuf_tensor(tag + f"{b}a_srow", [128, 16], F32))
        pool = [EC(nc.psum_tensor(f"{tag}{b}a_p{i}", [128, 512], F32)) for i in range(2)]
        ptr = [EC(nc.psum_tensor(f"{tag}{b}a_tr{i}", [128, 128], BF16)) for i in range(2)]
        pctx = [EC(nc.psum_tensor(f"{tag}{b}a_c{i}", [128, 512], F32)) for i in range(4)]

        p.em("gpsimd", [], lambda e: e.dma_start(
            out=Wo[:, :], in_=dram["W_outB_T"].rearrange("k p f -> p k f")),
            incs=[(GD, 16)])
        w_tok = GD.now()

        bank_free = {i: [] for i in range(2)}
        ptr_free = {0: [], 1: []}
        pctx_free = {i: [] for i in range(4)}
        scr_free = []
        abuf_free = []
        atT_free = []
        ctx_free = []
        xr_free = []

        for gg in range(4):
            q0 = gg * 512
            sm_toks = []
            for q4 in range(4):
                qt0 = q0 + q4 * 128
                ev_toks = []
                for nn in range(4):
                    bank = (q4 * 4 + nn) % 2
                    for k in range(HT):
                        w4 = []
                        if k == 0:
                            w4 += bank_free[bank]
                            if q4 == 0 and nn == 0 and gg == 0:
                                w4 += [w_tok] + scr_free
                            if q4 == 0 and nn == 0:
                                w4 += scr_free
                        p.em("tensor", w4,
                             lambda e, q4=q4, nn=nn, k=k, bank=bank, qt0=qt0: e.matmul(
                                 pool[bank][:],
                                 qb[:, k * L + qt0:k * L + qt0 + 128],
                                 kb[:, k * L + nn * 512:k * L + (nn + 1) * 512],
                                 start=(k == 0), stop=(k == HT - 1)),
                             incs=[(TS, 1)] if k == HT - 1 else [])
                    mm_tok = TS.now()
                    p.em("scalar", [mm_tok],
                         lambda e, q4=q4, nn=nn, bank=bank: e.copy(
                             scr[:, q4 * L + nn * 512:q4 * L + (nn + 1) * 512],
                             pool[bank][:]),
                         incs=[(SS, 1)])
                    ev_toks.append(SS.now())
                    bank_free[bank] = [ev_toks[-1]]
                nmx = srow[:, q4 * 4:q4 * 4 + 1]
                ism = srow[:, q4 * 4 + 1:q4 * 4 + 2]
                p.em("vector", [ev_toks[-1]],
                     lambda e, q4=q4, nmx=nmx: e.reduce_max(
                         nmx, scr[:, q4 * L:(q4 + 1) * L], axis=AX.X, negate=True),
                     incs=[(VS, 1)])
                p.em("scalar", [VS.now()],
                     lambda e, q4=q4, nmx=nmx: e.activation(
                         scr[:, q4 * L:(q4 + 1) * L], scr[:, q4 * L:(q4 + 1) * L],
                         AF.Exp, bias=nmx),
                     incs=[(SS, 1)])
                p.em("vector", [SS.now()],
                     lambda e, q4=q4, ism=ism: e.reduce_sum(
                         ism, scr[:, q4 * L:(q4 + 1) * L], axis=AX.X), incs=[(VS, 1)])
                p.em("vector", [VS.now()],
                     lambda e, ism=ism: e.reciprocal(ism, ism), incs=[(VS, 1)])
                p.em("vector", [VS.now()] + (abuf_free if q4 == 0 else []),
                     lambda e, q4=q4, ism=ism: e.tensor_scalar(
                         abuf[:, q4 * L:(q4 + 1) * L], scr[:, q4 * L:(q4 + 1) * L],
                         ism, None, ALU.mult),
                     incs=[(VS, 1)])
                sm_toks.append(VS.now())
            abuf_free = []
            scr_free = [sm_toks[-1]]
            tr_toks = []
            tix = 0
            for q4 in range(4):
                for j in range(NKT):
                    pb = tix % 2
                    tix += 1
                    w4 = [sm_toks[q4]] if j == 0 else []
                    w4 += ptr_free[pb]
                    if q4 == 0 and j == 0:
                        w4 += atT_free
                    p.em("tensor", w4,
                         lambda e, q4=q4, j=j, pb=pb: e.transpose(
                             ptr[pb][:],
                             abuf[:, q4 * L + j * 128:q4 * L + (j + 1) * 128],
                             g["ident"][:, :]),
                         incs=[(TS, 1)])
                    t_tok = TS.now()
                    p.em("scalar", [t_tok],
                         lambda e, q4=q4, j=j, pb=pb: e.copy(
                             atT[:, j * 512 + q4 * 128:j * 512 + (q4 + 1) * 128],
                             ptr[pb][:]),
                         incs=[(SS, 1)])
                    ptr_free[pb] = [SS.now()]
            atT_ready = SS.now()
            abuf_free = [TS.now()]
            ctx_toks = []
            for half in range(2):
                for o in range(4):
                    oo = half * 4 + o
                    for j in range(NKT):
                        w4 = []
                        if j == 0:
                            w4 += pctx_free[o]
                            if half == 0 and o == 0:
                                w4 += [atT_ready] + ctx_free
                        p.em("tensor", w4,
                             lambda e, oo=oo, j=j, o=o: e.matmul(
                                 pctx[o][:],
                                 vT[:, j * H + oo * 128:j * H + (oo + 1) * 128],
                                 atT[:, j * 512:(j + 1) * 512],
                                 start=(j == 0), stop=(j == NKT - 1)),
                             incs=[(TS, 1)] if j == NKT - 1 else [])
                    mm_tok = TS.now()
                    p.em("scalar", [mm_tok],
                         lambda e, oo=oo, o=o: e.copy(
                             ctx[:, oo * 512:(oo + 1) * 512], pctx[o][:]),
                         incs=[(SS, 1)])
                    pctx_free[o] = [SS.now()]
                    ctx_toks.append(SS.now())
            atT_free = [TS.now()]
            ctx_ready = ctx_toks[-1]
            p.em("sync", xr_free, lambda e, q0=q0: e.dma_start(
                out=xrs[:, :],
                in_=x_in[b, :, q0:q0 + 512].rearrange("(h p) t -> p h t", p=128)),
                incs=[(SD, 16)])
            xr_tok = SD.now()
            add_toks = []
            for o in range(HT):
                bank = o % 2
                for k in range(HT):
                    w4 = []
                    if k == 0:
                        w4 += bank_free[bank]
                        if o == 0:
                            w4 += [ctx_ready, w_tok]
                    p.em("tensor", w4,
                         lambda e, o=o, k=k, bank=bank: e.matmul(
                             pool[bank][:],
                             Wo[:, (k * HT + o) * 128:(k * HT + o + 1) * 128],
                             ctx[:, k * 512:(k + 1) * 512],
                             start=(k == 0), stop=(k == HT - 1)),
                         incs=[(TS, 1)] if k == HT - 1 else [])
                mo_tok = TS.now()
                p.em("vector", [mo_tok, xr_tok],
                     lambda e, o=o, bank=bank: e.tensor_add(
                         xrs[:, o * 512:(o + 1) * 512],
                         xrs[:, o * 512:(o + 1) * 512], pool[bank][:]),
                     incs=[(VS, 1)])
                add_toks.append(VS.now())
                bank_free[bank] = [add_toks[-1]]
            ctx_free = [TS.now()]
            p.em("sync", [add_toks[-1]], lambda e, q0=q0: e.dma_start(
                out=x_out[b, :, q0:q0 + 512].rearrange("(h p) t -> p h t", p=128),
                in_=xrs[:, :]), incs=[(SD, 16)])
            xr_free = [SD.now()]
            if final and xout is not None:
                p.em("sync", [], lambda e, q0=q0: e.dma_start(
                    out=xout[b, :, q0:q0 + 512].rearrange("(h p) t -> p h t", p=128),
                    in_=xrs[:, :]), incs=[(SD, 16)])
                xr_free.append(SD.now())



def _pack_pc(v):
    return np.ascontiguousarray(v.reshape(CT, 128).T)


def _prep_inputs(inputs):
    f = {}
    x = np.ascontiguousarray(inputs["L0_attn_a_x"])
    W_in = inputs["L0_attn_a_W_in"]
    f["W_in_T"] = np.ascontiguousarray(W_in.T).reshape(HT, 128, 2 * INNER)
    cw = inputs["L0_attn_a_conv_w"][:, 0, :]
    f["cw_p"] = np.ascontiguousarray(
        cw.reshape(CT, 128, 4).transpose(1, 0, 2).reshape(128, CT * 4))
    f["cb_p"] = _pack_pc(inputs["L0_attn_a_conv_b"])
    W_x = inputs["L0_attn_a_W_x"]
    f["W_BC_T"] = np.ascontiguousarray(W_x[:32].T).reshape(CT, 128, 32)
    f["W_dtin_T"] = np.ascontiguousarray(W_x[32:].T).reshape(CT, 128, INNER)
    f["W_dt_T"] = np.ascontiguousarray(inputs["L0_attn_a_W_dt"].T).reshape(CT, 128, INNER)
    f["bdt_p"] = _pack_pc(inputs["L0_attn_a_b_dt"])
    f["dv_p"] = _pack_pc(inputs["L0_attn_a_D"])
    f["W_outA_T"] = np.ascontiguousarray(inputs["L0_attn_a_W_out"].T).reshape(CT, 128, H)
    W_qkv = inputs["L0_attn_b_W_qkv"]
    f["Wq_T"] = np.ascontiguousarray(W_qkv[0::3].T).reshape(HT, 128, H)
    f["Wk_T"] = np.ascontiguousarray(W_qkv[1::3].T).reshape(HT, 128, H)
    f["Wv_T"] = np.ascontiguousarray(W_qkv[2::3].T).reshape(HT, 128, H)
    f["W_outB_T"] = np.ascontiguousarray(inputs["L0_attn_b_W_out"].T).reshape(HT, 128, H)
    f["Wr_T"] = np.ascontiguousarray(
        inputs["L0_ffn_router_weight"].T).reshape(HT, 128, NE)
    f["e_bias"] = np.ascontiguousarray(inputs["L0_ffn_e_score_bias"]).reshape(1, NE)
    f["Wg_T"] = np.ascontiguousarray(
        inputs["L0_ffn_shared_gate_w"].T).reshape(HT, 128, IS)
    f["Wu_T"] = np.ascontiguousarray(
        inputs["L0_ffn_shared_up_w"].T).reshape(HT, 128, IS)
    f["Wd_T"] = np.ascontiguousarray(
        inputs["L0_ffn_shared_down_w"].T).reshape(IS // 128, 128, H)
    f["gup_T"] = np.ascontiguousarray(
        inputs["L0_ffn_gate_up_proj"].transpose(0, 2, 1)).reshape(NE, HT, 128, 2 * I_EXP)
    f["down_T"] = np.ascontiguousarray(
        inputs["L0_ffn_down_proj"].transpose(0, 2, 1)).reshape(NE, 4, 128, H)

    shards = []
    for cid in range(8):
        d = dict(f)
        xs = x[cid * NB:(cid + 1) * NB]
        d["x0"] = np.ascontiguousarray(xs.transpose(0, 2, 1))
        shards.append(d)
    return shards


def kernel(**inputs):
    shards = _prep_inputs(inputs)
    nc = build_program(upto="all")
    res = run_bass_kernel_spmd(nc, shards, core_ids=list(range(8)))
    outs = []
    for cid in range(8):
        xo = res.results[cid]["xout"]
        outs.append(np.ascontiguousarray(np.asarray(xo).transpose(0, 2, 1)))
    return np.concatenate(outs, axis=0).astype(np.float32)


# version 14
# speedup vs baseline: 1.1774x; 1.0157x over previous
import numpy as np
import ml_dtypes
from contextlib import ExitStack

import concourse.bass as bass
import concourse.mybir as mybir
from concourse.bass_utils import run_bass_kernel_spmd

F32 = mybir.dt.float32
BF16 = mybir.dt.bfloat16
AF = mybir.ActivationFunctionType
ALU = mybir.AluOpType
AX = mybir.AxisListType

H = 1024
INNER = 2048
NB = 2
L = 2048
T = NB * L
NE = 32
I_EXP = 512
IS = 1024
EPS = 1e-5
HT = H // 128
CT = INNER // 128


class Ctr:
    def __init__(self, handle):
        self.h = handle
        self.n = 0

    def now(self):
        return (self.h, self.n)


class Prog:
    ENGINES = ("sync", "gpsimd", "vector", "scalar", "tensor")

    def __init__(self, nc):
        self.nc = nc
        self.q = {e: [] for e in self.ENGINES}
        self.pending = {e: [] for e in self.ENGINES}

    def em(self, eng, waits, f, incs=()):
        w = list(self.pending[eng]) + [x for x in waits if x]
        self.pending[eng] = []
        bumped = []
        for c, k in incs:
            c.n += k
            bumped.append((c.h, k))
        self.q[eng].append((tuple(w), f, tuple(bumped)))

    def join(self, tokens):
        for e in self.ENGINES:
            self.pending[e].extend(tokens)

    def run(self, block):
        hw = {e: {} for e in self.ENGINES}

        def make(eng_name):
            items = self.q[eng_name]
            marks = hw[eng_name]

            def body(engine):
                for waits, f, incs in items:
                    for (h, v) in waits:
                        if marks.get(h, -1) < v:
                            engine.wait_ge(h, v)
                            marks[h] = v
                    ins = f(engine)
                    for (h, k) in incs:
                        ins = ins.then_inc(h, k)
            return body

        block.sync(make("sync"))
        block.gpsimd(make("gpsimd"))
        block.vector(make("vector"))
        block.scalar(make("scalar"))
        block.tensor(make("tensor"))



def emit_ln(p, c, xr, x_ln_out, Tc, sc):
    SS, VS, TS = c["SS"], c["VS"], c["TS"]
    g = c["g"]
    n = HT * Tc
    fp32s = sc.get("fp32_stats", False)
    if fp32s:
        xb_tok = None
        p.em("scalar", sc["xr_ready"] + sc.get("xsq_free", []),
             lambda e: e.activation(sc["d32"][:, 0:n], xr[:, 0:n], AF.Square),
             incs=[(SS, 1)])
        xsq_tok = SS.now()
        for h in range(HT):
            p.em("tensor", (sc["xr_ready"] + sc["stat_free"]) if h == 0 else [],
                 lambda e, h=h: e.matmul(sc["pm"][:, 0:Tc], g["onem_f"][:, 0:1],
                                         xr[:, h * Tc:(h + 1) * Tc],
                                         start=(h == 0), stop=(h == HT - 1)),
                 incs=[(TS, 1)] if h == HT - 1 else [])
        pm_tok = TS.now()
        for h in range(HT):
            p.em("tensor", [xsq_tok] if h == 0 else [],
                 lambda e, h=h: e.matmul(sc["pv"][:, 0:Tc], g["onem_f"][:, 0:1],
                                         sc["d32"][:, h * Tc:(h + 1) * Tc],
                                         start=(h == 0), stop=(h == HT - 1)),
                 incs=[(TS, 1)] if h == HT - 1 else [])
        pv_tok = TS.now()
    else:
        p.em("scalar", sc["xr_ready"] + sc.get("xb_free", []),
             lambda e: e.copy(sc["xb"][:, 0:n], xr[:, 0:n]), incs=[(SS, 1)])
        xb_tok = SS.now()
        p.em("scalar", sc.get("xsq_free", []),
             lambda e: e.activation(sc["xsq"][:, 0:n], xr[:, 0:n], AF.Square),
             incs=[(SS, 1)])
        xsq_tok = SS.now()
        for h in range(HT):
            p.em("tensor", ([xb_tok] + sc["stat_free"]) if h == 0 else [],
                 lambda e, h=h: e.matmul(sc["pm"][:, 0:Tc], g["onem"][:, 0:1],
                                         sc["xb"][:, h * Tc:(h + 1) * Tc],
                                         start=(h == 0), stop=(h == HT - 1)),
                 incs=[(TS, 1)] if h == HT - 1 else [])
        pm_tok = TS.now()
        for h in range(HT):
            p.em("tensor", [xsq_tok] if h == 0 else [],
                 lambda e, h=h: e.matmul(sc["pv"][:, 0:Tc], g["onem"][:, 0:1],
                                         sc["xsq"][:, h * Tc:(h + 1) * Tc],
                                         start=(h == 0), stop=(h == HT - 1)),
                 incs=[(TS, 1)] if h == HT - 1 else [])
        pv_tok = TS.now()
    p.em("vector", [pm_tok], lambda e: e.tensor_copy(sc["mrow"][:, 0:Tc], sc["pm"][:, 0:Tc]),
         incs=[(VS, 1)])
    m_tok = VS.now()
    p.em("vector", [m_tok], lambda e: e.tensor_mul(
        sc["msq"][:, 0:Tc], sc["mrow"][:, 0:Tc], sc["mrow"][:, 0:Tc]), incs=[(VS, 1)])
    p.em("vector", [VS.now(), pv_tok], lambda e: e.tensor_sub(
        sc["vrow"][:, 0:Tc], sc["pv"][:, 0:Tc], sc["msq"][:, 0:Tc]), incs=[(VS, 1)])
    v_tok = VS.now()
    p.em("scalar", [v_tok], lambda e: e.activation(
        sc["lrow"][:, 0:Tc], sc["vrow"][:, 0:Tc], AF.Ln, bias=c["g"]["cbias"][0:1, 0:1]), incs=[(SS, 1)])
    p.em("scalar", [SS.now()], lambda e: e.activation(
        sc["rrow"][:, 0:Tc], sc["lrow"][:, 0:Tc], AF.Exp, scale=-0.5), incs=[(SS, 1)])
    r_tok = SS.now()
    sc["stat_free"] = [m_tok, v_tok]
    if fp32s:
        p.em("tensor", [m_tok] + sc["bc_free"], lambda e: e.matmul(
            sc["pmb"][:, 0:Tc], g["onesk_f"][:, 0:128], sc["mrow"][:, 0:Tc],
            start=True, stop=True), incs=[(TS, 1)])
        p.em("tensor", [r_tok], lambda e: e.matmul(
            sc["prb"][:, 0:Tc], g["onesk_f"][:, 0:128], sc["rrow"][:, 0:Tc],
            start=True, stop=True), incs=[(TS, 1)])
        bc_tok = TS.now()
    else:
        p.em("vector", [m_tok], lambda e: e.tensor_copy(
            sc["mrow_b"][:, 0:Tc], sc["mrow"][:, 0:Tc]), incs=[(VS, 1)])
        mb_tok = VS.now()
        p.em("scalar", [r_tok], lambda e: e.copy(
            sc["rrow_b"][:, 0:Tc], sc["rrow"][:, 0:Tc]), incs=[(SS, 1)])
        rb_tok = SS.now()
        p.em("tensor", [mb_tok] + sc["bc_free"], lambda e: e.matmul(
            sc["pmb"][:, 0:Tc], g["onesk"][:, 0:128], sc["mrow_b"][:, 0:Tc],
            start=True, stop=True), incs=[(TS, 1)])
        p.em("tensor", [rb_tok], lambda e: e.matmul(
            sc["prb"][:, 0:Tc], g["onesk"][:, 0:128], sc["rrow_b"][:, 0:Tc],
            start=True, stop=True), incs=[(TS, 1)])
        bc_tok = TS.now()
    x32 = sc.get("xln32_out")
    for h in range(HT):
        dt_v = sc["d32"] if x32 is not None else sc["dtile"]
        p.em("vector", [bc_tok] if h == 0 else [],
             lambda e, h=h, dt_v=dt_v: e.tensor_sub(dt_v[:, h * Tc:(h + 1) * Tc],
                                                    xr[:, h * Tc:(h + 1) * Tc],
                                                    sc["pmb"][:, 0:Tc]),
             incs=[(VS, 1)])
        p.em("vector", [VS.now()] + (sc.get("out_free", []) if h == 0 else []),
             lambda e, h=h, dt_v=dt_v: e.tensor_mul(x_ln_out[:, h * Tc:(h + 1) * Tc],
                                                    dt_v[:, h * Tc:(h + 1) * Tc],
                                                    sc["prb"][:, 0:Tc]),
             incs=[(VS, 1)])
        if x32 is not None:
            p.em("vector", [],
                 lambda e, h=h, dt_v=dt_v: e.tensor_mul(x32[:, h * Tc:(h + 1) * Tc],
                                                        dt_v[:, h * Tc:(h + 1) * Tc],
                                                        sc["prb"][:, 0:Tc]),
                 incs=[(VS, 1)])
    done = VS.now()
    sc["out_free"] = []
    sc["bc_free"] = [done]
    sc["xb_free"] = [pm_tok]
    sc["xsq_free"] = [pv_tok, done]
    return done



DEBUG_COMBT = False


def build_program(upto="all"):
    nc = bass.Bass("TRN2", target_bir_lowering=False)
    dram = {}

    def din(name, shape, dtype=F32):
        dram[name] = nc.dram_tensor(name, shape, dtype, kind="ExternalInput").ap()
        return dram[name]

    def dint(name, shape, dtype):
        dram[name] = nc.dram_tensor(name, shape, dtype, kind="Internal").ap()
        return dram[name]

    x0 = din("x0", [NB, H, L])
    din("W_in_T", [HT, 128, 2 * INNER])
    din("cw_p", [128, CT * 4])
    din("cb_p", [128, CT])
    din("W_BC_T", [CT, 128, 32])
    din("W_dtin_T", [CT, 128, INNER])
    din("W_dt_T", [CT, 128, INNER])
    din("bdt_p", [128, CT])
    din("dv_p", [128, CT])
    din("W_outA_T", [CT, 128, H])
    din("Wq_T", [HT, 128, H])
    din("Wk_T", [HT, 128, H])
    din("Wv_T", [HT, 128, H])
    din("W_outB_T", [HT, 128, H])
    din("Wr_T", [HT, 128, NE])
    din("e_bias", [1, NE])
    din("Wg_T", [HT, 128, IS])
    din("Wu_T", [HT, 128, IS])
    din("Wd_T", [IS // 128, 128, H])
    din("gup_T", [NE, HT, 128, 2 * I_EXP])
    din("down_T", [NE, 4, 128, H])

    xout = nc.dram_tensor("xout", [NB, H, L], F32, kind="ExternalOutput").ap()
    if DEBUG_COMBT:
        dram["dbg_combT"] = nc.dram_tensor("dbg_combT", [32, T], BF16,
                                           kind="ExternalOutput").ap()
        dram["dbg_xln"] = nc.dram_tensor("dbg_xln", [HT, 128, T], BF16,
                                         kind="ExternalOutput").ap()
        dram["dbg_sh"] = nc.dram_tensor("dbg_sh", [HT, 128, T], BF16,
                                        kind="ExternalOutput").ap()
        dram["dbg_acc"] = nc.dram_tensor("dbg_acc", [128, HT * L], F32,
                                         kind="ExternalOutput").ap()
        dram["dbg_hm"] = nc.dram_tensor("dbg_hm", [128, 4 * 512], BF16,
                                        kind="ExternalOutput").ap()
        dram["dbg_sg"] = nc.dram_tensor("dbg_sg", [128, 4 * 512], BF16,
                                        kind="ExternalOutput").ap()
    xa = dint("xa", [NB, H, L], F32)
    xb_h = dint("xb", [NB, H, L], F32)
    dint("sp_xconv", [NB, CT, 128, L], BF16)
    dint("sp_z", [NB, CT, 128, L], BF16)
    dint("sp_dtin", [NB, CT, 128, L], BF16)
    dint("sp_xln", [HT, 128, T], BF16)
    dint("sp_shared", [HT, 128, T], BF16)

    with ExitStack() as top:
        EC = top.enter_context
        sems = {}
        for nm in ("SD", "GD", "CD", "TS", "VS", "SS", "GS"):
            sems[nm] = Ctr(EC(nc.semaphore(nm)))
        p = Prog(nc)
        c = dict(p=p, nc=nc, dram=dram, sems=sems,
                 SD=sems["SD"], GD=sems["GD"], CD=sems["CD"], TS=sems["TS"],
                 VS=sems["VS"], SS=sems["SS"], GS=sems["GS"])

        g = {}
        g["onesk"] = EC(nc.sbuf_tensor("onesk", [1, 128], BF16))
        g["onem"] = EC(nc.sbuf_tensor("onem", [128, 1], BF16))
        g["selBC"] = EC(nc.sbuf_tensor("selBC", [32, 2], BF16))
        g["sel2"] = EC(nc.sbuf_tensor("sel2", [2, 256], BF16))
        g["ident"] = EC(nc.sbuf_tensor("ident", [128, 128], BF16))
        g["cw"] = EC(nc.sbuf_tensor("cw", [128, CT * 4], F32))
        g["cb"] = EC(nc.sbuf_tensor("cb", [128, CT], F32))
        g["bdt"] = EC(nc.sbuf_tensor("bdt", [128, CT], F32))
        g["dvec"] = EC(nc.sbuf_tensor("dvec", [128, CT], F32))
        g["ebias_bc"] = EC(nc.sbuf_tensor("ebias_bc", [128, NE], F32))
        g["onesk_f"] = EC(nc.sbuf_tensor("onesk_f", [1, 128], F32))
        g["cbias"] = EC(nc.sbuf_tensor("cbias", [128, 2], F32))
        g["onem_f"] = EC(nc.sbuf_tensor("onem_f", [128, 1], F32))
        g["ebias_row"] = EC(nc.sbuf_tensor("ebias_row", [1, NE], F32))
        c["g"] = g

        build_constants(p, c)

        phase_mamba(p, nc, c, x_in=x0, x_out=xa,
                    final=(upto == "mamba"), xout=xout)

        if upto in ("ffn1", "attn", "all"):
            phase_ffn(p, nc, c, x_in=xa, x_out=xb_h,
                      final=(upto == "ffn1"), xout=xout, tag="f1")
        if upto in ("attn", "all"):
            phase_attn(p, nc, c, x_in=xb_h, x_out=xa,
                       final=(upto == "attn"), xout=xout, tag="at")
        if upto == "all":
            phase_ffn(p, nc, c, x_in=xa, x_out=xout,
                      final=False, xout=None, tag="f2")

        SD, GD, CD = sems["SD"], sems["GD"], sems["CD"]
        p.em("sync", [SD.now(), GD.now(), CD.now()],
             lambda e: e.wait_ge(SD.h, SD.n))

        block = EC(nc.Block())
        p.run(block)
    return nc


def build_constants(p, c):
    nc, g, dram = c["nc"], c["g"], c["dram"]
    GS, VS, SD, GD = c["GS"], c["VS"], c["SD"], c["GD"]
    with ExitStack() as st:
        iosc = st.enter_context(nc.sbuf_tensor("iosc", [128, 512], F32))
        p.em("gpsimd", [], lambda e: e.iota(
            iosc[:, 0:128], pattern=[[1, 128]], base=0, channel_multiplier=-1,
            allow_small_or_imprecise_dtypes=True), incs=[(GS, 1)])
        p.em("vector", [GS.now()], lambda e: e.tensor_single_scalar(
            g["ident"][:], iosc[:, 0:128], 0.0, ALU.is_equal), incs=[(VS, 1)])
        p.em("vector", [], lambda e: e.memset(g["onesk"][:, :], 1.0), incs=[(VS, 1)])
        p.em("vector", [], lambda e: e.memset(g["onesk_f"][:, :], 1.0), incs=[(VS, 1)])
        p.em("vector", [], lambda e: e.memset(g["onem"][:, :], 1.0 / 1024.0),
             incs=[(VS, 1)])
        p.em("vector", [], lambda e: e.memset(g["cbias"][:, 0:1], EPS), incs=[(VS, 1)])
        p.em("vector", [], lambda e: e.memset(g["onem_f"][:, :], 1.0 / 1024.0),
             incs=[(VS, 1)])
        p.em("vector", [], lambda e: e.memset(g["cbias"][:, 1:2], 1.0), incs=[(VS, 1)])
        p.em("gpsimd", [], lambda e: e.iota(
            iosc[0:32, 128:130], pattern=[[0, 2]], base=0, channel_multiplier=1,
            allow_small_or_imprecise_dtypes=True), incs=[(GS, 1)])
        p.em("vector", [GS.now()], lambda e: e.tensor_single_scalar(
            iosc[0:32, 130:132], iosc[0:32, 128:130], 16.0, ALU.is_lt), incs=[(VS, 1)])
        tk2 = VS.now()
        p.em("vector", [tk2], lambda e: e.tensor_scalar(
            g["selBC"][0:32, 0:1], iosc[0:32, 130:131], 1.0 / 16.0, None, ALU.mult),
            incs=[(VS, 1)])
        p.em("vector", [tk2], lambda e: e.tensor_scalar(
            iosc[0:32, 132:133], iosc[0:32, 131:132], -1.0, 1.0, ALU.mult, ALU.add),
            incs=[(VS, 1)])
        p.em("vector", [VS.now()], lambda e: e.tensor_scalar(
            g["selBC"][0:32, 1:2], iosc[0:32, 132:133], 1.0 / 16.0, None, ALU.mult),
            incs=[(VS, 1)])
        p.em("gpsimd", [], lambda e: e.iota(
            iosc[0:2, 256:512], pattern=[[1, 2], [0, 128]], base=0,
            channel_multiplier=-1, allow_small_or_imprecise_dtypes=True),
            incs=[(GS, 1)])
        p.em("vector", [GS.now()], lambda e: e.tensor_single_scalar(
            g["sel2"][:], iosc[0:2, 256:512], 0.0, ALU.is_equal), incs=[(VS, 1)])
        for nm, dst in (("cw_p", g["cw"]), ("cb_p", g["cb"]), ("bdt_p", g["bdt"]),
                        ("dv_p", g["dvec"])):
            p.em("sync", [], lambda e, nm=nm, dst=dst: e.dma_start(
                out=dst[:, :], in_=dram[nm][:, :]), incs=[(SD, 16)])
        p.em("sync", [], lambda e: e.dma_start(
            out=g["ebias_row"][:, :], in_=dram["e_bias"][:, :]), incs=[(SD, 16)])
        p.join([VS.now(), GS.now(), SD.now()])



def phase_mamba(p, nc, c, x_in, x_out, final=False, xout=None):
    dram, g = c["dram"], c["g"]
    SD, GD, CD = c["SD"], c["GD"], c["CD"]
    TS, VS, SS, GS = c["TS"], c["VS"], c["SS"], c["GS"]
    Tc = 512
    NCH = L // Tc

    with ExitStack() as mst:
        g["bgcg"] = mst.enter_context(nc.sbuf_tensor("bgcg", [2, T], BF16))
        _phase_mamba_body(p, nc, c, x_in, x_out, final, xout, Tc, NCH)
    p.join([c["TS"].now(), c["VS"].now(), c["SS"].now(), c["GS"].now(),
            c["SD"].now(), c["GD"].now(), c["CD"].now()])


def _phase_mamba_body(p, nc, c, x_in, x_out, final, xout, Tc, NCH):
    TS, VS, SS, GS = c["TS"], c["VS"], c["SS"], c["GS"]
    SD, GD, CD = c["SD"], c["GD"], c["CD"]
    _m1(p, nc, c, x_in, Tc, NCH)
    p.join([TS.now(), VS.now(), SS.now(), GS.now(), SD.now(), GD.now(), CD.now()])
    _m2a(p, nc, c, Tc, NCH)
    p.join([TS.now(), VS.now(), SS.now(), GS.now(), SD.now(), GD.now(), CD.now()])
    _m2b(p, nc, c, x_in, x_out, final, xout)


def _m1(p, nc, c, x_in, Tc, NCH):
    dram, g = c["dram"], c["g"]
    SD, GD, CD = c["SD"], c["GD"], c["CD"]
    TS, VS, SS, GS = c["TS"], c["VS"], c["SS"], c["GS"]

    with ExitStack() as st:
        EC = st.enter_context
        W = EC(nc.sbuf_tensor("m1_W", [128, HT * 2 * INNER], BF16))
        xr = EC(nc.sbuf_tensor("m1_xr", [128, HT * Tc], F32))
        xlnb = EC(nc.sbuf_tensor("m1_xln", [128, HT * Tc], BF16))
        xbq = EC(nc.sbuf_tensor("m1_xbq", [128, 2 * HT * Tc], BF16))
        rows = EC(nc.sbuf_tensor("m1_rows", [1, 5 * Tc], F32))
        rows_b = EC(nc.sbuf_tensor("m1_rows_b", [1, 2 * Tc], BF16))
        xp = EC(nc.sbuf_tensor("m1_xp", [128, 2 * CT * (Tc + 3)], BF16))
        xcv = EC(nc.sbuf_tensor("m1_xcv", [128, CT * Tc], BF16))
        zs = EC(nc.sbuf_tensor("m1_zs", [128, CT * Tc], BF16))
        cacc = EC(nc.sbuf_tensor("m1_cacc", [128, 2 * Tc], F32))
        pmm = [EC(nc.psum_tensor(f"m1_p{i}", [128, Tc], F32)) for i in range(4)]
        pst = [EC(nc.psum_tensor(f"m1_s{i}", [1, Tc], F32)) for i in range(2)]
        pbc = [EC(nc.psum_tensor(f"m1_b{i}", [128, Tc], F32)) for i in range(2)]

        p.em("gpsimd", [], lambda e: e.dma_start(
            out=W[:, :], in_=dram["W_in_T"].rearrange("k p f -> p k f")),
            incs=[(GD, 16)])
        w_tok = GD.now()

        lnsc = dict(onesk=g["onesk"], onem=g["onem"], pm=pst[0], pv=pst[1],
                    pmb=pbc[0], prb=pbc[1], bc_free=[], stat_free=[])
        bank_free = {i: [] for i in range(4)}
        xr_free = {0: [], 1: []}
        xp_free = {cix: [] for cix in range(CT)}
        zs_guard = []
        xcv_guard = []

        XPS = CT * (Tc + 3)
        acc_free = {0: [], 1: []}
        pendc = {"v": None}

        def _emit_conv(state):
            nonlocal xcv_guard
            b2, ch2, t02, gp2, evtoks = state
            xpv = xp[:, gp2 * XPS:(gp2 + 1) * XPS]
            xpn = xp[:, (1 - gp2) * XPS:(2 - gp2) * XPS]
            conv_last = None
            for cix in range(CT):
                base = cix * (Tc + 3)
                par = cix % 2
                acc = cacc[:, par * Tc:(par + 1) * Tc]
                p.em("vector", [evtoks[cix]] + acc_free[par],
                     lambda e, cix=cix, base=base, acc=acc, xpv=xpv: e.tensor_scalar(
                         acc, xpv[:, base + 0:base + 0 + Tc],
                         g["cw"][:, cix * 4 + 0:cix * 4 + 1], None, ALU.mult),
                     incs=[])
                for k in range(1, 4):
                    p.em("vector", [],
                         lambda e, cix=cix, k=k, base=base, acc=acc, xpv=xpv:
                         e.scalar_tensor_tensor(
                             acc, xpv[:, base + k:base + k + Tc],
                             g["cw"][:, cix * 4 + k:cix * 4 + k + 1], acc,
                             ALU.mult, ALU.add),
                         incs=[(VS, 1)] if k == 3 else [])
                cv_tok = VS.now()
                p.em("scalar", [cv_tok] + xcv_guard,
                     lambda e, cix=cix, acc=acc: e.activation(
                         xcv[:, cix * Tc:(cix + 1) * Tc], acc, AF.Silu,
                         bias=g["cb"][:, cix:cix + 1]),
                     incs=[(SS, 1)])
                xcv_guard = []
                conv_last = SS.now()
                acc_free[par] = [conv_last]
                if ch2 < NCH - 1:
                    p.em("vector", [],
                         lambda e, base=base, xpv=xpv, xpn=xpn: e.tensor_copy(
                             xpn[:, base:base + 3], xpv[:, base + Tc:base + Tc + 3]),
                         incs=[(VS, 1)])
                xp_free[gp2] = [VS.now()]
            p.em("scalar", [conv_last], lambda e, b2=b2, t02=t02: e.dma_start(
                out=dram["sp_xconv"][b2, :, :, t02:t02 + Tc]
                .rearrange("c p t -> p c t"),
                in_=xcv[:, :]), incs=[(CD, 16)])
            xcv_guard = [CD.now()]

        for b in range(NB):
            for ch in range(NCH):
                buf = ch % 2
                gp = (b * NCH + ch) % 2
                t0 = ch * Tc
                if ch == 0:
                    p.em("vector", xp_free[gp], lambda e, gp=gp: e.memset(
                        xp[:, gp * XPS:(gp + 1) * XPS]
                        .rearrange("p (c t) -> p c t", c=CT)[:, :, 0:3], 0.0),
                        incs=[(VS, 1)])
                    xp_free[gp] = [VS.now()]
                xr_v = xr[:, :]
                p.em("sync", xr_free[0], lambda e, b=b, t0=t0, xr_v=xr_v: e.dma_start(
                    out=xr_v,
                    in_=x_in[b, :, t0:t0 + Tc].rearrange("(h p) t -> p h t", p=128)),
                    incs=[(SD, 16)])
                xr_tok = SD.now()

                lnsc.update(
                    xb=xbq[:, 0:HT * Tc], xsq=xbq[:, HT * Tc:2 * HT * Tc],
                    mrow=rows[:, 0:Tc], msq=rows[:, Tc:2 * Tc],
                    vrow=rows[:, 2 * Tc:3 * Tc], lrow=rows[:, 3 * Tc:4 * Tc],
                    rrow=rows[:, 4 * Tc:5 * Tc],
                    mrow_b=rows_b[:, 0:Tc], rrow_b=rows_b[:, Tc:2 * Tc],
                    dtile=xbq[:, HT * Tc:2 * HT * Tc],
                    xr_ready=[xr_tok],
                )
                xln_tok = emit_ln(p, c, xr_v, xlnb, Tc, lnsc)
                xr_free[0] = [xln_tok]

                evac_tok = {}
                for o in range(32):
                    bank = o % 4
                    for k in range(HT):
                        waits = []
                        if k == 0:
                            waits += bank_free[bank]
                            if o == 0:
                                waits += [xln_tok, w_tok]
                        p.em("tensor", waits,
                             lambda e, o=o, k=k, bank=bank: e.matmul(
                                 pmm[bank][:],
                                 W[:, (k * 32 + o) * 128:(k * 32 + o + 1) * 128],
                                 xlnb[:, k * Tc:(k + 1) * Tc],
                                 start=(k == 0), stop=(k == HT - 1)),
                             incs=[(TS, 1)] if k == HT - 1 else [])
                    mm_tok = TS.now()
                    if o < CT:
                        dst = xp[:, gp * XPS + o * (Tc + 3) + 3:
                                 gp * XPS + o * (Tc + 3) + 3 + Tc]
                        w2 = [mm_tok] + xp_free[gp]
                        p.em("scalar", w2,
                             lambda e, dst=dst, bank=bank: e.copy(dst, pmm[bank][:]),
                             incs=[(SS, 1)])
                    else:
                        cix = o - CT
                        p.em("scalar", [mm_tok] + zs_guard,
                             lambda e, cix=cix, bank=bank: e.activation(
                                 zs[:, cix * Tc:(cix + 1) * Tc], pmm[bank][:], AF.Silu),
                             incs=[(SS, 1)])
                        zs_guard = []
                    evac_tok[o] = SS.now()
                    bank_free[bank] = [evac_tok[o]]

                p.em("scalar", [evac_tok[31]], lambda e, b=b, t0=t0: e.dma_start(
                    out=dram["sp_z"][b, :, :, t0:t0 + Tc].rearrange("c p t -> p c t"),
                    in_=zs[:, :]), incs=[(CD, 16)])
                zst_tok = CD.now()
                zs_guard = [zst_tok]

                if pendc["v"] is not None:
                    _emit_conv(pendc["v"])
                pendc["v"] = (b, ch, t0, gp, evac_tok)
        _emit_conv(pendc["v"])
        pendc["v"] = None

    p.join([TS.now(), VS.now(), SS.now(), GS.now(), SD.now(), GD.now(), CD.now()])

def _m2a(p, nc, c, Tc, NCH):
    dram, g = c["dram"], c["g"]
    SD, GD, CD = c["SD"], c["GD"], c["CD"]
    TS, VS, SS, GS = c["TS"], c["VS"], c["SS"], c["GS"]

    with ExitStack() as st:
        EC = st.enter_context
        W = EC(nc.sbuf_tensor("m2a_W", [128, CT * INNER], BF16))
        Wbc = EC(nc.sbuf_tensor("m2a_Wbc", [128, CT * 32], BF16))
        xcv = EC(nc.sbuf_tensor("m2a_xcv", [128, 2 * CT * Tc], BF16))
        dti = EC(nc.sbuf_tensor("m2a_dti", [128, CT * Tc], BF16))
        bcv = EC(nc.sbuf_tensor("m2a_bcv", [32, Tc], BF16))
        pmm = [EC(nc.psum_tensor(f"m2a_p{i}", [128, Tc], F32)) for i in range(6)]
        pbcm = EC(nc.psum_tensor("m2a_bc", [32, Tc], F32))
        pmean = EC(nc.psum_tensor("m2a_mean", [2, Tc], F32))

        p.em("gpsimd", [], lambda e: e.dma_start(
            out=W[:, :], in_=dram["W_dtin_T"].rearrange("k p f -> p k f")),
            incs=[(GD, 16)])
        p.em("gpsimd", [], lambda e: e.dma_start(
            out=Wbc[:, :], in_=dram["W_BC_T"].rearrange("k p f -> p k f")),
            incs=[(GD, 16)])
        w_tok = GD.now()

        bank_free = {i: [] for i in range(6)}
        xcv_free = {0: [], 1: []}
        bc_free = []
        mean_free = []
        store_free = []

        for b in range(NB):
            for ch in range(NCH):
                buf = ch % 2
                t0 = ch * Tc
                xcv_v = xcv[:, buf * CT * Tc:(buf + 1) * CT * Tc]
                p.em("sync", xcv_free[buf], lambda e, b=b, t0=t0, xcv_v=xcv_v:
                     e.dma_start(
                         out=xcv_v,
                         in_=dram["sp_xconv"][b, :, :, t0:t0 + Tc]
                         .rearrange("c p t -> p c t")),
                     incs=[(SD, 16)])
                x_tok = SD.now()

                evac_tok = {}
                for o in range(CT):
                    bank = o % 6
                    for k in range(CT):
                        waits = []
                        if k == 0:
                            waits += bank_free[bank]
                            if o == 0:
                                waits += [x_tok, w_tok]
                        p.em("tensor", waits,
                             lambda e, o=o, k=k, bank=bank, xcv_v=xcv_v: e.matmul(
                                 pmm[bank][:],
                                 W[:, (k * CT + o) * 128:(k * CT + o + 1) * 128],
                                 xcv_v[:, k * Tc:(k + 1) * Tc],
                                 start=(k == 0), stop=(k == CT - 1)),
                             incs=[(TS, 1)] if k == CT - 1 else [])
                    mm_tok = TS.now()
                    p.em("scalar", [mm_tok] + (store_free if o == 0 else []),
                         lambda e, o=o, bank=bank: e.copy(
                             dti[:, o * Tc:(o + 1) * Tc], pmm[bank][:]),
                         incs=[(SS, 1)])
                    evac_tok[o] = SS.now()
                    bank_free[bank] = [evac_tok[o]]
                    if o == 0:
                        store_free = []
                for k in range(CT):
                    p.em("tensor", bc_free if k == 0 else [],
                         lambda e, k=k, xcv_v=xcv_v: e.matmul(
                             pbcm[:], Wbc[:, k * 32:(k + 1) * 32],
                             xcv_v[:, k * Tc:(k + 1) * Tc],
                             start=(k == 0), stop=(k == CT - 1)),
                         incs=[(TS, 1)] if k == CT - 1 else [])
                bcm_tok = TS.now()
                p.em("scalar", [bcm_tok], lambda e: e.copy(bcv[:, :], pbcm[:]),
                     incs=[(SS, 1)])
                bcv_tok = SS.now()
                bc_free = [bcv_tok]
                p.em("tensor", [bcv_tok] + mean_free, lambda e: e.matmul(
                    pmean[:], g["selBC"][:, :], bcv[:, :], start=True, stop=True),
                    incs=[(TS, 1)])
                mean_tok = TS.now()
                p.em("scalar", [mean_tok], lambda e, b=b, t0=t0: e.activation(
                    g["bgcg"][0:2, b * L + t0:b * L + t0 + Tc], pmean[:], AF.Sigmoid),
                    incs=[(SS, 1)])
                mean_free = [SS.now()]
                xcv_free[buf] = [mm_tok]
                p.em("scalar", [evac_tok[CT - 1]], lambda e, b=b, t0=t0: e.dma_start(
                    out=dram["sp_dtin"][b, :, :, t0:t0 + Tc].rearrange("c p t -> p c t"),
                    in_=dti[:, :]), incs=[(CD, 16)])
                store_free = [CD.now()]


def _m2b(p, nc, c, x_in, x_out, final, xout):
    dram, g = c["dram"], c["g"]
    SD, GD, CD = c["SD"], c["GD"], c["CD"]
    TS, VS, SS, GS = c["TS"], c["VS"], c["SS"], c["GS"]

    Tc2 = 256
    NCH2 = L // Tc2
    with ExitStack() as st:
        EC = st.enter_context
        Wdt = EC(nc.sbuf_tensor("m2b_Wdt", [128, CT * INNER], BF16))
        Wo = EC(nc.sbuf_tensor("m2b_Wo", [128, CT * H], BF16))
        dti = EC(nc.sbuf_tensor("m2b_dti", [128, CT * Tc2], BF16))
        xcv = EC(nc.sbuf_tensor("m2b_xcv", [128, CT * Tc2], BF16))
        zsb = EC(nc.sbuf_tensor("m2b_zs", [128, CT * Tc2], BF16))
        xrs = EC(nc.sbuf_tensor("m2b_xrs", [128, HT * Tc2], F32))
        U = EC(nc.sbuf_tensor("m2b_U", [128, CT * Tc2], F32))
        A = EC(nc.sbuf_tensor("m2b_A", [128, CT * Tc2], F32))
        Hs = EC(nc.sbuf_tensor("m2b_H", [128, CT * Tc2], F32))
        YB = EC(nc.sbuf_tensor("m2b_YB", [128, 2 * CT * Tc2], BF16))
        carry = EC(nc.sbuf_tensor("m2b_carry", [128, CT], F32))
        pmm = [EC(nc.psum_tensor(f"m2b_p{i}", [128, Tc2], F32)) for i in range(4)]
        pout = [EC(nc.psum_tensor(f"m2b_o{i}", [128, Tc2], F32)) for i in range(2)]
        pg = [EC(nc.psum_tensor(f"m2b_g{i}", [128, Tc2], F32)) for i in range(2)]

        p.em("gpsimd", [], lambda e: e.dma_start(
            out=Wdt[:, :], in_=dram["W_dt_T"].rearrange("k p f -> p k f")),
            incs=[(GD, 16)])
        p.em("gpsimd", [], lambda e: e.dma_start(
            out=Wo[:, :], in_=dram["W_outA_T"].rearrange("k p f -> p k f")),
            incs=[(GD, 16)])
        w_tok = GD.now()

        bank_free = {i: [] for i in range(4)}
        pout_free = {i: [] for i in range(2)}
        dti_free = {0: [], 1: []}
        xcv_free = []
        zs_free = []
        xr_free = []
        yb_free = {0: [], 1: []}
        chain_free_h = {0: [], 1: []}
        pg_free = []
        n2 = CT * Tc2
        pendq = []

        def _emit_outproj(state):
            b2, t02, par2, y2 = state
            yv = YB[:, par2 * n2:(par2 + 1) * n2]
            p.em("sync", xr_free, lambda e, b2=b2, t02=t02: e.dma_start(
                out=xrs[:, :],
                in_=x_in[b2, :, t02:t02 + Tc2].rearrange("(h p) t -> p h t", p=128)),
                incs=[(SD, 16)])
            xr_tok = SD.now()
            add_toks = []
            for o in range(HT):
                bank = o % 2
                for k in range(CT):
                    waits = []
                    if k == 0:
                        waits += pout_free[bank]
                        if o == 0:
                            waits.append(y2)
                    p.em("tensor", waits,
                         lambda e, o=o, k=k, bank=bank, yv=yv: e.matmul(
                             pout[bank][:],
                             Wo[:, (k * HT + o) * 128:(k * HT + o + 1) * 128],
                             yv[:, k * Tc2:(k + 1) * Tc2],
                             start=(k == 0), stop=(k == CT - 1)),
                         incs=[(TS, 1)] if k == CT - 1 else [])
                mo_tok = TS.now()
                p.em("vector", [mo_tok, xr_tok],
                     lambda e, o=o, bank=bank: e.tensor_add(
                         xrs[:, o * Tc2:(o + 1) * Tc2],
                         xrs[:, o * Tc2:(o + 1) * Tc2], pout[bank][:]),
                     incs=[(VS, 1)])
                add_toks.append(VS.now())
                pout_free[bank] = [add_toks[-1]]
            yb_free[par2] = [TS.now()]
            p.em("sync", [add_toks[-1]], lambda e, b2=b2, t02=t02: e.dma_start(
                out=x_out[b2, :, t02:t02 + Tc2].rearrange("(h p) t -> p h t", p=128),
                in_=xrs[:, :]), incs=[(SD, 16)])
            toks = [SD.now()]
            if final and xout is not None:
                p.em("sync", [], lambda e, b2=b2, t02=t02: e.dma_start(
                    out=xout[b2, :, t02:t02 + Tc2].rearrange("(h p) t -> p h t", p=128),
                    in_=xrs[:, :]), incs=[(SD, 16)])
                toks.append(SD.now())
            return toks

        for b in range(NB):
            p.em("vector", list(chain_free_h[0]) + list(chain_free_h[1]),
                 lambda e: e.memset(carry[:, :], 0.0),
                 incs=[(VS, 1)])
            carry_tok = VS.now()
            for ch in range(NCH2):
                buf = ch % 2
                par = (b * NCH2 + ch) % 2
                t0 = ch * Tc2
                dti_v = dti[:, 0:n2]
                xcv_v = xcv[:, 0:n2]
                p.em("sync", dti_free[0], lambda e, b=b, t0=t0, dti_v=dti_v:
                     e.dma_start(out=dti_v,
                                 in_=dram["sp_dtin"][b, :, :, t0:t0 + Tc2]
                                 .rearrange("c p t -> p c t")),
                     incs=[(SD, 16)])
                dti_tok = SD.now()
                p.em("sync", xcv_free, lambda e, b=b, t0=t0, xcv_v=xcv_v:
                     e.dma_start(out=xcv_v,
                                 in_=dram["sp_xconv"][b, :, :, t0:t0 + Tc2]
                                 .rearrange("c p t -> p c t")),
                     incs=[(SD, 16)])
                xcv_tok = SD.now()
                p.em("sync", zs_free, lambda e, b=b, t0=t0: e.dma_start(
                    out=zsb[:, :],
                    in_=dram["sp_z"][b, :, :, t0:t0 + Tc2].rearrange("c p t -> p c t")),
                    incs=[(SD, 16)])
                zs_tok = SD.now()

                evac_tok = {}
                for o in range(CT):
                    bank = o % 4
                    for k in range(CT):
                        waits = []
                        if k == 0:
                            waits += bank_free[bank]
                            if o == 0:
                                waits += [dti_tok, w_tok]
                        p.em("tensor", waits,
                             lambda e, o=o, k=k, bank=bank, dti_v=dti_v: e.matmul(
                                 pmm[bank][:],
                                 Wdt[:, (k * CT + o) * 128:(k * CT + o + 1) * 128],
                                 dti_v[:, k * Tc2:(k + 1) * Tc2],
                                 start=(k == 0), stop=(k == CT - 1)),
                             incs=[(TS, 1)] if k == CT - 1 else [])
                    mm_tok = TS.now()
                    cguard = chain_free_h[0] if o == 0 else (
                        chain_free_h[1] if o == 8 else [])
                    p.em("scalar", [mm_tok] + cguard,
                         lambda e, o=o, bank=bank: e.activation(
                             U[:, o * Tc2:(o + 1) * Tc2], pmm[bank][:],
                             AF.Identity, bias=g["bdt"][:, o:o + 1]),
                         incs=[(SS, 1)])
                    evac_tok[o] = SS.now()
                    bank_free[bank] = [evac_tok[o]]
                u_tok = evac_tok[CT - 1]
                dti_free[0] = [mm_tok]
                off = b * L + t0
                p.em("tensor", pg_free, lambda e, off=off: e.matmul(
                    pg[0][:], g["sel2"][:, 0:128], g["bgcg"][0:2, off:off + Tc2],
                    start=True, stop=True), incs=[(TS, 1)])
                p.em("tensor", [], lambda e, off=off: e.matmul(
                    pg[1][:], g["sel2"][:, 128:256], g["bgcg"][0:2, off:off + Tc2],
                    start=True, stop=True), incs=[(TS, 1)])
                gbc_tok = TS.now()

                if len(pendq) >= 2:
                    xr_free = _emit_outproj(pendq.pop(0))

                n2h = n2 // 2
                HC = [range(0, 8), range(8, 16)]
                sl = [slice(0, n2h), slice(n2h, n2)]
                u_h = [evac_tok[7], evac_tok[15]]
                tok = {}
                for hc in range(2):
                    p.em("scalar", [u_h[hc]], lambda e, s=sl[hc]: e.activation(
                        A[:, s], U[:, s], AF.Abs), incs=[(SS, 1)])
                    tok[("abs", hc)] = SS.now()
                for hc in range(2):
                    p.em("scalar", [], lambda e, s=sl[hc]: e.activation(
                        A[:, s], A[:, s], AF.Exp, scale=-1.0), incs=[(SS, 1)])
                for hc in range(2):
                    p.em("scalar", [], lambda e, s=sl[hc]: e.activation(
                        A[:, s], A[:, s], AF.Ln, bias=g["cbias"][:, 1:2]),
                        incs=[(SS, 1)])
                    tok[("ln", hc)] = SS.now()
                for hc in range(2):
                    p.em("vector", [tok[("ln", hc)], u_h[hc]],
                         lambda e, s=sl[hc]: e.scalar_tensor_tensor(
                             U[:, s], U[:, s], 0.0, A[:, s], ALU.max, ALU.add),
                         incs=[(VS, 1)])
                    tok[("spu", hc)] = VS.now()
                for hc in range(2):
                    p.em("scalar", [tok[("spu", hc)]],
                         lambda e, s=sl[hc]: e.activation(
                             A[:, s], U[:, s], AF.Sigmoid, scale=0.1),
                         incs=[(SS, 1)])
                    tok[("sig", hc)] = SS.now()
                for hc in range(2):
                    p.em("gpsimd", [tok[("sig", hc)]],
                         lambda e, s=sl[hc]: e.tensor_scalar(
                             U[:, s], A[:, s], -1.0, 1.0, ALU.mult, ALU.add),
                         incs=[(GS, 1)])
                    tok[("g1", hc)] = GS.now()
                for hc in range(2):
                    for i, cix in enumerate(HC[hc]):
                        p.em("vector",
                             [gbc_tok, tok[("g1", hc)]] if i == 0 else [],
                             lambda e, cix=cix: e.tensor_mul(
                                 U[:, cix * Tc2:(cix + 1) * Tc2],
                                 U[:, cix * Tc2:(cix + 1) * Tc2], pg[0][:]),
                             incs=[(VS, 1)] if i == 7 else [])
                    tok[("gb", hc)] = VS.now()
                for hc in range(2):
                    p.em("vector", [tok[("gb", hc)], xcv_tok],
                         lambda e, s=sl[hc], xcv_v=xcv_v: e.tensor_mul(
                             U[:, s], U[:, s], xcv_v[:, s]), incs=[(VS, 1)])
                    tok[("gx", hc)] = VS.now()
                for hc in range(2):
                    p.em("vector", [], lambda e, s=sl[hc]: e.tensor_scalar(
                        A[:, s], A[:, s], 0.01, 0.99, ALU.max, ALU.min),
                        incs=[(VS, 1)])
                    tok[("ac", hc)] = VS.now()
                for hc in range(2):
                    for i, cix in enumerate(HC[hc]):
                        w3 = [tok[("ac", hc)], tok[("gx", hc)]] if i == 0 else []
                        if ch == 0 and cix == 0:
                            w3.append(carry_tok)
                        p.em("vector", w3,
                             lambda e, cix=cix: e.tensor_tensor_scan(
                                 Hs[:, cix * Tc2:(cix + 1) * Tc2],
                                 A[:, cix * Tc2:(cix + 1) * Tc2],
                                 U[:, cix * Tc2:(cix + 1) * Tc2],
                                 carry[:, cix:cix + 1], ALU.mult, ALU.add),
                             incs=[(VS, 1)])
                        p.em("vector", [VS.now()],
                             lambda e, cix=cix: e.tensor_copy(
                                 carry[:, cix:cix + 1],
                                 Hs[:, (cix + 1) * Tc2 - 1:(cix + 1) * Tc2]),
                             incs=[(VS, 1)])
                for hc in range(2):
                    for cix in HC[hc]:
                        p.em("vector", [],
                             lambda e, cix=cix: e.tensor_mul(
                                 Hs[:, cix * Tc2:(cix + 1) * Tc2],
                                 Hs[:, cix * Tc2:(cix + 1) * Tc2], pg[1][:]),
                             incs=[])
                    for i, cix in enumerate(HC[hc]):
                        p.em("vector", [],
                             lambda e, cix=cix, xcv_v=xcv_v: e.scalar_tensor_tensor(
                                 U[:, cix * Tc2:(cix + 1) * Tc2],
                                 xcv_v[:, cix * Tc2:(cix + 1) * Tc2],
                                 g["dvec"][:, cix:cix + 1],
                                 Hs[:, cix * Tc2:(cix + 1) * Tc2],
                                 ALU.mult, ALU.add),
                             incs=[(VS, 1)] if i == 7 else [])
                    tok[("gd", hc)] = VS.now()
                    yv = YB[:, par * n2:(par + 1) * n2]
                    p.em("gpsimd",
                         [tok[("gd", hc)], zs_tok] +
                         (yb_free[par] + chain_free for _ in ())
                         if False else
                         [tok[("gd", hc)], zs_tok] +
                         (yb_free[par] if hc == 0 else []),
                         lambda e, s=sl[hc], yv=yv: e.tensor_mul(
                             yv[:, s], U[:, s], zsb[:, s]), incs=[(GS, 1)])
                    tok[("y", hc)] = GS.now()
                y_tok = tok[("y", 1)]
                gd_tok = tok[("gd", 1)]
                pg_free = [tok[("gd", 0)], gd_tok, y_tok]
                chain_free_h[0] = [tok[("y", 0)]]
                chain_free_h[1] = [tok[("y", 1)]]
                zs_free = [y_tok]
                xcv_free = [y_tok]
                pendq.append((b, t0, par, y_tok))
            while pendq:
                xr_free = _emit_outproj(pendq.pop(0))

    p.join([TS.now(), VS.now(), SS.now(), GS.now(), SD.now(), GD.now(), CD.now()])



def phase_ffn(p, nc, c, x_in, x_out, final, xout, tag):
    TS, VS, SS, GS = c["TS"], c["VS"], c["SS"], c["GS"]
    SD, GD, CD = c["SD"], c["GD"], c["CD"]
    with ExitStack() as fst:
        EC = fst.enter_context
        combT = EC(nc.sbuf_tensor(tag + "_combT", [32, T], BF16))
        indE = EC(nc.sbuf_tensor(tag + "_indE", [32, NE * 128], BF16))
        identF = EC(nc.sbuf_tensor(tag + "_idF", [128, 128], F32))
        fg = dict(combT=combT, indE=indE, identF=identF)
        with ExitStack() as st2:
            iosc = st2.enter_context(nc.sbuf_tensor(tag + "_iosc", [128, NE * 128], F32))
            p.em("gpsimd", [], lambda e: e.iota(
                iosc[0:32, :], pattern=[[1, NE], [0, 128]], base=0,
                channel_multiplier=-1, allow_small_or_imprecise_dtypes=True),
                incs=[(GS, 1)])
            p.em("vector", [GS.now()], lambda e: e.tensor_single_scalar(
                indE[:], iosc[0:32, :], 0.0, ALU.is_equal), incs=[(VS, 1)])
            p.em("gpsimd", [VS.now()], lambda e: e.iota(
                iosc[:, 0:128], pattern=[[1, 128]], base=0,
                channel_multiplier=-1, allow_small_or_imprecise_dtypes=True),
                incs=[(GS, 1)])
            p.em("vector", [GS.now()], lambda e: e.tensor_single_scalar(
                identF[:], iosc[:, 0:128], 0.0, ALU.is_equal),
                incs=[(VS, 1)])
            p.join([VS.now(), GS.now()])
        _f0(p, nc, c, fg, x_in, tag)
        if DEBUG_COMBT and tag == "f1":
            p.em("sync", [c["SS"].now(), c["CD"].now()], lambda e: e.dma_start(
                out=c["dram"]["dbg_combT"][:, :], in_=combT[:, :]),
                incs=[(SD, 16)])
            p.em("sync", [], lambda e: e.dma_start(
                out=c["dram"]["dbg_xln"][:, :, :], in_=c["dram"]["sp_xln"][:, :, :]),
                incs=[(SD, 16)])
            p.em("sync", [], lambda e: e.dma_start(
                out=c["dram"]["dbg_sh"][:, :, :], in_=c["dram"]["sp_shared"][:, :, :]),
                incs=[(SD, 16)])
        p.join([TS.now(), VS.now(), SS.now(), GS.now(), SD.now(), GD.now(), CD.now()])
        for hh in range(2):
            _fhalf(p, nc, c, fg, x_in, x_out, final, xout, hh, tag)
            p.join([TS.now(), VS.now(), SS.now(), GS.now(),
                    SD.now(), GD.now(), CD.now()])


def _f0(p, nc, c, fg, x_in, tag):
    dram, g = c["dram"], c["g"]
    SD, GD, CD = c["SD"], c["GD"], c["CD"]
    TS, VS, SS, GS = c["TS"], c["VS"], c["SS"], c["GS"]
    Tc = 512
    NCH = T // Tc
    with ExitStack() as st:
        EC = st.enter_context
        Wg = EC(nc.sbuf_tensor(tag + "0_Wg", [128, HT * IS], BF16))
        Wu = EC(nc.sbuf_tensor(tag + "0_Wu", [128, HT * IS], BF16))
        Wd = EC(nc.sbuf_tensor(tag + "0_Wd", [128, (IS // 128) * H], BF16))
        xr = EC(nc.sbuf_tensor(tag + "0_xr", [128, 2 * HT * Tc], F32))
        xlnb = EC(nc.sbuf_tensor(tag + "0_xln", [128, HT * Tc], BF16))
        xbq = EC(nc.sbuf_tensor(tag + "0_xbq", [128, 2 * HT * Tc], BF16))
        rows = EC(nc.sbuf_tensor(tag + "0_rows", [1, 5 * Tc], F32))
        rows_b = EC(nc.sbuf_tensor(tag + "0_rows_b", [1, 2 * Tc], BF16))
        sg = EC(nc.sbuf_tensor(tag + "0_sg", [128, HT * Tc], BF16))
        su = EC(nc.sbuf_tensor(tag + "0_su", [128, HT * Tc], BF16))
        sh = EC(nc.sbuf_tensor(tag + "0_sh", [128, HT * Tc], BF16))
        sho = EC(nc.sbuf_tensor(tag + "0_sho", [128, HT * Tc], BF16))
        rsc = EC(nc.sbuf_tensor(tag + "0_rsc", [128, 320], F32))
        d32 = EC(nc.sbuf_tensor(tag + "0_d32", [128, HT * Tc], F32))
        xln32 = EC(nc.sbuf_tensor(tag + "0_x32", [128, HT * Tc], F32))
        Wr32 = EC(nc.sbuf_tensor(tag + "0_Wr32", [128, HT * NE], F32))
        pool = [EC(nc.psum_tensor(f"{tag}0_p{i}", [128, Tc], F32)) for i in range(4)]
        pst = [EC(nc.psum_tensor(f"{tag}0_s{i}", [1, Tc], F32)) for i in range(2)]
        pbc = [EC(nc.psum_tensor(f"{tag}0_b{i}", [128, Tc], F32)) for i in range(2)]

        p.em("gpsimd", [], lambda e: e.dma_start(
            out=Wg[:, :], in_=dram["Wg_T"].rearrange("k p f -> p k f")),
            incs=[(GD, 16)])
        p.em("gpsimd", [], lambda e: e.dma_start(
            out=Wu[:, :], in_=dram["Wu_T"].rearrange("k p f -> p k f")),
            incs=[(GD, 16)])
        p.em("gpsimd", [], lambda e: e.dma_start(
            out=Wd[:, :], in_=dram["Wd_T"].rearrange("k p f -> p k f")),
            incs=[(GD, 16)])
        p.em("sync", [], lambda e: e.dma_start(
            out=Wr32[:, :], in_=dram["Wr_T"].rearrange("k p f -> p k f")),
            incs=[(SD, 16)])
        w_tok = GD.now()
        wr_tok = SD.now()

        ebb = pool[0]
        p.em("tensor", [], lambda e: e.matmul(
            ebb[:, 0:NE], g["onesk_f"][:, 0:128], g["ebias_row"][:, 0:NE],
            start=True, stop=True), incs=[(TS, 1)])
        p.em("vector", [TS.now()], lambda e: e.tensor_copy(
            g["ebias_bc"][:, :], ebb[:, 0:NE]), incs=[(VS, 1)])
        ebias_tok = VS.now()
        p.pending["tensor"].append(ebias_tok)

        lnsc = dict(onesk=g["onesk"], onem=g["onem"], pm=pst[0], pv=pst[1],
                    pmb=pbc[0], prb=pbc[1], bc_free=[], stat_free=[])
        bank_free = {i: [] for i in range(4)}
        xr_free = {0: [], 1: []}
        sgsu_guard = []
        sho_guard = []
        comb_cols = 32

        for ch in range(NCH):
            buf = ch % 2
            gt0 = ch * Tc
            b = gt0 // L
            t0 = gt0 % L
            xr_v = xr[:, buf * HT * Tc:(buf + 1) * HT * Tc]
            p.em("sync", xr_free[buf], lambda e, b=b, t0=t0, xr_v=xr_v: e.dma_start(
                out=xr_v,
                in_=x_in[b, :, t0:t0 + Tc].rearrange("(h p) t -> p h t", p=128)),
                incs=[(SD, 16)])
            xr_tok = SD.now()
            lnsc.update(
                xb=xbq[:, 0:HT * Tc], xsq=xbq[:, HT * Tc:2 * HT * Tc],
                mrow=rows[:, 0:Tc], msq=rows[:, Tc:2 * Tc],
                vrow=rows[:, 2 * Tc:3 * Tc], lrow=rows[:, 3 * Tc:4 * Tc],
                rrow=rows[:, 4 * Tc:5 * Tc],
                mrow_b=rows_b[:, 0:Tc], rrow_b=rows_b[:, Tc:2 * Tc],
                dtile=xbq[:, HT * Tc:2 * HT * Tc],
                d32=d32, xln32_out=xln32, fp32_stats=True,
                xr_ready=[xr_tok],
            )
            xln_tok = emit_ln(p, c, xr_v, xlnb, Tc, lnsc)
            xr_free[buf] = [xln_tok]
            p.em("scalar", [xln_tok], lambda e, gt0=gt0: e.dma_start(
                out=dram["sp_xln"][:, :, gt0:gt0 + Tc].rearrange("h p t -> p h t"),
                in_=xlnb[:, :]), incs=[(CD, 16)])
            lnsc["out_free"] = [CD.now()]

            for tt in range(4):
                bank = tt % 4
                for k in range(HT):
                    w4 = []
                    if k == 0:
                        w4 += bank_free[bank]
                        if tt == 0:
                            w4 += [xln_tok, w_tok, wr_tok]
                    p.em("tensor", w4,
                         lambda e, tt=tt, k=k, bank=bank: e.matmul(
                             pool[bank][:, 0:NE],
                             xln32[:, k * Tc + tt * 128:k * Tc + tt * 128 + 128],
                             Wr32[:, k * NE:(k + 1) * NE],
                             start=(k == 0), stop=(k == HT - 1)),
                         incs=[(TS, 1)] if k == HT - 1 else [])
                rl_tok = TS.now()
                s0 = tt * 32
                rr = rsc[:, s0:s0 + 32]
                sm = rsc[:, 128 + s0:128 + s0 + 32]
                t8 = rsc[:, 256 + tt * 8:256 + tt * 8 + 8]
                nmx = rsc[:, 288 + tt:288 + tt + 1]
                ism = rsc[:, 296 + tt:296 + tt + 1]
                p.em("vector", [rl_tok],
                     lambda e, rr=rr, bank=bank: e.tensor_copy(rr, pool[bank][:, 0:NE]),
                     incs=[(VS, 1)])
                rr_tok = VS.now()
                bank_free[bank] = [rr_tok]
                p.em("vector", [rr_tok],
                     lambda e, rr=rr, nmx=nmx: e.reduce_max(
                         nmx, rr, axis=AX.X, negate=True), incs=[(VS, 1)])
                p.em("scalar", [VS.now()],
                     lambda e, rr=rr, sm=sm, nmx=nmx: e.activation(
                         sm, rr, AF.Exp, bias=nmx), incs=[(SS, 1)])
                p.em("vector", [SS.now()],
                     lambda e, sm=sm, ism=ism: e.reduce_sum(ism, sm, axis=AX.X),
                     incs=[(VS, 1)])
                p.em("vector", [VS.now()],
                     lambda e, ism=ism: e.reciprocal(ism, ism), incs=[(VS, 1)])
                p.em("vector", [VS.now()],
                     lambda e, sm=sm, ism=ism: e.tensor_scalar(
                         sm, sm, ism, None, ALU.mult), incs=[(VS, 1)])
                rw_tok = VS.now()
                p.em("vector", [rw_tok, ebias_tok],
                     lambda e, rr=rr, sm=sm: e.tensor_add(rr, sm, g["ebias_bc"][:, :]),
                     incs=[(VS, 1)])
                p.em("vector", [VS.now()],
                     lambda e, rr=rr, t8=t8: e.max(t8, rr), incs=[(VS, 1)])
                p.em("vector", [VS.now()],
                     lambda e, rr=rr, t8=t8: e.tensor_single_scalar(
                         rr, rr, t8[:, 5:6], ALU.is_ge), incs=[(VS, 1)])
                p.em("vector", [VS.now()],
                     lambda e, rr=rr, sm=sm: e.tensor_mul(sm, rr, sm), incs=[(VS, 1)])
                cm_tok = VS.now()
                p.em("vector", [cm_tok],
                     lambda e, sm=sm, ism=ism: e.reduce_sum(ism, sm, axis=AX.X),
                     incs=[(VS, 1)])
                p.em("vector", [VS.now()],
                     lambda e, ism=ism: e.tensor_scalar_max(ism, ism, 1e-12),
                     incs=[(VS, 1)])
                p.em("vector", [VS.now()],
                     lambda e, ism=ism: e.reciprocal(ism, ism), incs=[(VS, 1)])
                p.em("vector", [VS.now()],
                     lambda e, sm=sm, ism=ism: e.tensor_scalar(
                         sm, sm, ism, None, ALU.mult), incs=[(VS, 1)])
                comb_tok = VS.now()
                bank2 = (tt + 1) % 4
                p.em("tensor", [comb_tok] + bank_free[bank2],
                     lambda e, sm=sm, bank2=bank2: e.transpose(
                         pool[bank2][0:32, 0:128], sm, fg["identF"][:, :]),
                     incs=[(TS, 1)])
                tr_tok = TS.now()
                p.em("scalar", [tr_tok],
                     lambda e, gt0=gt0, tt=tt, bank2=bank2: e.copy(
                         fg["combT"][0:32, gt0 + tt * 128:gt0 + (tt + 1) * 128],
                         pool[bank2][0:32, 0:128]),
                     incs=[(SS, 1)])
                bank_free[bank2] = [SS.now()]

            ev_sg = {}
            for o in range(HT):
                bank = o % 4
                for k in range(HT):
                    w4 = []
                    if k == 0:
                        w4 += bank_free[bank]
                        if o == 0:
                            w4 += sgsu_guard
                    p.em("tensor", w4,
                         lambda e, o=o, k=k, bank=bank: e.matmul(
                             pool[bank][:],
                             Wg[:, (k * HT + o) * 128:(k * HT + o + 1) * 128],
                             xlnb[:, k * Tc:(k + 1) * Tc],
                             start=(k == 0), stop=(k == HT - 1)),
                         incs=[(TS, 1)] if k == HT - 1 else [])
                mm_tok = TS.now()
                p.em("scalar", [mm_tok],
                     lambda e, o=o, bank=bank: e.activation(
                         sg[:, o * Tc:(o + 1) * Tc], pool[bank][:], AF.Silu),
                     incs=[(SS, 1)])
                ev_sg[o] = SS.now()
                bank_free[bank] = [ev_sg[o]]
            sgsu_guard = []
            ev_su = {}
            for o in range(HT):
                bank = o % 4
                for k in range(HT):
                    w4 = bank_free[bank] if k == 0 else []
                    p.em("tensor", w4,
                         lambda e, o=o, k=k, bank=bank: e.matmul(
                             pool[bank][:],
                             Wu[:, (k * HT + o) * 128:(k * HT + o + 1) * 128],
                             xlnb[:, k * Tc:(k + 1) * Tc],
                             start=(k == 0), stop=(k == HT - 1)),
                         incs=[(TS, 1)] if k == HT - 1 else [])
                mm_tok = TS.now()
                p.em("scalar", [mm_tok],
                     lambda e, o=o, bank=bank: e.copy(
                         su[:, o * Tc:(o + 1) * Tc], pool[bank][:]),
                     incs=[(SS, 1)])
                ev_su[o] = SS.now()
                bank_free[bank] = [ev_su[o]]
            p.em("vector", [ev_sg[HT - 1], ev_su[HT - 1]],
                 lambda e: e.tensor_mul(sh[:, :], sg[:, :], su[:, :]), incs=[(VS, 1)])
            sh_tok = VS.now()
            sgsu_guard = [sh_tok]
            ev_sd = {}
            for o in range(HT):
                bank = o % 4
                for k in range(HT):
                    w4 = []
                    if k == 0:
                        w4 += bank_free[bank]
                        if o == 0:
                            w4 += [sh_tok] + sho_guard
                    p.em("tensor", w4,
                         lambda e, o=o, k=k, bank=bank: e.matmul(
                             pool[bank][:],
                             Wd[:, (k * HT + o) * 128:(k * HT + o + 1) * 128],
                             sh[:, k * Tc:(k + 1) * Tc],
                             start=(k == 0), stop=(k == HT - 1)),
                         incs=[(TS, 1)] if k == HT - 1 else [])
                mm_tok = TS.now()
                p.em("scalar", [mm_tok] + (sho_guard if o == 0 else []),
                     lambda e, o=o, bank=bank: e.copy(
                         sho[:, o * Tc:(o + 1) * Tc], pool[bank][:]),
                     incs=[(SS, 1)])
                ev_sd[o] = SS.now()
                bank_free[bank] = [ev_sd[o]]
            sho_guard = []
            p.em("scalar", [ev_sd[HT - 1]], lambda e, gt0=gt0: e.dma_start(
                out=dram["sp_shared"][:, :, gt0:gt0 + Tc].rearrange("h p t -> p h t"),
                in_=sho[:, :]), incs=[(CD, 16)])
            sho_guard = [CD.now()]


def _fhalf(p, nc, c, fg, x_in, x_out, final, xout, hh, tag):
    dram, g = c["dram"], c["g"]
    SD, GD, CD = c["SD"], c["GD"], c["CD"]
    TS, VS, SS, GS = c["TS"], c["VS"], c["SS"], c["GS"]
    Tc = 512
    NCHH = L // Tc
    gbase = hh * L
    b = hh
    with ExitStack() as st:
        EC = st.enter_context
        xlh = EC(nc.sbuf_tensor(tag + f"h{hh}_xln", [128, HT * L], BF16))
        acc = EC(nc.sbuf_tensor(tag + f"h{hh}_acc", [128, HT * L], F32))
        wgu = EC(nc.sbuf_tensor(tag + f"h{hh}_wgu", [128, 2 * HT * 2 * I_EXP], BF16))
        wdn = EC(nc.sbuf_tensor(tag + f"h{hh}_wdn", [128, 2 * 4 * H], BF16))
        sgt = EC(nc.sbuf_tensor(tag + f"h{hh}_sg", [128, 2 * 4 * Tc], BF16))
        hmt = EC(nc.sbuf_tensor(tag + f"h{hh}_hm", [128, 2 * 4 * Tc], BF16))
        xrf = EC(nc.sbuf_tensor(tag + f"h{hh}_xr", [128, HT * Tc], F32))
        pool = [EC(nc.psum_tensor(f"{tag}h{hh}_p{i}", [128, Tc], F32)) for i in range(4)]
        pacc = [EC(nc.psum_tensor(f"{tag}h{hh}_a{i}", [128, Tc], F32)) for i in range(2)]
        pcbc = EC(nc.psum_tensor(f"{tag}h{hh}_cb", [128, Tc], F32))

        p.em("sync", [], lambda e: e.dma_start(
            out=xlh[:, :],
            in_=dram["sp_xln"][:, :, gbase:gbase + L].rearrange("h p t -> p h t")),
            incs=[(SD, 16)])
        ld_tok = SD.now()

        wready = {}
        def load_expert(e_ix, slot, waits):
            p.em("gpsimd", waits, lambda e, e_ix=e_ix, slot=slot: e.dma_start(
                out=wgu[:, slot * HT * 2 * I_EXP:(slot + 1) * HT * 2 * I_EXP],
                in_=dram["gup_T"][e_ix].rearrange("k p f -> p k f")),
                incs=[(GD, 16)])
            p.em("gpsimd", [], lambda e, e_ix=e_ix, slot=slot: e.dma_start(
                out=wdn[:, slot * 4 * H:(slot + 1) * 4 * H],
                in_=dram["down_T"][e_ix].rearrange("k p f -> p k f")),
                incs=[(GD, 16)])
            wready[e_ix] = GD.now()

        wfree = {0: [], 1: []}
        load_expert(0, 0, [])
        p.em("gpsimd", [], lambda e: e.dma_start(
            out=acc[:, :],
            in_=dram["sp_shared"][:, :, gbase:gbase + L].rearrange("h p t -> p h t")),
            incs=[(GD, 16)])
        acc_init = GD.now()
        load_expert(1, 1, [])
        bank_free = {i: [] for i in range(4)}
        acc_bank_free = {i: [] for i in range(2)}
        sg_free = {0: [], 1: []}
        hm_free = {0: [], 1: []}
        NSTEP = NE * NCHH
        pcbc_free = {0: []}
        pend = None

        def _emit_down(state):
            e2, ch2, par2, hm2t = state
            t02 = ch2 * Tc
            slot2 = e2 % 2
            wd_v = wdn[:, slot2 * 4 * H:(slot2 + 1) * 4 * H]
            hm_v2 = hmt[:, par2 * 4 * Tc:(par2 + 1) * 4 * Tc]
            for oo in range(8):
                bk = oo % 2
                for k in range(4):
                    w4 = []
                    if k == 0:
                        w4 += acc_bank_free[bk]
                        if oo == 0:
                            w4 += [hm2t]
                    p.em("tensor", w4,
                         lambda e, oo=oo, k=k, bk=bk, wd_v=wd_v, hm_v2=hm_v2:
                         e.matmul(pacc[bk][:],
                                  wd_v[:, k * H + oo * 128:k * H + (oo + 1) * 128],
                                  hm_v2[:, k * Tc:(k + 1) * Tc],
                                  start=(k == 0), stop=(k == 3)),
                         incs=[(TS, 1)] if k == 3 else [])
                mm_tok = TS.now()
                a_sl = acc[:, oo * L + t02:oo * L + t02 + Tc]
                w5 = [mm_tok] + ([acc_init] if e2 == 0 else [])
                p.em("vector", w5,
                     lambda e, a_sl=a_sl, bk=bk: e.tensor_add(a_sl, a_sl, pacc[bk][:]),
                     incs=[(VS, 1)])
                acc_bank_free[bk] = [VS.now()]
            hm_free[par2] = [TS.now()]
            if ch2 == NCHH - 1:
                wfree[slot2] = [TS.now()]

        for s in range(NSTEP):
            e_ix, ch = divmod(s, NCHH)
            slot = e_ix % 2
            par = s % 2
            t0 = ch * Tc
            wg_v = wgu[:, slot * HT * 2 * I_EXP:(slot + 1) * HT * 2 * I_EXP]
            sg_v = sgt[:, par * 4 * Tc:(par + 1) * 4 * Tc]
            hm_v = hmt[:, par * 4 * Tc:(par + 1) * 4 * Tc]
            if ch == 1 and 1 <= e_ix and e_ix + 1 < NE:
                load_expert(e_ix + 1, 1 - slot, wfree[1 - slot])
            ev = {}
            for o in range(4):
                for k in range(HT):
                    w4 = []
                    if k == 0:
                        w4 += bank_free[o]
                        if o == 0 and ch == 0:
                            w4 += [wready[e_ix]]
                        if s == 0:
                            w4 += [ld_tok]
                    p.em("tensor", w4,
                         lambda e, o=o, k=k, wg_v=wg_v, t0=t0: e.matmul(
                             pool[o][:],
                             wg_v[:, k * 2 * I_EXP + o * 128:
                                  k * 2 * I_EXP + (o + 1) * 128],
                             xlh[:, k * L + t0:k * L + t0 + Tc],
                             start=(k == 0), stop=(k == HT - 1)),
                         incs=[(TS, 1)] if k == HT - 1 else [])
                ev[o] = TS.now()
            sg_toks = {}
            for o in range(4):
                p.em("scalar", [ev[o]] + (sg_free[par] if o == 0 else []),
                     lambda e, o=o, sg_v=sg_v: e.activation(
                         sg_v[:, o * Tc:(o + 1) * Tc], pool[o][:], AF.Silu),
                     incs=[(SS, 1)])
                sg_toks[o] = SS.now()
            evu = {}
            for o in range(4):
                for k in range(HT):
                    w4 = [sg_toks[o]] if k == 0 else []
                    p.em("tensor", w4,
                         lambda e, o=o, k=k, wg_v=wg_v, t0=t0: e.matmul(
                             pool[o][:],
                             wg_v[:, k * 2 * I_EXP + (o + 4) * 128:
                                  k * 2 * I_EXP + (o + 5) * 128],
                             xlh[:, k * L + t0:k * L + t0 + Tc],
                             start=(k == 0), stop=(k == HT - 1)),
                         incs=[(TS, 1)] if k == HT - 1 else [])
                evu[o] = TS.now()
            p.em("tensor", pcbc_free[0],
                 lambda e, e_ix=e_ix, t0=t0: e.matmul(
                     pcbc[:], fg["indE"][:, e_ix * 128:(e_ix + 1) * 128],
                     fg["combT"][0:32, gbase + t0:gbase + t0 + Tc],
                     start=True, stop=True), incs=[(TS, 1)])
            cbc_tok = TS.now()
            hm_toks = {}
            for o in range(4):
                p.em("vector", [evu[o]] + (hm_free[par] if o == 0 else []),
                     lambda e, o=o, sg_v=sg_v, hm_v=hm_v: e.tensor_mul(
                         hm_v[:, o * Tc:(o + 1) * Tc],
                         sg_v[:, o * Tc:(o + 1) * Tc], pool[o][:]),
                     incs=[(VS, 1)])
                hm_toks[o] = VS.now()
            sg_free[par] = [hm_toks[3]]
            for o in range(4):
                p.em("vector", [cbc_tok] if o == 0 else [],
                     lambda e, o=o, hm_v=hm_v: e.tensor_mul(
                         hm_v[:, o * Tc:(o + 1) * Tc],
                         hm_v[:, o * Tc:(o + 1) * Tc], pcbc[:]),
                     incs=[(VS, 1)])
            hm2_tok = VS.now()
            pcbc_free[0] = [hm2_tok]
            for o in range(4):
                bank_free[o] = [hm_toks[o]]
            bank_free[0] = [hm2_tok]
            if pend is not None:
                _emit_down(pend)
            pend = (e_ix, ch, par, hm2_tok)
        _emit_down(pend)

        if DEBUG_COMBT and tag == "f1" and hh == 0:
            p.em("sync", [VS.now()], lambda e: e.dma_start(
                out=dram["dbg_acc"][:, :], in_=acc[:, :]), incs=[(SD, 16)])

        if DEBUG_COMBT and tag == "f1" and hh == 0:
            p.em("sync", [VS.now()], lambda e: e.dma_start(
                out=dram["dbg_acc"][:, :], in_=acc[:, :]), incs=[(SD, 16)])
        xr_free2 = []
        for ch in range(NCHH):
            t0 = ch * Tc
            p.em("sync", xr_free2, lambda e, t0=t0: e.dma_start(
                out=xrf[:, :],
                in_=x_in[b, :, t0:t0 + Tc].rearrange("(h p) t -> p h t", p=128)),
                incs=[(SD, 16)])
            xr_tok = SD.now()
            add_toks = []
            for o in range(HT):
                p.em("vector", [xr_tok],
                     lambda e, o=o, t0=t0: e.tensor_add(
                         xrf[:, o * Tc:(o + 1) * Tc],
                         xrf[:, o * Tc:(o + 1) * Tc],
                         acc[:, o * L + t0:o * L + t0 + Tc]),
                     incs=[(VS, 1)])
                add_toks.append(VS.now())
            p.em("sync", [add_toks[-1]], lambda e, t0=t0: e.dma_start(
                out=x_out[b, :, t0:t0 + Tc].rearrange("(h p) t -> p h t", p=128),
                in_=xrf[:, :]), incs=[(SD, 16)])
            xr_free2 = [SD.now()]
            if final and xout is not None:
                p.em("sync", [], lambda e, t0=t0: e.dma_start(
                    out=xout[b, :, t0:t0 + Tc].rearrange("(h p) t -> p h t", p=128),
                    in_=xrf[:, :]), incs=[(SD, 16)])
                xr_free2.append(SD.now())


def phase_attn(p, nc, c, x_in, x_out, final, xout, tag):
    TS, VS, SS, GS = c["TS"], c["VS"], c["SS"], c["GS"]
    SD, GD, CD = c["SD"], c["GD"], c["CD"]
    for b in range(NB):
        with ExitStack() as ast_:
            EC = ast_.enter_context
            qb = EC(nc.sbuf_tensor(tag + f"{b}_q", [128, HT * L], BF16))
            kb = EC(nc.sbuf_tensor(tag + f"{b}_k", [128, HT * L], BF16))
            vT = EC(nc.sbuf_tensor(tag + f"{b}_vT", [128, (L // 128) * H], BF16))
            _attn_qkv(p, nc, c, x_in, b, qb, kb, vT, tag)
            p.join([TS.now(), VS.now(), SS.now(), GS.now(),
                    SD.now(), GD.now(), CD.now()])
            _attn_core(p, nc, c, x_in, x_out, final, xout, b, qb, kb, vT, tag)
            p.join([TS.now(), VS.now(), SS.now(), GS.now(),
                    SD.now(), GD.now(), CD.now()])


def _attn_qkv(p, nc, c, x_in, b, qb, kb, vT, tag):
    dram, g = c["dram"], c["g"]
    SD, GD, CD = c["SD"], c["GD"], c["CD"]
    TS, VS, SS, GS = c["TS"], c["VS"], c["SS"], c["GS"]
    Tc = 512
    NCH = L // Tc
    with ExitStack() as st:
        EC = st.enter_context
        Wq = EC(nc.sbuf_tensor(tag + f"{b}q_Wq", [128, HT * H], BF16))
        Wk = EC(nc.sbuf_tensor(tag + f"{b}q_Wk", [128, HT * H], BF16))
        Wv = EC(nc.sbuf_tensor(tag + f"{b}q_Wv", [128, HT * H], BF16))
        xr = EC(nc.sbuf_tensor(tag + f"{b}q_xr", [128, HT * Tc], F32))
        xlnb = EC(nc.sbuf_tensor(tag + f"{b}q_xln", [128, HT * Tc], BF16))
        xbq = EC(nc.sbuf_tensor(tag + f"{b}q_xbq", [128, 2 * HT * Tc], BF16))
        rows = EC(nc.sbuf_tensor(tag + f"{b}q_rows", [1, 5 * Tc], F32))
        rows_b = EC(nc.sbuf_tensor(tag + f"{b}q_rows_b", [1, 2 * Tc], BF16))
        pool = [EC(nc.psum_tensor(f"{tag}{b}q_p{i}", [128, Tc], F32)) for i in range(4)]
        pst = [EC(nc.psum_tensor(f"{tag}{b}q_s{i}", [1, Tc], F32)) for i in range(2)]
        pbc = [EC(nc.psum_tensor(f"{tag}{b}q_b{i}", [128, Tc], F32)) for i in range(2)]

        for nm, dst in (("Wq_T", Wq), ("Wk_T", Wk), ("Wv_T", Wv)):
            p.em("gpsimd", [], lambda e, nm=nm, dst=dst: e.dma_start(
                out=dst[:, :], in_=dram[nm].rearrange("k p f -> p k f")),
                incs=[(GD, 16)])
        w_tok = GD.now()

        lnsc = dict(onesk=g["onesk"], onem=g["onem"], pm=pst[0], pv=pst[1],
                    pmb=pbc[0], prb=pbc[1], bc_free=[], stat_free=[])
        bank_free = {i: [] for i in range(4)}
        xr_free = {0: [], 1: []}

        for ch in range(NCH):
            buf = 0
            t0 = ch * Tc
            xr_v = xr[:, :]
            p.em("sync", xr_free[buf], lambda e, t0=t0, xr_v=xr_v: e.dma_start(
                out=xr_v,
                in_=x_in[b, :, t0:t0 + Tc].rearrange("(h p) t -> p h t", p=128)),
                incs=[(SD, 16)])
            xr_tok = SD.now()
            lnsc.update(
                xb=xbq[:, 0:HT * Tc], xsq=xbq[:, HT * Tc:2 * HT * Tc],
                mrow=rows[:, 0:Tc], msq=rows[:, Tc:2 * Tc],
                vrow=rows[:, 2 * Tc:3 * Tc], lrow=rows[:, 3 * Tc:4 * Tc],
                rrow=rows[:, 4 * Tc:5 * Tc],
                mrow_b=rows_b[:, 0:Tc], rrow_b=rows_b[:, Tc:2 * Tc],
                dtile=xbq[:, HT * Tc:2 * HT * Tc],
                xr_ready=[xr_tok],
            )
            xln_tok = emit_ln(p, c, xr_v, xlnb, Tc, lnsc)
            xr_free[buf] = [xln_tok]

            for which, W, dst, scale in (("q", Wq, qb, 1.0 / 32.0),
                                         ("k", Wk, kb, None)):
                for o in range(HT):
                    bank = o % 4
                    for k in range(HT):
                        w4 = []
                        if k == 0:
                            w4 += bank_free[bank]
                            if o == 0 and which == "q":
                                w4 += [xln_tok, w_tok]
                        p.em("tensor", w4,
                             lambda e, o=o, k=k, bank=bank, W=W: e.matmul(
                                 pool[bank][:],
                                 W[:, (k * HT + o) * 128:(k * HT + o + 1) * 128],
                                 xlnb[:, k * Tc:(k + 1) * Tc],
                                 start=(k == 0), stop=(k == HT - 1)),
                             incs=[(TS, 1)] if k == HT - 1 else [])
                    mm_tok = TS.now()
                    if scale is not None:
                        p.em("scalar", [mm_tok],
                             lambda e, o=o, bank=bank, dst=dst, t0=t0, scale=scale:
                             e.activation(
                                 dst[:, o * L + t0:o * L + t0 + Tc], pool[bank][:],
                                 AF.Copy, scale=scale),
                             incs=[(SS, 1)])
                    else:
                        p.em("scalar", [mm_tok],
                             lambda e, o=o, bank=bank, dst=dst, t0=t0: e.copy(
                                 dst[:, o * L + t0:o * L + t0 + Tc], pool[bank][:]),
                             incs=[(SS, 1)])
                    bank_free[bank] = [SS.now()]
            for tt in range(4):
                kt = ch * 4 + tt
                for nn in range(2):
                    bank = (tt * 2 + nn) % 4
                    for k in range(HT):
                        w4 = bank_free[bank] if k == 0 else []
                        p.em("tensor", w4,
                             lambda e, tt=tt, nn=nn, k=k, bank=bank: e.matmul(
                                 pool[bank][:],
                                 xlnb[:, k * Tc + tt * 128:k * Tc + tt * 128 + 128],
                                 Wv[:, k * H + nn * 512:k * H + (nn + 1) * 512],
                                 start=(k == 0), stop=(k == HT - 1)),
                             incs=[(TS, 1)] if k == HT - 1 else [])
                    mm_tok = TS.now()
                    p.em("scalar", [mm_tok],
                         lambda e, kt=kt, nn=nn, bank=bank: e.copy(
                             vT[:, kt * H + nn * 512:kt * H + (nn + 1) * 512],
                             pool[bank][:]),
                         incs=[(SS, 1)])
                    bank_free[bank] = [SS.now()]


def _attn_core(p, nc, c, x_in, x_out, final, xout, b, qb, kb, vT, tag):
    dram, g = c["dram"], c["g"]
    SD, GD, CD = c["SD"], c["GD"], c["CD"]
    TS, VS, SS, GS = c["TS"], c["VS"], c["SS"], c["GS"]
    NKT = L // 128
    with ExitStack() as st:
        EC = st.enter_context
        Wo = EC(nc.sbuf_tensor(tag + f"{b}a_Wo", [128, HT * H], BF16))
        scr = EC(nc.sbuf_tensor(tag + f"{b}a_scr", [128, 4 * L], F32))
        abuf = EC(nc.sbuf_tensor(tag + f"{b}a_ab", [128, 4 * L], BF16))
        atT = EC(nc.sbuf_tensor(tag + f"{b}a_atT", [128, NKT * 512], BF16))
        ctx = EC(nc.sbuf_tensor(tag + f"{b}a_ctx", [128, HT * 512], BF16))
        xrs = EC(nc.sbuf_tensor(tag + f"{b}a_xr", [128, HT * 512], F32))
        srow = EC(nc.sbuf_tensor(tag + f"{b}a_srow", [128, 16], F32))
        pool = [EC(nc.psum_tensor(f"{tag}{b}a_p{i}", [128, 512], F32)) for i in range(2)]
        ptr = [EC(nc.psum_tensor(f"{tag}{b}a_tr{i}", [128, 128], BF16)) for i in range(2)]
        pctx = [EC(nc.psum_tensor(f"{tag}{b}a_c{i}", [128, 512], F32)) for i in range(4)]

        p.em("gpsimd", [], lambda e: e.dma_start(
            out=Wo[:, :], in_=dram["W_outB_T"].rearrange("k p f -> p k f")),
            incs=[(GD, 16)])
        w_tok = GD.now()

        bank_free = {i: [] for i in range(2)}
        ptr_free = {0: [], 1: []}
        pctx_free = {i: [] for i in range(4)}
        scr_free = []
        abuf_free = []
        atT_free = []
        ctx_free = []
        xr_free = []

        for gg in range(4):
            q0 = gg * 512
            sm_toks = []
            for q4 in range(4):
                qt0 = q0 + q4 * 128
                ev_toks = []
                for nn in range(4):
                    bank = (q4 * 4 + nn) % 2
                    for k in range(HT):
                        w4 = []
                        if k == 0:
                            w4 += bank_free[bank]
                            if q4 == 0 and nn == 0 and gg == 0:
                                w4 += [w_tok] + scr_free
                            if q4 == 0 and nn == 0:
                                w4 += scr_free
                        p.em("tensor", w4,
                             lambda e, q4=q4, nn=nn, k=k, bank=bank, qt0=qt0: e.matmul(
                                 pool[bank][:],
                                 qb[:, k * L + qt0:k * L + qt0 + 128],
                                 kb[:, k * L + nn * 512:k * L + (nn + 1) * 512],
                                 start=(k == 0), stop=(k == HT - 1)),
                             incs=[(TS, 1)] if k == HT - 1 else [])
                    mm_tok = TS.now()
                    p.em("scalar", [mm_tok],
                         lambda e, q4=q4, nn=nn, bank=bank: e.copy(
                             scr[:, q4 * L + nn * 512:q4 * L + (nn + 1) * 512],
                             pool[bank][:]),
                         incs=[(SS, 1)])
                    ev_toks.append(SS.now())
                    bank_free[bank] = [ev_toks[-1]]
                nmx = srow[:, q4 * 4:q4 * 4 + 1]
                ism = srow[:, q4 * 4 + 1:q4 * 4 + 2]
                p.em("vector", [ev_toks[-1]],
                     lambda e, q4=q4, nmx=nmx: e.reduce_max(
                         nmx, scr[:, q4 * L:(q4 + 1) * L], axis=AX.X, negate=True),
                     incs=[(VS, 1)])
                p.em("scalar", [VS.now()],
                     lambda e, q4=q4, nmx=nmx: e.activation(
                         scr[:, q4 * L:(q4 + 1) * L], scr[:, q4 * L:(q4 + 1) * L],
                         AF.Exp, bias=nmx),
                     incs=[(SS, 1)])
                p.em("vector", [SS.now()],
                     lambda e, q4=q4, ism=ism: e.reduce_sum(
                         ism, scr[:, q4 * L:(q4 + 1) * L], axis=AX.X), incs=[(VS, 1)])
                p.em("vector", [VS.now()],
                     lambda e, ism=ism: e.reciprocal(ism, ism), incs=[(VS, 1)])
                p.em("vector", [VS.now()] + (abuf_free if q4 == 0 else []),
                     lambda e, q4=q4, ism=ism: e.tensor_scalar(
                         abuf[:, q4 * L:(q4 + 1) * L], scr[:, q4 * L:(q4 + 1) * L],
                         ism, None, ALU.mult),
                     incs=[(VS, 1)])
                sm_toks.append(VS.now())
            abuf_free = []
            scr_free = [sm_toks[-1]]
            tr_toks = []
            tix = 0
            for q4 in range(4):
                for j in range(NKT):
                    pb = tix % 2
                    tix += 1
                    w4 = [sm_toks[q4]] if j == 0 else []
                    w4 += ptr_free[pb]
                    if q4 == 0 and j == 0:
                        w4 += atT_free
                    p.em("tensor", w4,
                         lambda e, q4=q4, j=j, pb=pb: e.transpose(
                             ptr[pb][:],
                             abuf[:, q4 * L + j * 128:q4 * L + (j + 1) * 128],
                             g["ident"][:, :]),
                         incs=[(TS, 1)])
                    t_tok = TS.now()
                    p.em("scalar", [t_tok],
                         lambda e, q4=q4, j=j, pb=pb: e.copy(
                             atT[:, j * 512 + q4 * 128:j * 512 + (q4 + 1) * 128],
                             ptr[pb][:]),
                         incs=[(SS, 1)])
                    ptr_free[pb] = [SS.now()]
            atT_ready = SS.now()
            abuf_free = [TS.now()]
            ctx_toks = []
            for half in range(2):
                for o in range(4):
                    oo = half * 4 + o
                    for j in range(NKT):
                        w4 = []
                        if j == 0:
                            w4 += pctx_free[o]
                            if half == 0 and o == 0:
                                w4 += [atT_ready] + ctx_free
                        p.em("tensor", w4,
                             lambda e, oo=oo, j=j, o=o: e.matmul(
                                 pctx[o][:],
                                 vT[:, j * H + oo * 128:j * H + (oo + 1) * 128],
                                 atT[:, j * 512:(j + 1) * 512],
                                 start=(j == 0), stop=(j == NKT - 1)),
                             incs=[(TS, 1)] if j == NKT - 1 else [])
                    mm_tok = TS.now()
                    p.em("scalar", [mm_tok],
                         lambda e, oo=oo, o=o: e.copy(
                             ctx[:, oo * 512:(oo + 1) * 512], pctx[o][:]),
                         incs=[(SS, 1)])
                    pctx_free[o] = [SS.now()]
                    ctx_toks.append(SS.now())
            atT_free = [TS.now()]
            ctx_ready = ctx_toks[-1]
            p.em("sync", xr_free, lambda e, q0=q0: e.dma_start(
                out=xrs[:, :],
                in_=x_in[b, :, q0:q0 + 512].rearrange("(h p) t -> p h t", p=128)),
                incs=[(SD, 16)])
            xr_tok = SD.now()
            add_toks = []
            for o in range(HT):
                bank = o % 2
                for k in range(HT):
                    w4 = []
                    if k == 0:
                        w4 += bank_free[bank]
                        if o == 0:
                            w4 += [ctx_ready, w_tok]
                    p.em("tensor", w4,
                         lambda e, o=o, k=k, bank=bank: e.matmul(
                             pool[bank][:],
                             Wo[:, (k * HT + o) * 128:(k * HT + o + 1) * 128],
                             ctx[:, k * 512:(k + 1) * 512],
                             start=(k == 0), stop=(k == HT - 1)),
                         incs=[(TS, 1)] if k == HT - 1 else [])
                mo_tok = TS.now()
                p.em("vector", [mo_tok, xr_tok],
                     lambda e, o=o, bank=bank: e.tensor_add(
                         xrs[:, o * 512:(o + 1) * 512],
                         xrs[:, o * 512:(o + 1) * 512], pool[bank][:]),
                     incs=[(VS, 1)])
                add_toks.append(VS.now())
                bank_free[bank] = [add_toks[-1]]
            ctx_free = [TS.now()]
            p.em("sync", [add_toks[-1]], lambda e, q0=q0: e.dma_start(
                out=x_out[b, :, q0:q0 + 512].rearrange("(h p) t -> p h t", p=128),
                in_=xrs[:, :]), incs=[(SD, 16)])
            xr_free = [SD.now()]
            if final and xout is not None:
                p.em("sync", [], lambda e, q0=q0: e.dma_start(
                    out=xout[b, :, q0:q0 + 512].rearrange("(h p) t -> p h t", p=128),
                    in_=xrs[:, :]), incs=[(SD, 16)])
                xr_free.append(SD.now())



def _pack_pc(v):
    return np.ascontiguousarray(v.reshape(CT, 128).T)


def _prep_inputs(inputs):
    f = {}
    x = np.ascontiguousarray(inputs["L0_attn_a_x"])
    W_in = inputs["L0_attn_a_W_in"]
    f["W_in_T"] = np.ascontiguousarray(W_in.T).reshape(HT, 128, 2 * INNER)
    cw = inputs["L0_attn_a_conv_w"][:, 0, :]
    f["cw_p"] = np.ascontiguousarray(
        cw.reshape(CT, 128, 4).transpose(1, 0, 2).reshape(128, CT * 4))
    f["cb_p"] = _pack_pc(inputs["L0_attn_a_conv_b"])
    W_x = inputs["L0_attn_a_W_x"]
    f["W_BC_T"] = np.ascontiguousarray(W_x[:32].T).reshape(CT, 128, 32)
    f["W_dtin_T"] = np.ascontiguousarray(W_x[32:].T).reshape(CT, 128, INNER)
    f["W_dt_T"] = np.ascontiguousarray(inputs["L0_attn_a_W_dt"].T).reshape(CT, 128, INNER)
    f["bdt_p"] = _pack_pc(inputs["L0_attn_a_b_dt"])
    f["dv_p"] = _pack_pc(inputs["L0_attn_a_D"])
    f["W_outA_T"] = np.ascontiguousarray(inputs["L0_attn_a_W_out"].T).reshape(CT, 128, H)
    W_qkv = inputs["L0_attn_b_W_qkv"]
    f["Wq_T"] = np.ascontiguousarray(W_qkv[0::3].T).reshape(HT, 128, H)
    f["Wk_T"] = np.ascontiguousarray(W_qkv[1::3].T).reshape(HT, 128, H)
    f["Wv_T"] = np.ascontiguousarray(W_qkv[2::3].T).reshape(HT, 128, H)
    f["W_outB_T"] = np.ascontiguousarray(inputs["L0_attn_b_W_out"].T).reshape(HT, 128, H)
    f["Wr_T"] = np.ascontiguousarray(
        inputs["L0_ffn_router_weight"].T).reshape(HT, 128, NE)
    f["e_bias"] = np.ascontiguousarray(inputs["L0_ffn_e_score_bias"]).reshape(1, NE)
    f["Wg_T"] = np.ascontiguousarray(
        inputs["L0_ffn_shared_gate_w"].T).reshape(HT, 128, IS)
    f["Wu_T"] = np.ascontiguousarray(
        inputs["L0_ffn_shared_up_w"].T).reshape(HT, 128, IS)
    f["Wd_T"] = np.ascontiguousarray(
        inputs["L0_ffn_shared_down_w"].T).reshape(IS // 128, 128, H)
    f["gup_T"] = np.ascontiguousarray(
        inputs["L0_ffn_gate_up_proj"].transpose(0, 2, 1)).reshape(NE, HT, 128, 2 * I_EXP)
    f["down_T"] = np.ascontiguousarray(
        inputs["L0_ffn_down_proj"].transpose(0, 2, 1)).reshape(NE, 4, 128, H)

    shards = []
    for cid in range(8):
        d = dict(f)
        xs = x[cid * NB:(cid + 1) * NB]
        d["x0"] = np.ascontiguousarray(xs.transpose(0, 2, 1))
        shards.append(d)
    return shards


def kernel(**inputs):
    shards = _prep_inputs(inputs)
    nc = build_program(upto="all")
    res = run_bass_kernel_spmd(nc, shards, core_ids=list(range(8)))
    outs = []
    for cid in range(8):
        xo = res.results[cid]["xout"]
        outs.append(np.ascontiguousarray(np.asarray(xo).transpose(0, 2, 1)))
    return np.concatenate(outs, axis=0).astype(np.float32)
